# version 05
# baseline (speedup 1.0000x reference)
import sys
sys.path.insert(0, '/opt/trn_rl_repo')
from contextlib import ExitStack

import numpy as np
import ml_dtypes

import concourse.bass as bass
import concourse.mybir as mybir
from concourse.bass_utils import run_bass_kernel_spmd

F32 = mybir.dt.float32
BF16 = mybir.dt.bfloat16
AF = mybir.ActivationFunctionType
ALU = mybir.AluOpType
AX = mybir.AxisListType

S = 2048
D = 1024
DT = D // 128
BPC = 2
NEG = -1.0e30
KSCALE = float(1.0 / np.sqrt(128.0))



def host_constants():
    bf = ml_dtypes.bfloat16
    c = {}
    c["ident_f32"] = np.eye(128, dtype=np.float32)
    c["ident_bf16"] = np.eye(128, dtype=bf)
    c["ones_r_f32"] = np.ones((1, 128), np.float32)
    c["ones_c_f32"] = np.ones((128, 1), np.float32)
    c["ones_c_bf16"] = np.ones((128, 1), bf)
    c["ones_row2k"] = np.ones((8, S), ml_dtypes.bfloat16)
    kp = np.arange(128)[:, None]
    cq = np.arange(896)[None, :]
    c["mask_causal"] = np.where(kp <= cq - 384, 0.0, NEG).astype(np.float32)
    t_ = np.arange(128)
    c["mask_tri"] = np.where(t_[:, None] <= t_[None, :], 0.0, NEG).astype(np.float32)
    hd = 64
    inv = 1.0 / (10000.0 ** (np.arange(0, hd, 2, dtype=np.float64) / hd))
    pos = np.arange(S, dtype=np.float64)
    j = np.arange(128)
    th = pos[None, :] * inv[(j % hd) % 32][:, None]
    c["rope_cos"] = np.cos(th).astype(np.float32)
    sgn = np.where((j % hd) < 32, -1.0, 1.0)[:, None]
    c["rope_sin"] = (sgn * np.sin(th)).astype(np.float32)
    c["epsln"] = np.full((1, 1), 1e-5, np.float32)
    sel = np.zeros((16, 16, 128), np.float32)
    for e in range(16):
        sel[e, e, :] = 1.0
    c["sel16"] = sel
    return c


def _rope_perm():
    j = np.arange(1024)
    h, d = j // 64, j % 64
    sw = np.where(d < 32, d + 32, d - 32)
    return h * 64 + sw


def prep_inputs(inputs):
    f = np.ascontiguousarray
    perm = _rope_perm()
    p = {}
    p["wqT"] = f(inputs["L0_attn_a_wq"].T)
    p["wqTs"] = f(p["wqT"][:, perm])
    p["kvdT"] = f(inputs["L0_attn_a_w_kv_down"].T)
    p["kupT"] = f(inputs["L0_attn_a_w_k_up"].T)
    p["kupTs"] = f(p["kupT"][:, perm])
    p["vupT"] = f(inputs["L0_attn_a_w_v_up"].T)
    p["woT"] = f(inputs["L0_attn_a_wo"].T)
    p["rw"] = f(inputs["L0_ffn_router_w"])
    p["rb_row"] = f(inputs["L0_ffn_router_b"].reshape(1, 16))
    p["w1"] = f(inputs["L0_ffn_w1"])
    p["b1l"] = f(inputs["L0_ffn_b1"].reshape(16, 16, 128).transpose(0, 2, 1))
    p["w2"] = f(inputs["L0_ffn_w2"])
    p["b2"] = f(inputs["L0_ffn_b2"])
    p["hWiT"] = f(inputs["L0_attn_b_Wi"].T)
    p["hWfT"] = f(inputs["L0_attn_b_Wf"].T)
    p["hWgT"] = f(inputs["L0_attn_b_Wg"].T)
    p["hWoT"] = f(inputs["L0_attn_b_Wo"].T)
    p["hnw"] = f(inputs["L0_attn_b_norm_w"].reshape(DT, 128).T)
    p["gWq"] = f(inputs["L1_attn_b_Wq"])
    p["gWk"] = f(inputs["L1_attn_b_Wk"])
    p["gWv"] = f(inputs["L1_attn_b_Wv"])
    p["gWo"] = f(inputs["L1_attn_b_Wo"])
    p["gWg"] = f(inputs["L1_attn_b_Wg"].reshape(DT, 128, 8).transpose(1, 0, 2))
    p["gbg"] = f(inputs["L1_attn_b_bg"].reshape(8, 1))
    p["gWog"] = f(inputs["L1_attn_b_Wog"])
    p["fWrT"] = f(inputs["L1_ffn_W_r"].T)
    p["fWkT"] = f(inputs["L1_ffn_W_k"].T)
    p["fWvT"] = f(inputs["L1_ffn_W_v"].T)
    p["fmur"] = f(inputs["L1_ffn_mu_r"].reshape(DT, 128).T)
    p["fmuk"] = f(inputs["L1_ffn_mu_k"].reshape(DT, 128).T)
    return p




class Bld:
    def __init__(self, nc):
        self.nc = nc
        self.es = ExitStack()
        self.t, self.v, self.s, self.g, self.y = (
            nc.tensor, nc.vector, nc.scalar, nc.gpsimd, nc.sync)
        self.sems = {}
        for n in ("st", "se_t", "se_v", "se_s", "se_g"):
            self.sems[n] = [self.es.enter_context(nc.semaphore(n)), 0]
        self.NLD = 4
        self.ld = [[self.es.enter_context(nc.semaphore(f"ld{i}")), 0]
                   for i in range(self.NLD)]
        self.ld_rr = 0
        self.uid = 0

    def nm(self, base):
        self.uid += 1
        return f"{base}_{self.uid}"

    def inc(self, instr, name, amt=1):
        h = self.sems[name]
        h[1] += amt
        instr.then_inc(h[0], amt)
        return h[1]

    def val(self, name):
        return self.sems[name][1]

    def wait(self, eng, name, v=None):
        h = self.sems[name]
        eng.wait_ge(h[0], h[1] if v is None else v)

    def vself(self, instr):
        v = self.inc(instr, "se_v")
        self.v.wait_ge(self.sems["se_v"][0], v)
        return v

    def sself(self, instr):
        v = self.inc(instr, "se_s")
        self.s.wait_ge(self.sems["se_s"][0], v)
        return v

    def sb(self, base, shape, dtype, scope=None, side="left"):
        return (scope or self.es).enter_context(
            self.nc.sbuf_tensor(self.nm(base), shape, dtype, side=side))

    def ps(self, base, shape, dtype, scope=None):
        return (scope or self.es).enter_context(
            self.nc.psum_tensor(self.nm(base), shape, dtype))

    def load(self, out, in_):
        si = self.ld_rr
        self.ld_rr = (self.ld_rr + 1) % self.NLD
        h = self.ld[si]
        if h[1] > 0:
            self.y.wait_ge(h[0], h[1])
        h[1] += 16
        self.y.dma_start(out=out, in_=in_).then_inc(h[0], 16)
        return (si, h[1])

    def wait_ld(self, eng, tok):
        eng.wait_ge(self.ld[tok[0]][0], tok[1])

    def wait_ld_all(self, eng):
        for h in self.ld:
            if h[1] > 0:
                eng.wait_ge(h[0], h[1])

    def store(self, out, in_):
        return self.inc(self.y.dma_start(out=out, in_=in_), "st", 16)

    def drain_st(self):
        h = self.sems["st"]
        if h[1] > 0:
            for e in (self.t, self.v, self.s, self.g, self.y):
                e.wait_ge(h[0], h[1])

    def barrier(self):
        self.nc.all_engine_barrier()


class Scratch:
    def __init__(self, b: Bld):
        self.pbig = b.ps("p_big", [128, 3, 512], F32)
        self.paux = b.ps("p_aux", [128, 2, 512], F32)
        self.pst = b.ps("p_st", [1, 2, 512], F32)
        self.pbf = b.ps("p_bf", [128, 1024], BF16)
        self.sq2 = b.sb("s_sq2", [128, 2, 512], BF16)
        self.ab = b.sb("s_ab", [1, 2, S], F32)
        self.mv = b.sb("s_mv", [1, 2, 512], F32)
        self.Abc = b.sb("s_Abc", [128, 512], F32)
        self.Bbc = b.sb("s_Bbc", [128, 512], F32)
        self.tmpa = b.sb("s_tmpa", [128, 512], F32)
        self.tmpb = b.sb("s_tmpb", [128, 512], F32)
        self.stage = b.sb("s_stage", [128, 2, 512], F32)
        self.stg_par = 0
        self.stg_tok = [0, 0]
        self.ptok = {}

    def pacq(self, b: Bld, key):
        tok = self.ptok.get(key)
        if tok is not None:
            b.wait(b.t, tok[0], tok[1])

    def prel(self, key, sem, val):
        self.ptok[key] = (sem, val)


def load_wbf(b: Bld, scr: Scratch, dst, src_ap, kt_total, fdim, tguard=None):
    src3 = src_ap.rearrange("(kt p) f -> p kt f", p=128)
    nh = (fdim + 511) // 512
    gi = None
    if tguard is not None:
        b.g.wait_ge(b.sems["se_t"][0], tguard)
    for k0 in range(kt_total):
        for hh in range(nh):
            f0 = hh * 512
            f1 = min(fdim, f0 + 512)
            par = scr.stg_par
            scr.stg_par = 1 - scr.stg_par
            if scr.stg_tok[par] > 0:
                b.y.wait_ge(b.sems["se_g"][0], scr.stg_tok[par])
            tok = b.load(scr.stage[:, par, 0:f1 - f0], src3[:, k0, f0:f1])
            b.wait_ld(b.g, tok)
            gi = b.inc(b.g.tensor_copy(dst[:, k0, f0:f1], scr.stage[:, par, 0:f1 - f0]),
                       "se_g")
            scr.stg_tok[par] = gi
    return gi



def emit_ln(b, K, scr, x3, ln3, ncols, apply=True, out_off=0):
    cw = min(512, ncols)
    si3 = vi_st = None
    vln = None
    tis = []
    for c0 in range(0, ncols, cw):
        sl = slice(c0, c0 + cw)
        osl = slice(c0 + out_off, c0 + out_off + cw)
        for kt in range(DT):
            if len(tis) >= 2:
                b.s.wait_ge(b.sems["se_t"][0], tis[-2])
            sqs = scr.sq2[:, kt % 2, 0:cw]
            si = b.inc(b.s.activation(sqs, x3[:, kt, sl], AF.Square), "se_s")
            if kt == 0:
                if vi_st is not None:
                    b.t.wait_ge(b.sems["se_v"][0], vi_st)
                scr.pacq(b, "pst0")
                scr.pacq(b, "pst1")
            b.t.matmul(scr.pst[:, 0, 0:cw], K.ones_c_f32[:], x3[:, kt, sl],
                       start=(kt == 0), stop=(kt == DT - 1))
            b.t.wait_ge(b.sems["se_s"][0], si)
            mm = b.t.matmul(scr.pst[:, 1, 0:cw], K.ones_c_bf16[:], sqs,
                            start=(kt == 0), stop=(kt == DT - 1))
            tis.append(b.inc(mm, "se_t"))
        ti2 = tis[-1]
        m, var = scr.mv[0:1, 0, 0:cw], scr.mv[0:1, 1, 0:cw]
        a, bsh = scr.ab[0:1, 0, osl], scr.ab[0:1, 1, osl]
        b.v.wait_ge(b.sems["se_t"][0], ti2)
        b.vself(b.v.tensor_scalar(m, scr.pst[:, 0, 0:cw], 1.0 / D, None, op0=ALU.mult))
        b.vself(b.v.tensor_scalar(var, scr.pst[:, 1, 0:cw], 1.0 / D, None, op0=ALU.mult))
        b.vself(b.v.tensor_tensor(scr.tmpa[0:1, 0:cw], m, m, op=ALU.mult))
        vi_st = b.vself(b.v.tensor_tensor(var, var, scr.tmpa[0:1, 0:cw], op=ALU.subtract))
        scr.prel("pst0", "se_v", vi_st)
        scr.prel("pst1", "se_v", vi_st)
        b.s.wait_ge(b.sems["se_v"][0], vi_st)
        b.sself(b.s.activation(scr.tmpa[0:1, 0:cw], var, AF.Ln, bias=scr.eps[0:1, 0:1]))
        si2 = b.sself(b.s.activation(a, scr.tmpa[0:1, 0:cw], AF.Exp, scale=-0.5))
        b.v.wait_ge(b.sems["se_s"][0], si2)
        vi2 = b.vself(b.v.scalar_tensor_tensor(bsh, m, -1.0, a, op0=ALU.mult, op1=ALU.mult))
        b.t.wait_ge(b.sems["se_v"][0], vi2)
        if si3 is not None:
            b.t.wait_ge(b.sems["se_s"][0], si3)
        scr.pacq(b, "paux0")
        scr.pacq(b, "paux1")
        b.t.matmul(scr.paux[:, 0, 0:cw], K.ones_r_f32[:], a, start=True, stop=True)
        ti3 = b.inc(b.t.matmul(scr.paux[:, 1, 0:cw], K.ones_r_f32[:], bsh,
                               start=True, stop=True), "se_t")
        b.s.wait_ge(b.sems["se_t"][0], ti3)
        b.s.copy(scr.Abc[:, 0:cw], scr.paux[:, 0, 0:cw])
        si3 = b.inc(b.s.copy(scr.Bbc[:, 0:cw], scr.paux[:, 1, 0:cw]), "se_s")
        scr.prel("paux0", "se_s", si3)
        scr.prel("paux1", "se_s", si3)
        if apply:
            b.v.wait_ge(b.sems["se_s"][0], si3)
            for kt in range(DT):
                b.v.tensor_tensor(scr.tmpb[:, 0:cw], x3[:, kt, sl], scr.Abc[:, 0:cw],
                                  op=ALU.mult)
                ins = b.v.tensor_tensor(ln3[:, kt, osl], scr.tmpb[:, 0:cw], scr.Bbc[:, 0:cw],
                                       op=ALU.add)
            vln = b.inc(ins, "se_v")
    return vln


def xap3(x_ap, batch):
    return x_ap[batch].rearrange("(kt p) s -> p kt s", p=128)


def ln_from_hbm(b, K, scr, xin, bt, ln):
    for half in range(2):
        sch = ExitStack()
        xh = b.sb("lnx", [128, DT, 1024], F32, sch)
        b.drain_st()
        tx = b.load(xh[:], xap3(xin, bt)[:, :, half * 1024:(half + 1) * 1024])
        for e in (b.s, b.t, b.v):
            b.wait_ld(e, tx)
        vln = emit_ln(b, K, scr, xh, ln, 1024, out_off=half * 1024)
        b.barrier()
        sch.close()
    return vln



def emit_l1ffn(b, K, scr, Wap, xin, xout):
    sc = ExitStack()
    b.drain_st()
    mu = b.sb("f_mu", [128, 2, DT], F32, sc)
    om = b.sb("f_om", [128, 2, DT], F32, sc)
    wr = b.sb("f_wr", [128, DT, D], BF16, sc)
    wv = b.sb("f_wv", [128, DT, D], BF16, sc)
    ln = b.sb("f_ln", [128, DT, S], BF16, sc)
    xr = b.sb("f_xr", [128, DT, S], BF16, sc)
    rt = b.sb("f_rt", [128, 512], BF16, sc)
    kact = ln

    tmu1 = b.load(mu[:, 0, :], Wap["fmur"])
    tmu2 = b.load(mu[:, 1, :], Wap["fmuk"])
    b.wait_ld(b.v, tmu1)
    b.wait_ld(b.v, tmu2)
    vom = b.inc(b.v.tensor_scalar(om[:], mu[:], -1.0, 1.0, op0=ALU.mult, op1=ALU.add),
                "se_v")
    gr = load_wbf(b, scr, wr, Wap["fWrT"], DT, D)
    gv = load_wbf(b, scr, wv, Wap["fWvT"], DT, D)

    for bt in range(BPC):
        vln = ln_from_hbm(b, K, scr, xin, bt, ln)
        scb = ExitStack()
        xk = b.sb("f_xk", [128, DT, S], BF16, scb)
        wk = b.sb("f_wk", [128, DT, D], BF16, scb)
        gk = load_wbf(b, scr, wk, Wap["fWkT"], DT, D)
        b.v.wait_ge(b.sems["se_v"][0], vom)
        for kt in range(DT):
            for dst, mi in ((xr, 0), (xk, 1)):
                b.v.tensor_scalar(dst[:, kt, :], ln[:, kt, :], mu[:, mi, kt:kt + 1],
                                  None, op0=ALU.mult)
                for c0 in range(0, S, 512):
                    lo = max(c0, 1)
                    hi = c0 + 512
                    b.v.tensor_scalar(scr.tmpb[:, 0:hi - lo], ln[:, kt, lo - 1:hi - 1],
                                      om[:, mi, kt:kt + 1], None, op0=ALU.mult)
                    ins = b.v.tensor_tensor(dst[:, kt, lo:hi], dst[:, kt, lo:hi],
                                            scr.tmpb[:, 0:hi - lo], op=ALU.add)
        vsh = b.inc(ins, "se_v")
        b.t.wait_ge(b.sems["se_v"][0], vsh)
        b.t.wait_ge(b.sems["se_g"][0], gk)
        for c in range(4):
            sl = slice(c * 512, (c + 1) * 512)
            for ft in range(DT):
                slot = (ft * 4 + c) % 3
                key = f"pb{slot}"
                scr.pacq(b, key)
                for kt in range(DT):
                    mm = b.t.matmul(scr.pbig[:, slot, :], wk[:, kt, ft * 128:(ft + 1) * 128],
                                    xk[:, kt, sl], start=(kt == 0), stop=(kt == DT - 1))
                ti = b.inc(mm, "se_t")
                b.s.wait_ge(b.sems["se_t"][0], ti)
                si = b.inc(b.s.activation(scr.tmpa[:, :], scr.pbig[:, slot, :], AF.Relu),
                           "se_s")
                scr.prel(key, "se_s", si)
                b.v.wait_ge(b.sems["se_s"][0], si)
                vi = b.inc(b.v.tensor_tensor(kact[:, ft, sl], scr.tmpa[:, :], scr.tmpa[:, :],
                                             op=ALU.mult), "se_v")
        scb.close()
        scb2 = ExitStack()
        x2 = b.sb("f_x2", [128, DT, S], F32, scb2)
        b.wait(b.y, "se_v")
        b.drain_st()
        tx2 = b.load(x2[:], xap3(xin, bt))
        b.wait_ld(b.v, tx2)
        b.t.wait_ge(b.sems["se_v"][0], vi)
        b.t.wait_ge(b.sems["se_g"][0], gv)
        for c in range(4):
            sl = slice(c * 512, (c + 1) * 512)
            for ft in range(DT):
                slot = (ft + c) % 2
                key = f"pb{slot}"
                scr.pacq(b, key)
                for kt in range(DT):
                    mm = b.t.matmul(scr.pbig[:, slot, :], wr[:, kt, ft * 128:(ft + 1) * 128],
                                    xr[:, kt, sl], start=(kt == 0), stop=(kt == DT - 1))
                ti = b.inc(mm, "se_t")
                b.s.wait_ge(b.sems["se_t"][0], ti)
                b.s.wait_ge(b.sems["se_v"][0], b.val("se_v"))
                si = b.inc(b.s.activation(rt[:], scr.pbig[:, slot, :], AF.Sigmoid), "se_s")
                scr.prel(key, "se_s", si)
                scr.pacq(b, "pb2")
                for kt in range(DT):
                    mm = b.t.matmul(scr.pbig[:, 2, :], wv[:, kt, ft * 128:(ft + 1) * 128],
                                    kact[:, kt, sl], start=(kt == 0), stop=(kt == DT - 1))
                ti = b.inc(mm, "se_t")
                b.v.wait_ge(b.sems["se_t"][0], ti)
                b.v.wait_ge(b.sems["se_s"][0], si)
                b.v.tensor_tensor(scr.tmpa[:, :], scr.pbig[:, 2, :], rt[:], op=ALU.mult)
                vi = b.inc(b.v.tensor_tensor(x2[:, ft, sl], x2[:, ft, sl], scr.tmpa[:, :],
                                             op=ALU.add), "se_v")
                scr.prel("pb2", "se_v", vi)
        b.wait(b.y, "se_v")
        b.store(xap3(xout, bt), x2[:])
        b.barrier()
        scb2.close()
    sc.close()



def emit_hgrn(b, K, scr, Wap, xin, xout):
    sc = ExitStack()
    b.drain_st()
    nw = b.sb("h_nw", [128, DT], F32, sc)
    ln = b.sb("h_ln", [128, DT, S], BF16, sc)
    h = b.sb("h_h", [128, DT, S], BF16, sc)

    b.load(nw[:], Wap["hnw"])

    for bt in range(BPC):
        ln_from_hbm(b, K, scr, xin, bt, ln)
        scb = ExitStack()
        wi = b.sb("h_wi", [128, DT, D], BF16, scb)
        wf = b.sb("h_wf", [128, DT, D], BF16, scb)
        sit = b.sb("h_si", [128, S], BF16, scb)
        ftg = b.sb("h_f", [128, S], BF16, scb)
        inp = b.sb("h_inp", [128, S], BF16, scb)
        tmpS = b.sb("h_tmpS", [128, S], F32, scb)
        load_wbf(b, scr, wi, Wap["hWiT"], DT, D)
        gf = load_wbf(b, scr, wf, Wap["hWfT"], DT, D)
        b.t.wait_ge(b.sems["se_g"][0], gf)
        vsc = b.val("se_v")
        for dt_ in range(DT):
            b.s.wait_ge(b.sems["se_v"][0], vsc)
            for c in range(4):
                sl = slice(c * 512, (c + 1) * 512)
                for w_, act, dst, slot in ((wi, AF.Silu, sit, c % 2), (wf, AF.Sigmoid, ftg, 2)):
                    key = f"pb{slot}"
                    scr.pacq(b, key)
                    for kt in range(DT):
                        mm = b.t.matmul(scr.pbig[:, slot, :],
                                        w_[:, kt, dt_ * 128:(dt_ + 1) * 128],
                                        ln[:, kt, sl], start=(kt == 0), stop=(kt == DT - 1))
                    ti = b.inc(mm, "se_t")
                    b.s.wait_ge(b.sems["se_t"][0], ti)
                    si = b.inc(b.s.activation(dst[:, sl], scr.pbig[:, slot, :], act), "se_s")
                    scr.prel(key, "se_s", si)
            b.v.wait_ge(b.sems["se_s"][0], si)
            b.v.tensor_scalar(tmpS[:], ftg[:], -1.0, 1.0, op0=ALU.mult, op1=ALU.add)
            b.v.tensor_tensor(inp[:], sit[:], tmpS[:], op=ALU.mult)
            vsc = b.inc(b.v.tensor_tensor_scan(h[:, dt_, :], ftg[:], inp[:], 0.0,
                                               op0=ALU.mult, op1=ALU.add), "se_v")
        b.barrier()
        scb.close()
        scb = ExitStack()
        Arms = b.sb("h_Arms", [128, 4, 512], F32, scb)
        wg = b.sb("h_wg", [128, DT, D], BF16, scb)
        gg = load_wbf(b, scr, wg, Wap["hWgT"], DT, D)
        b.s.wait_ge(b.sems["se_v"][0], vsc)
        tis = []
        for c in range(4):
            sl = slice(c * 512, (c + 1) * 512)
            for kt in range(DT):
                if len(tis) >= 2:
                    b.s.wait_ge(b.sems["se_t"][0], tis[-2])
                sqs = scr.sq2[:, kt % 2, :]
                si = b.inc(b.s.activation(sqs, h[:, kt, sl], AF.Square), "se_s")
                if kt == 0:
                    scr.pacq(b, "pst1")
                b.t.wait_ge(b.sems["se_s"][0], si)
                mm = b.t.matmul(scr.pst[:, 1, :], K.ones_c_bf16[:], sqs,
                                start=(kt == 0), stop=(kt == DT - 1))
                tis.append(b.inc(mm, "se_t"))
            ti2 = tis[-1]
            b.s.wait_ge(b.sems["se_t"][0], ti2)
            b.sself(b.s.activation(scr.ab[0:1, 0, 0:512], scr.pst[:, 1, :], AF.Ln,
                                   bias=scr.eps[0:1, 0:1], scale=1.0 / D))
            si = b.sself(b.s.activation(scr.ab[0:1, 1, 0:512], scr.ab[0:1, 0, 0:512],
                                        AF.Exp, scale=-0.5))
            scr.prel("pst1", "se_s", si)
            b.t.wait_ge(b.sems["se_s"][0], si)
            scr.pacq(b, "paux0")
            ti3 = b.inc(b.t.matmul(scr.paux[:, 0, :], K.ones_r_f32[:],
                                   scr.ab[0:1, 1, 0:512], start=True, stop=True), "se_t")
            b.s.wait_ge(b.sems["se_t"][0], ti3)
            si = b.inc(b.s.copy(Arms[:, c, :], scr.paux[:, 0, :]), "se_s")
            scr.prel("paux0", "se_s", si)
        b.t.wait_ge(b.sems["se_g"][0], gg)
        b.v.wait_ge(b.sems["se_s"][0], si)
        for dt_ in range(DT):
            for c in range(4):
                sl = slice(c * 512, (c + 1) * 512)
                slot = (dt_ * 4 + c) % 3
                key = f"pb{slot}"
                scr.pacq(b, key)
                for kt in range(DT):
                    mm = b.t.matmul(scr.pbig[:, slot, :], wg[:, kt, dt_ * 128:(dt_ + 1) * 128],
                                    ln[:, kt, sl], start=(kt == 0), stop=(kt == DT - 1))
                ti = b.inc(mm, "se_t")
                b.s.wait_ge(b.sems["se_t"][0], ti)
                b.s.wait_ge(b.sems["se_v"][0], b.val("se_v"))
                si = b.inc(b.s.activation(scr.tmpa[:, :], scr.pbig[:, slot, :], AF.Silu),
                           "se_s")
                scr.prel(key, "se_s", si)
                b.v.wait_ge(b.sems["se_s"][0], si)
                b.v.tensor_tensor(scr.tmpb[:, :], h[:, dt_, sl], Arms[:, c, :], op=ALU.mult)
                b.v.tensor_scalar(scr.tmpb[:, :], scr.tmpb[:, :], nw[:, dt_:dt_ + 1],
                                  None, op0=ALU.mult)
                vi = b.inc(b.v.tensor_tensor(h[:, dt_, sl], scr.tmpb[:, :], scr.tmpa[:, :],
                                             op=ALU.mult), "se_v")
        b.barrier()
        scb.close()
        scb2 = ExitStack()
        wo = b.sb("h_wo", [128, DT, D], BF16, scb2)
        x2 = b.sb("h_x2", [128, DT, S], F32, scb2)
        go = load_wbf(b, scr, wo, Wap["hWoT"], DT, D)
        b.wait(b.y, "se_v")
        b.drain_st()
        tx2 = b.load(x2[:], xap3(xin, bt))
        b.wait_ld(b.v, tx2)
        b.t.wait_ge(b.sems["se_g"][0], go)
        b.t.wait_ge(b.sems["se_v"][0], vi)
        for c in range(4):
            sl = slice(c * 512, (c + 1) * 512)
            for ft in range(DT):
                slot = (ft * 4 + c) % 3
                key = f"pb{slot}"
                scr.pacq(b, key)
                for kt in range(DT):
                    mm = b.t.matmul(scr.pbig[:, slot, :], wo[:, kt, ft * 128:(ft + 1) * 128],
                                    h[:, kt, sl], start=(kt == 0), stop=(kt == DT - 1))
                ti = b.inc(mm, "se_t")
                b.v.wait_ge(b.sems["se_t"][0], ti)
                vi = b.inc(b.v.tensor_tensor(x2[:, ft, sl], x2[:, ft, sl],
                                             scr.pbig[:, slot, :], op=ALU.add), "se_v")
                scr.prel(key, "se_v", vi)
        b.wait(b.y, "se_v")
        b.store(xap3(xout, bt), x2[:])
        b.barrier()
        scb2.close()
    sc.close()



def emit_moe(b, K, scr, Wap, xin, xout):
    HS = 1024
    sc = ExitStack()
    b.drain_st()
    rw = b.sb("m_rw", [128, DT, 16], F32, sc)
    rbr = b.sb("m_rbr", [1, 16], F32, sc)
    rbbc = b.sb("m_rbbc", [128, 16], F32, sc)
    csbc = b.sb("m_csbc", [128, 16], F32, sc)
    sel = b.sb("m_sel", [16, 16, 128], F32, sc)
    b2 = b.sb("m_b2", [16, D], F32, sc)
    cT = b.sb("m_cT", [16, HS], F32, sc)
    l0T = cT
    atok = b.sb("m_atok", [128, 2], F32, sc)
    ttok = b.sb("m_ttok", [128, 3, 16], F32, sc)
    red = b.sb("m_red", [128, 4], F32, sc)
    b1sb = b.sb("m_b1", [128, 16], F32, sc)
    cbc = b.sb("m_cbc", [128, 512], F32, sc)
    hmid = b.sb("m_hmid", [128, 16, 512], BF16, sc)
    gel = b.sb("m_gel", [128, 512], BF16, sc)
    ln = b.sb("m_ln", [128, DT, HS], BF16, sc)
    w1 = b.sb("m_w1", [128, DT, 2048], BF16, sc)
    w2 = b.sb("m_w2", [128, 16, D], BF16, sc)
    x_sb = b.sb("m_x", [128, DT, HS], F32, sc)

    b.load(rw[:], Wap["rw"].rearrange("(kt p) e -> p kt e", p=128))
    b.load(sel[:], Wap["sel16"])
    b.load(rbr[:], Wap["rb_row"])
    tb2 = b.load(b2[:], Wap["b2"])
    b.wait_ld(b.t, tb2)
    scr.pacq(b, "paux0")
    ti = b.inc(b.t.matmul(scr.paux[:, 0, 0:16], K.ones_r_f32[:], rbr[:],
                          start=True, stop=True), "se_t")
    b.s.wait_ge(b.sems["se_t"][0], ti)
    si = b.inc(b.s.copy(rbbc[:], scr.paux[:, 0, 0:16]), "se_s")
    scr.prel("paux0", "se_s", si)
    scr.pacq(b, "pst0")
    for kt in range(DT):
        mm = b.t.matmul(scr.pst[:, 0, 0:16], K.ones_c_f32[:], rw[:, kt, :],
                        start=(kt == 0), stop=(kt == DT - 1))
    ti = b.inc(mm, "se_t")
    b.s.wait_ge(b.sems["se_t"][0], ti)
    si = b.inc(b.s.copy(scr.ab[0:1, 0, 0:16], scr.pst[:, 0, 0:16]), "se_s")
    scr.prel("pst0", "se_s", si)
    b.t.wait_ge(b.sems["se_s"][0], si)
    scr.pacq(b, "paux0")
    ti = b.inc(b.t.matmul(scr.paux[:, 0, 0:16], K.ones_r_f32[:], scr.ab[0:1, 0, 0:16],
                          start=True, stop=True), "se_t")
    b.s.wait_ge(b.sems["se_t"][0], ti)
    si = b.inc(b.s.copy(csbc[:], scr.paux[:, 0, 0:16]), "se_s")
    scr.prel("paux0", "se_s", si)

    lastw1 = lastw2 = lastgel = None
    for bt in range(BPC):
        for hf in range(2):
            hsl = slice(hf * HS, (hf + 1) * HS)
            b.drain_st()
            tx = b.load(x_sb[:], xap3(xin, bt)[:, :, hsl])
            for e in (b.s, b.t, b.v):
                b.wait_ld(e, tx)
            emit_ln(b, K, scr, x_sb, ln, HS)
            for c in range(2):
                sl = slice(c * 512, (c + 1) * 512)
                slot = c % 3
                key = f"pb{slot}"
                scr.pacq(b, key)
                for kt in range(DT):
                    mm = b.t.matmul(scr.pbig[0:16, slot, :], rw[:, kt, :], x_sb[:, kt, sl],
                                    start=(kt == 0), stop=(kt == DT - 1))
                ti = b.inc(mm, "se_t")
                b.s.wait_ge(b.sems["se_t"][0], ti)
                si = b.inc(b.s.copy(l0T[:, sl], scr.pbig[0:16, slot, :]), "se_s")
                scr.prel(key, "se_s", si)
            if hasattr(K, "dbg_lg"):
                b.wait(b.y, "se_s")
                b.store(K.dbg_lg[bt * 2 + hf], l0T[:])
            for it in range(8):
                tsl = slice(it * 128, (it + 1) * 128)
                b.t.wait_ge(b.sems["se_s"][0], si)
                scr.pacq(b, "pb0")
                tlg = b.inc(b.t.transpose(scr.pbig[:, 0, 0:16], l0T[:, tsl],
                                          K.ident_f32[0:16, 0:16]), "se_t")
                scr.pacq(b, "pb1")
                b.t.transpose(scr.pbig[:, 1, 0:1], scr.ab[0:1, 0, tsl],
                              K.ident_f32[0:1, 0:1])
                tab = b.inc(b.t.transpose(scr.pbig[:, 1, 1:2], scr.ab[0:1, 1, tsl],
                                          K.ident_f32[0:1, 0:1]), "se_t")
                b.v.wait_ge(b.sems["se_t"][0], tab)
                b.vself(b.v.tensor_copy(atok[:, 0:1], scr.pbig[:, 1, 0:1]))
                vi = b.vself(b.v.tensor_copy(atok[:, 1:2], scr.pbig[:, 1, 1:2]))
                scr.prel("pb1", "se_v", vi)
                b.vself(b.v.tensor_scalar(ttok[:, 0, :], scr.pbig[:, 0, 0:16],
                                          atok[:, 0:1], None, op0=ALU.mult))
                vi = b.vself(b.v.scalar_tensor_tensor(ttok[:, 0, :], csbc[:], atok[:, 1:2],
                                                      ttok[:, 0, :], op0=ALU.mult,
                                                      op1=ALU.add))
                scr.prel("pb0", "se_v", vi)
                b.vself(b.v.tensor_tensor(ttok[:, 0, :], ttok[:, 0, :], rbbc[:], op=ALU.add))
                vi = b.vself(b.v.reduce_max(red[:, 0:1], ttok[:, 0, :], axis=AX.X,
                                            negate=True))
                b.s.wait_ge(b.sems["se_v"][0], vi)
                si = b.inc(b.s.activation(ttok[:, 1, :], ttok[:, 0, :], AF.Exp,
                                          bias=red[:, 0:1]), "se_s")
                b.v.wait_ge(b.sems["se_s"][0], si)
                b.vself(b.v.tensor_copy(ttok[:, 0, :], ttok[:, 1, :]))
                b.vself(b.v.memset(ttok[:, 2, :], 0.0))
                b.vself(b.v.memset(red[:, 2:3], 0.0))
                for j in range(4):
                    b.vself(b.v.reduce_max(red[:, 1:2], ttok[:, 0, :], axis=AX.X))
                    b.vself(b.v.tensor_tensor(red[:, 2:3], red[:, 2:3], red[:, 1:2],
                                              op=ALU.add))
                    b.vself(b.v.tensor_scalar(scr.tmpa[:, 0:16], ttok[:, 0, :],
                                              red[:, 1:2], None, op0=ALU.is_ge))
                    b.vself(b.v.tensor_tensor(ttok[:, 2, :], ttok[:, 2, :],
                                              scr.tmpa[:, 0:16], op=ALU.add))
                    b.vself(b.v.scalar_tensor_tensor(ttok[:, 0, :], scr.tmpa[:, 0:16], NEG,
                                                     ttok[:, 0, :], op0=ALU.mult,
                                                     op1=ALU.add))
                b.vself(b.v.reciprocal(red[:, 3:4], red[:, 2:3]))
                b.vself(b.v.tensor_tensor(ttok[:, 1, :], ttok[:, 1, :], ttok[:, 2, :],
                                          op=ALU.mult))
                vi = b.vself(b.v.tensor_scalar(ttok[:, 1, :], ttok[:, 1, :], red[:, 3:4],
                                               None, op0=ALU.mult))
                b.t.wait_ge(b.sems["se_v"][0], vi)
                scr.pacq(b, "pb2")
                ti = b.inc(b.t.transpose(scr.pbig[0:16, 2, 0:128], ttok[:, 1, :],
                                         K.ident_f32[:]), "se_t")
                b.s.wait_ge(b.sems["se_t"][0], ti)
                si = b.inc(b.s.copy(cT[:, tsl], scr.pbig[0:16, 2, 0:128]), "se_s")
                scr.prel("pb2", "se_s", si)
            b.wait(b.t, "se_s")
            if hasattr(K, "dbg_cT"):
                b.wait(b.y, "se_s")
                b.store(K.dbg_cT[bt * 2 + hf], cT[:])
            for e in range(getattr(K, "dbg_nexp", 16)):
                if lastgel is not None:
                    b.y.wait_ge(b.sems["se_s"][0], lastgel)
                tb1 = b.load(b1sb[:], Wap["b1l"][e])
                g1 = load_wbf(b, scr, w1, Wap["w1"][e], DT, 2048, tguard=lastw1)
                g2 = load_wbf(b, scr, w2, Wap["w2"][e], 16, D, tguard=lastw2)
                b.wait_ld(b.s, tb1)
                b.t.wait_ge(b.sems["se_g"][0], g1)
                for c in range(2):
                    sl = slice(c * 512, (c + 1) * 512)
                    scr.pacq(b, "paux0")
                    ti = b.inc(b.t.matmul(scr.paux[:, 0, :], sel[:, e, :], cT[:, sl],
                                          start=True, stop=True), "se_t")
                    b.s.wait_ge(b.sems["se_t"][0], ti)
                    b.s.wait_ge(b.sems["se_v"][0], b.val("se_v"))
                    sic = b.inc(b.s.copy(cbc[:], scr.paux[:, 0, :]), "se_s")
                    scr.prel("paux0", "se_s", sic)
                    b.v.wait_ge(b.sems["se_s"][0], sic)
                    for fi in range(16):
                        slot = fi % 3
                        key = f"pb{slot}"
                        scr.pacq(b, key)
                        for kt in range(DT):
                            mm = b.t.matmul(scr.pbig[:, slot, :],
                                            w1[:, kt, fi * 128:(fi + 1) * 128],
                                            ln[:, kt, sl], start=(kt == 0),
                                            stop=(kt == DT - 1))
                        ti = b.inc(mm, "se_t")
                        lastw1 = ti
                        b.s.wait_ge(b.sems["se_t"][0], ti)
                        b.s.wait_ge(b.sems["se_v"][0], b.val("se_v"))
                        si = b.inc(b.s.activation(gel[:], scr.pbig[:, slot, :], AF.Gelu,
                                                  bias=b1sb[:, fi:fi + 1]), "se_s")
                        lastgel = si
                        scr.prel(key, "se_s", si)
                        b.v.wait_ge(b.sems["se_s"][0], si)
                        vi = b.inc(b.v.tensor_tensor(hmid[:, fi, :], gel[:], cbc[:],
                                                     op=ALU.mult), "se_v")
                    if hasattr(K, "dbg_hmid") and bt == 0 and hf == 0 and e == 0 and c == 0:
                        b.wait(b.y, "se_v")
                        b.store(K.dbg_hmid, hmid[:])
                    if c == 0:
                        b.t.wait_ge(b.sems["se_g"][0], g2)
                    b.t.wait_ge(b.sems["se_v"][0], vi)
                    nob2 = hasattr(K, "dbg_nob2") or e > 0
                    for dt_ in range(DT):
                        key = f"paux{dt_ % 2}"
                        scr.pacq(b, key)
                        if not nob2:
                            b.t.matmul(scr.paux[:, dt_ % 2, :],
                                       b2[:, dt_ * 128:(dt_ + 1) * 128],
                                       cT[:, sl], start=True, stop=False)
                        for ki in range(16):
                            mm = b.t.matmul(scr.paux[:, dt_ % 2, :],
                                            w2[:, ki, dt_ * 128:(dt_ + 1) * 128],
                                            hmid[:, ki, :], start=(nob2 and ki == 0),
                                            stop=(ki == 15))
                        ti = b.inc(mm, "se_t")
                        lastw2 = ti
                        b.v.wait_ge(b.sems["se_t"][0], ti)
                        vi2 = b.inc(b.v.tensor_tensor(x_sb[:, dt_, sl], x_sb[:, dt_, sl],
                                                      scr.paux[:, dt_ % 2, :], op=ALU.add),
                                    "se_v")
                        scr.prel(key, "se_v", vi2)
            b.wait(b.y, "se_v")
            b.store(xap3(xout, bt)[:, :, hsl], x_sb[:])
            b.barrier()
    sc.close()



def emit_l0a(b, K, scr, Wap, xin, xout):
    sc = ExitStack()
    b.drain_st()
    Ebuf = b.sb("a_E", [128, 2, 512], BF16, sc)
    bz = b.sb("a_bz", [128, 512], F32, sc)
    mc = b.sb("a_mc", [128, 896], F32, sc)
    b.load(mc[:], K.aps["mask_causal"])

    def rope_pair(wa, wb, src3, kt_tot, dst, rc, rs):
        for ft in range(DT):
            for c in range(4):
                sl = slice(c * 512, (c + 1) * 512)
                scr.pacq(b, "pb0")
                scr.pacq(b, "pb1")
                for kt in range(kt_tot):
                    b.t.matmul(scr.pbig[:, 0, :], wa[:, kt, ft * 128:(ft + 1) * 128],
                               src3[:, kt, sl], start=(kt == 0), stop=(kt == kt_tot - 1))
                for kt in range(kt_tot):
                    mm = b.t.matmul(scr.pbig[:, 1, :], wb[:, kt, ft * 128:(ft + 1) * 128],
                                    src3[:, kt, sl], start=(kt == 0), stop=(kt == kt_tot - 1))
                ti = b.inc(mm, "se_t")
                b.v.wait_ge(b.sems["se_t"][0], ti)
                b.v.tensor_tensor(scr.tmpa[:, :], scr.pbig[:, 0, :], rc[:, sl], op=ALU.mult)
                b.v.tensor_tensor(scr.tmpb[:, :], scr.pbig[:, 1, :], rs[:, sl], op=ALU.mult)
                vi = b.inc(b.v.tensor_tensor(dst[:, ft, sl], scr.tmpa[:, :], scr.tmpb[:, :],
                                             op=ALU.add), "se_v")
                scr.prel("pb0", "se_v", vi)
                scr.prel("pb1", "se_v", vi)

    for bt in range(BPC):
        sc_ln = ExitStack()
        ln = b.sb("a_ln", [128, DT, S], BF16, sc_ln)
        ln_from_hbm(b, K, scr, xin, bt, ln)
        rb_attn = ExitStack()
        attn = b.sb("a_attn", [128, DT, S], BF16, rb_attn, side="right")
        rb2 = ExitStack()
        q = b.sb("a_q", [128, DT, S], BF16, rb2, side="right")
        scw = ExitStack()
        rc = b.sb("a_rc", [128, S], F32, scw)
        rs = b.sb("a_rs", [128, S], F32, scw)
        wq = b.sb("a_wq", [128, DT, D], BF16, scw)
        wqs = b.sb("a_wqs", [128, DT, D], BF16, scw)
        b.load(rc[:], K.aps["rope_cos"])
        b.load(rs[:], K.aps["rope_sin"])
        load_wbf(b, scr, wq, Wap["wqT"], DT, D)
        gqs = load_wbf(b, scr, wqs, Wap["wqTs"], DT, D)
        b.t.wait_ge(b.sems["se_g"][0], gqs)
        rope_pair(wq, wqs, ln, DT, q, rc, rs)
        b.barrier()
        scw.close()
        lat = b.sb("a_lat", [128, 4, S], BF16, rb2, side="right")
        scw = ExitStack()
        kvd = b.sb("a_kvd", [128, DT, 512], BF16, scw)
        gl = load_wbf(b, scr, kvd, Wap["kvdT"], DT, 512)
        b.t.wait_ge(b.sems["se_g"][0], gl)
        for ft in range(4):
            for c in range(4):
                sl = slice(c * 512, (c + 1) * 512)
                slot = (ft * 4 + c) % 3
                key = f"pb{slot}"
                scr.pacq(b, key)
                for kt in range(DT):
                    mm = b.t.matmul(scr.pbig[:, slot, :], kvd[:, kt, ft * 128:(ft + 1) * 128],
                                    ln[:, kt, sl], start=(kt == 0), stop=(kt == DT - 1))
                ti = b.inc(mm, "se_t")
                b.s.wait_ge(b.sems["se_t"][0], ti)
                si = b.inc(b.s.copy(lat[:, ft, sl], scr.pbig[:, slot, :]), "se_s")
                scr.prel(key, "se_s", si)
        b.barrier()
        scw.close()
        sc_ln.close()
        k = b.sb("a_k", [128, DT, S], BF16, rb2, side="right")
        scw = ExitStack()
        rc = b.sb("a_rc2", [128, S], F32, scw)
        rs = b.sb("a_rs2", [128, S], F32, scw)
        kup = b.sb("a_kup", [128, 4, D], BF16, scw)
        kups = b.sb("a_kups", [128, 4, D], BF16, scw)
        b.load(rc[:], K.aps["rope_cos"])
        b.load(rs[:], K.aps["rope_sin"])
        load_wbf(b, scr, kup, Wap["kupT"], 4, D)
        gks = load_wbf(b, scr, kups, Wap["kupTs"], 4, D)
        b.t.wait_ge(b.sems["se_g"][0], gks)
        rope_pair(kup, kups, lat, 4, k, rc, rs)
        b.barrier()
        scw.close()
        v1 = b.sb("a_v1", [128, 16, 16, 64], BF16, rb2, side="right")
        scw = ExitStack()
        vup = b.sb("a_vup", [128, 4, D], BF16, scw)
        gv = load_wbf(b, scr, vup, Wap["vupT"], 4, D)
        b.t.wait_ge(b.sems["se_g"][0], gv)
        for st_ in range(16):
            for fg in range(2):
                slot = (st_ * 2 + fg) % 3
                key = f"pb{slot}"
                scr.pacq(b, key)
                for lt in range(4):
                    mm = b.t.matmul(scr.pbig[:, slot, :],
                                    lat[:, lt, st_ * 128:(st_ + 1) * 128],
                                    vup[:, lt, fg * 512:(fg + 1) * 512],
                                    start=(lt == 0), stop=(lt == 3))
                ti = b.inc(mm, "se_t")
                b.s.wait_ge(b.sems["se_t"][0], ti)
                si = b.inc(b.s.copy(
                    v1[:, st_, fg * 8:(fg + 1) * 8, :],
                    scr.pbig[:, slot, :].rearrange("p (h f) -> p h f", f=64)), "se_s")
                scr.prel(key, "se_s", si)
        b.t.wait_ge(b.sems["se_s"][0], si)
        b.barrier()
        scw.close()
        for h in range(16):
            ft, pb = h // 2, (h % 2) * 64
            for qc in range(4):
                qsl = slice(qc * 512, (qc + 1) * 512)
                ktmax = 4 * qc + 4
                scr.pacq(b, "paux0")
                for kt in range(ktmax):
                    slot = kt % 2
                    key = f"pb{slot}"
                    scr.pacq(b, key)
                    mm = b.t.matmul(scr.pbig[:, slot, :],
                                    k[pb:pb + 64, ft, kt * 128:(kt + 1) * 128],
                                    q[pb:pb + 64, ft, qsl], start=True, stop=True)
                    ti = b.inc(mm, "se_t")
                    if kt >= 4 * qc:
                        o = 384 - 128 * (kt - 4 * qc)
                        b.v.wait_ge(b.sems["se_t"][0], ti)
                        vi = b.inc(b.v.tensor_tensor(scr.pbig[:, slot, :],
                                                     scr.pbig[:, slot, :],
                                                     mc[:, o:o + 512], op=ALU.add), "se_v")
                        b.s.wait_ge(b.sems["se_v"][0], vi)
                    else:
                        b.s.wait_ge(b.sems["se_t"][0], ti)
                    si = b.inc(b.s.activation(Ebuf[:, slot, :], scr.pbig[:, slot, :],
                                              AF.Exp, scale=0.125), "se_s")
                    scr.prel(key, "se_s", si)
                    b.t.wait_ge(b.sems["se_s"][0], si)
                    if kt == 0:
                        scr.pacq(b, "pst0")
                    b.t.matmul(scr.pst[:, 0, :], K.ones_c_bf16[:], Ebuf[:, slot, :],
                               start=(kt == 0), stop=(kt == ktmax - 1))
                    mm = b.t.matmul(scr.paux[pb:pb + 64, 0, :], v1[:, kt, h, :],
                                    Ebuf[:, slot, :],
                                    start=(kt == 0), stop=(kt == ktmax - 1))
                tpv = b.inc(mm, "se_t")
                b.v.wait_ge(b.sems["se_t"][0], tpv)
                vz = b.inc(b.v.reciprocal(scr.ab[0:1, 0, 0:512], scr.pst[:, 0, :]),
                           "se_v")
                scr.prel("pst0", "se_v", vz)
                b.t.wait_ge(b.sems["se_v"][0], vz)
                scr.pacq(b, "paux1")
                ti = b.inc(b.t.matmul(scr.paux[pb:pb + 64, 1, :], K.ones_r_f32[:, 0:64],
                                      scr.ab[0:1, 0, 0:512], start=True, stop=True), "se_t")
                b.s.wait_ge(b.sems["se_t"][0], ti)
                si = b.inc(b.s.copy(bz[pb:pb + 64, :], scr.paux[pb:pb + 64, 1, :]), "se_s")
                scr.prel("paux1", "se_s", si)
                b.v.wait_ge(b.sems["se_s"][0], si)
                vi = b.inc(b.v.tensor_tensor(attn[pb:pb + 64, ft, qsl],
                                             scr.paux[pb:pb + 64, 0, :], bz[pb:pb + 64, :],
                                             op=ALU.mult), "se_v")
                scr.prel("paux0", "se_v", vi)
        b.barrier()
        rb2.close()
        scw = ExitStack()
        wo = b.sb("a_wo", [128, DT, D], BF16, scw)
        x2 = b.sb("a_x2", [128, DT, S], F32, scw)
        go = load_wbf(b, scr, wo, Wap["woT"], DT, D)
        b.wait(b.y, "se_v")
        b.drain_st()
        tx2 = b.load(x2[:], xap3(xin, bt))
        b.t.wait_ge(b.sems["se_g"][0], go)
        b.wait_ld(b.v, tx2)
        for c in range(4):
            sl = slice(c * 512, (c + 1) * 512)
            for ftd in range(DT):
                slot = (ftd * 4 + c) % 3
                key = f"pb{slot}"
                scr.pacq(b, key)
                for kt in range(DT):
                    mm = b.t.matmul(scr.pbig[:, slot, :], wo[:, kt, ftd * 128:(ftd + 1) * 128],
                                    attn[:, kt, sl], start=(kt == 0), stop=(kt == DT - 1))
                ti = b.inc(mm, "se_t")
                b.v.wait_ge(b.sems["se_t"][0], ti)
                vi = b.inc(b.v.tensor_tensor(x2[:, ftd, sl], x2[:, ftd, sl],
                                             scr.pbig[:, slot, :], op=ALU.add), "se_v")
                scr.prel(key, "se_v", vi)
        b.wait(b.y, "se_v")
        b.store(xap3(xout, bt), x2[:])
        b.barrier()
        scw.close()
        rb_attn.close()
    sc.close()



def emit_l1a(b, K, scr, Wap, xin, xout):
    sc = ExitStack()
    b.drain_st()
    ln = b.sb("c_ln", [128, DT, S], BF16, sc)
    x_sb = b.sb("c_x", [128, DT, S], F32, sc)
    xc = b.sb("c_xc", [128, S], BF16, sc)
    sq = b.sb("c_sq", [128, S], BF16, sc)
    nd = b.sb("c_nd", [1, S], F32, sc)
    x1 = b.sb("c_x1", [128, 16, 64], BF16, sc)
    Ebuf = b.sb("c_E", [128, 2, 512], BF16, sc)
    bz = b.sb("c_bz", [128, 512], F32, sc)
    mrow = b.sb("c_m", [128, 1], F32, sc)

    for bt in range(BPC):
        tx = b.load(x_sb[:], xap3(xin, bt))
        for e in (b.s, b.t, b.v):
            b.wait_ld(e, tx)
        emit_ln(b, K, scr, x_sb, ln, S)
        for h in range(16):
            ft, pb = h // 2, (h % 2) * 64
            b.vself(b.v.reduce_sum(mrow[pb:pb + 64, :], ln[pb:pb + 64, ft, :], axis=AX.X,
                                   negate=True))
            b.vself(b.v.tensor_scalar(mrow[pb:pb + 64, :], mrow[pb:pb + 64, :], 1.0 / S,
                                      None, op0=ALU.mult))
            b.vself(b.v.tensor_scalar(xc[pb:pb + 64, :], ln[pb:pb + 64, ft, :],
                                      mrow[pb:pb + 64, :], None, op0=ALU.add))
            vi = b.vself(b.v.tensor_tensor(sq[pb:pb + 64, :], xc[pb:pb + 64, :],
                                           xc[pb:pb + 64, :], op=ALU.mult))
            b.t.wait_ge(b.sems["se_v"][0], vi)
            for c in range(4):
                sl = slice(c * 512, (c + 1) * 512)
                scr.pacq(b, "pst0")
                mm = b.t.matmul(scr.pst[:, 0, :], K.ones_c_bf16[pb:pb + 64, :],
                                sq[pb:pb + 64, sl], start=True, stop=True)
                ti = b.inc(mm, "se_t")
                b.v.wait_ge(b.sems["se_t"][0], ti)
                vi = b.inc(b.v.tensor_scalar(nd[0:1, sl], scr.pst[:, 0, :], -1.0, None,
                                             op0=ALU.mult), "se_v")
                scr.prel("pst0", "se_v", vi)
            scr.pacq(b, "pbf")
            for Tt in range(16):
                mm = b.t.transpose(scr.pbf[:, Tt * 64:(Tt + 1) * 64],
                                   ln[pb:pb + 64, ft, Tt * 128:(Tt + 1) * 128],
                                   K.ident_bf16[pb:pb + 64, pb:pb + 64])
            ti = b.inc(mm, "se_t")
            b.s.wait_ge(b.sems["se_t"][0], ti)
            sx1 = b.inc(b.s.copy(x1[:, :, :],
                                 scr.pbf[:].rearrange("p (t f) -> p t f", f=64)), "se_s")
            scr.prel("pbf", "se_s", sx1)
            b.t.wait_ge(b.sems["se_s"][0], sx1)
            b.t.wait_ge(b.sems["se_v"][0], vi)
            for tc in range(4):
                tsl = slice(tc * 512, (tc + 1) * 512)
                scr.pacq(b, "paux0")
                for Tt in range(16):
                    slot = Tt % 2
                    key = f"pb{slot}"
                    scr.pacq(b, key)
                    b.t.matmul(scr.pbig[:, slot, :], xc[pb:pb + 64, Tt * 128:(Tt + 1) * 128],
                               xc[pb:pb + 64, tsl], start=True, stop=False)
                    mm = b.t.matmul(scr.pbig[:, slot, :], K.ones_r_f32[:], nd[0:1, tsl],
                                    start=False, stop=True)
                    ti = b.inc(mm, "se_t")
                    b.s.wait_ge(b.sems["se_t"][0], ti)
                    si = b.inc(b.s.activation(Ebuf[:, slot, :], scr.pbig[:, slot, :],
                                              AF.Exp), "se_s")
                    scr.prel(key, "se_s", si)
                    b.t.wait_ge(b.sems["se_s"][0], si)
                    if Tt == 0:
                        scr.pacq(b, "pst0")
                    b.t.matmul(scr.pst[:, 0, :], K.ones_c_bf16[:], Ebuf[:, slot, :],
                               start=(Tt == 0), stop=(Tt == 15))
                    mm = b.t.matmul(scr.paux[pb:pb + 64, 0, :], x1[:, Tt, :],
                                    Ebuf[:, slot, :], start=(Tt == 0), stop=(Tt == 15))
                tpv = b.inc(mm, "se_t")
                b.v.wait_ge(b.sems["se_t"][0], tpv)
                vz = b.inc(b.v.reciprocal(scr.ab[0:1, 0, 0:512], scr.pst[:, 0, :]),
                           "se_v")
                scr.prel("pst0", "se_v", vz)
                b.t.wait_ge(b.sems["se_v"][0], vz)
                scr.pacq(b, "paux1")
                ti = b.inc(b.t.matmul(scr.paux[pb:pb + 64, 1, :], K.ones_r_f32[:, 0:64],
                                      scr.ab[0:1, 0, 0:512], start=True, stop=True), "se_t")
                b.s.wait_ge(b.sems["se_t"][0], ti)
                si = b.inc(b.s.copy(bz[pb:pb + 64, :], scr.paux[pb:pb + 64, 1, :]), "se_s")
                scr.prel("paux1", "se_s", si)
                b.v.wait_ge(b.sems["se_s"][0], si)
                b.v.tensor_tensor(scr.tmpa[pb:pb + 64, :], scr.paux[pb:pb + 64, 0, :],
                                  bz[pb:pb + 64, :], op=ALU.mult)
                vi = b.inc(b.v.tensor_tensor(x_sb[pb:pb + 64, ft, tsl],
                                             x_sb[pb:pb + 64, ft, tsl],
                                             scr.tmpa[pb:pb + 64, :], op=ALU.add), "se_v")
                scr.prel("paux0", "se_v", vi)
        b.wait(b.y, "se_v")
        b.store(xap3(xout, bt), x_sb[:])
        b.barrier()
    sc.close()



def emit_gla(b, K, scr, Wap, xin, xout):
    sc = ExitStack()
    b.drain_st()
    wq_ = b.sb("g_wq", [128, DT, D], BF16, sc)
    wk_ = b.sb("g_wk", [128, DT, D], BF16, sc)
    wv_ = b.sb("g_wv", [128, DT, 2048], BF16, sc)
    wog = b.sb("g_wog", [128, DT, 2048], BF16, sc)
    wo_ = b.sb("g_wo", [128, 16, D], BF16, sc)
    wg_ = b.sb("g_wg", [128, DT, 8], F32, sc)
    bg16 = b.sb("g_bg16", [8, 1], F32, sc)
    csg = b.sb("g_csg", [8, 1], F32, sc)
    xc_ = b.sb("g_xc", [128, DT, 128], F32, sc)
    lnc = b.sb("g_lnc", [128, DT, 128], BF16, sc)
    qh = b.sb("g_qh", [128, 128], BF16, sc)
    kh = b.sb("g_kh", [128, 128], BF16, sc)
    ktl = b.sb("g_ktl", [128, 128], BF16, sc)
    kd = b.sb("g_kd", [128, 128], BF16, sc)
    vh = b.sb("g_vh", [128, 256], BF16, sc)
    ogh = b.sb("g_ogh", [128, 256], BF16, sc)
    gt_ = b.sb("g_gt", [128, 256], BF16, sc)
    S0 = b.sb("g_S0", [128, 2, 8, 256], BF16, sc)
    Lf = b.sb("g_Lf", [8, S], F32, sc)
    Ltok = b.sb("g_Ltok", [128, 8], F32, sc)
    nLtok = b.sb("g_nLtok", [128, 8], F32, sc)
    Lrow = b.sb("g_Lrow", [1, 128], F32, sc)
    lcsave = b.sb("g_lcs", [1, 8, 1], F32, sc)
    lcend = b.sb("g_lce", [1, 8, 1], F32, sc)
    lc0 = b.sb("g_lc0", [128, 2], F32, sc)
    Dbf = b.sb("g_D", [128, 128], BF16, sc)
    Msb = b.sb("g_M", [128, 128], BF16, sc)
    osb = b.sb("g_osb", [128, 256], BF16, sc)
    intr = b.sb("g_intr", [128, 256], F32, sc)
    are = b.sb("g_are", [128, 3], F32, sc)
    gT = b.sb("g_gT", [128, 16, 128], BF16, sc)
    mtri = b.sb("g_mtri", [128, 128], F32, sc)

    b.load(mtri[:], K.aps["mask_tri"])
    load_wbf(b, scr, wq_, Wap["gWq"], DT, D)
    load_wbf(b, scr, wk_, Wap["gWk"], DT, D)
    load_wbf(b, scr, wv_, Wap["gWv"], DT, 2048)
    load_wbf(b, scr, wog, Wap["gWog"], DT, 2048)
    go = load_wbf(b, scr, wo_, Wap["gWo"], 16, D)
    tg = b.load(wg_[:], Wap["gWg"])
    tbg = b.load(bg16[:], Wap["gbg"])
    b.wait_ld(b.s, tbg)
    si = b.inc(b.s.mul(bg16[:], bg16[:], 1.0 / 16.0), "se_s")
    b.wait_ld(b.t, tg)
    scr.pacq(b, "pst0")
    for kt in range(DT):
        mm = b.t.matmul(scr.pst[:, 0, 0:8], K.ones_c_f32[:], wg_[:, kt, :],
                        start=(kt == 0), stop=(kt == DT - 1))
    ti = b.inc(mm, "se_t")
    b.s.wait_ge(b.sems["se_t"][0], ti)
    si = b.inc(b.s.copy(scr.ab[0:1, 0, 0:8], scr.pst[:, 0, 0:8]), "se_s")
    scr.prel("pst0", "se_s", si)
    b.t.wait_ge(b.sems["se_s"][0], si)
    scr.pacq(b, "pb0")
    ti = b.inc(b.t.transpose(scr.pbig[0:8, 0, 0:1], scr.ab[0:1, 0, 0:8],
                             K.ident_f32[0:1, 0:1]), "se_t")
    b.s.wait_ge(b.sems["se_t"][0], ti)
    si = b.inc(b.s.copy(csg[:], scr.pbig[0:8, 0, 0:1]), "se_s")
    scr.prel("pb0", "se_s", si)
    b.t.wait_ge(b.sems["se_g"][0], go)

    def pmm(slot, out_ap, mk_lhsT, mk_rhs, nkt=DT, consumer="se_s"):
        key = f"pb{slot}"
        scr.pacq(b, key)
        for kt in range(nkt):
            mm = b.t.matmul(out_ap, mk_lhsT(kt), mk_rhs(kt),
                            start=(kt == 0), stop=(kt == nkt - 1))
        return b.inc(mm, "se_t"), key

    for bt in range(BPC):
        loga = Lf
        b.drain_st()
        vz = b.inc(b.v.memset(S0[:, 0, :, :], 0.0), "se_v")
        vz = b.inc(b.v.memset(lcsave[:], 0.0), "se_v")
        b.t.wait_ge(b.sems["se_v"][0], vz)
        vS = vz
        lcs_tok = None
        for ch in range(16):
            csl = slice(ch * 128, (ch + 1) * 128)
            b.wait(b.y, "se_v")
            tx = b.load(xc_[:], xap3(xin, bt)[:, :, csl])
            for e in (b.s, b.t, b.v):
                b.wait_ld(e, tx)
            scr.pacq(b, "pb2")
            for kt in range(DT):
                mm = b.t.matmul(scr.pbig[0:8, 2, 0:128], wg_[:, kt, :], xc_[:, kt, :],
                                start=(kt == 0), stop=(kt == DT - 1))
            tig = b.inc(mm, "se_t")
            emit_ln(b, K, scr, xc_, lnc, 128, apply=False)
            b.v.wait_ge(b.sems["se_t"][0], tig)
            b.v.wait_ge(b.sems["se_s"][0], b.val("se_s"))
            b.vself(b.v.tensor_tensor(scr.tmpa[0:8, 0:128], scr.pbig[0:8, 2, 0:128],
                                      scr.Abc[0:8, 0:128], op=ALU.mult))
            vi = b.vself(b.v.scalar_tensor_tensor(scr.tmpa[0:8, 0:128], scr.Bbc[0:8, 0:128],
                                                  csg[:, 0:1], scr.tmpa[0:8, 0:128],
                                                  op0=ALU.mult, op1=ALU.add))
            scr.prel("pb2", "se_v", vi)
            if getattr(K, "dbg_gla_x1", None) is not None:
                b.wait(b.y, "se_v")
                b.store(K.dbg_gla_x1[bt, ch], xc_[:])
                b.store(K.dbg_gla_lg[bt, ch], scr.tmpa[0:8, 0:128])
                b.wait(b.s, "st")
                b.wait(b.v, "st")
            b.s.wait_ge(b.sems["se_v"][0], vi)
            b.sself(b.s.activation(scr.tmpb[0:8, 0:128], scr.tmpa[0:8, 0:128], AF.Sigmoid,
                                   scale=1.0 / 16.0, bias=bg16[:]))
            sig = b.sself(b.s.activation(loga[:, csl], scr.tmpb[0:8, 0:128], AF.Ln))
        b.v.wait_ge(b.sems["se_s"][0], sig)
        vL = b.inc(b.v.tensor_tensor_scan(Lf[:], K.ones_row2k[:], loga[:], 0.0,
                                          op0=ALU.mult, op1=ALU.add), "se_v")
        b.t.wait_ge(b.sems["se_v"][0], vL)
        if getattr(K, "dbg_gla_Lf", None) is not None:
            b.wait(b.y, "se_v")
            b.store(K.dbg_gla_Lf[bt], Lf[:])
            b.wait(b.s, "st")
        for ch in range(16):
            csl = slice(ch * 128, (ch + 1) * 128)
            par, oth = ch % 2, 1 - ch % 2
            b.wait(b.y, "se_v")
            tx = b.load(xc_[:], xap3(xin, bt)[:, :, csl])
            for e in (b.s, b.t, b.v):
                b.wait_ld(e, tx)
            vln = emit_ln(b, K, scr, xc_, lnc, 128)
            b.t.wait_ge(b.sems["se_v"][0], vln)
            b.t.wait_ge(b.sems["se_v"][0], vS)
            if lcs_tok is not None:
                b.t.wait_ge(b.sems["se_s"][0], lcs_tok)
            scr.pacq(b, "pb0")
            t1 = b.inc(b.t.transpose(scr.pbig[:, 0, 0:8], Lf[:, csl],
                                     K.ident_f32[0:8, 0:8]), "se_t")
            b.s.wait_ge(b.sems["se_t"][0], t1)
            s1 = b.inc(b.s.copy(Ltok[:], scr.pbig[:, 0, 0:8]), "se_s")
            scr.prel("pb0", "se_s", s1)
            b.v.wait_ge(b.sems["se_s"][0], s1)
            vnl = b.vself(b.v.tensor_scalar(nLtok[:], Ltok[:], -1.0, None, op0=ALU.mult))
            b.v.wait_ge(b.sems["se_v"][0], vnl)
            for hh in range(8):
                hs4 = slice(hh * 256, (hh + 1) * 256)
                scr.pacq(b, "pb0")
                t2 = b.inc(b.t.transpose(scr.pbig[0:1, 0, 0:128], Ltok[:, hh:hh + 1],
                                         K.ident_f32[:]), "se_t")
                b.s.wait_ge(b.sems["se_t"][0], t2)
                s2 = b.inc(b.s.copy(Lrow[:], scr.pbig[0:1, 0, 0:128]), "se_s")
                s2 = b.inc(b.s.copy(lcend[0:1, hh, :], Lrow[0:1, 127:128]), "se_s")
                scr.prel("pb0", "se_s", s2)
                b.t.wait_ge(b.sems["se_s"][0], s2)
                scr.pacq(b, "pb2")
                b.t.matmul(scr.pbig[:, 2, 0:1], K.ones_r_f32[:], lcsave[0:1, hh, 0:1],
                           start=True, stop=True)
                tiE = b.inc(b.t.matmul(scr.pbig[:, 2, 1:2], K.ones_r_f32[:],
                                       Lrow[0:1, 127:128], start=True, stop=True), "se_t")
                b.s.wait_ge(b.sems["se_t"][0], tiE)
                sE = b.inc(b.s.copy(lc0[:, 0:2], scr.pbig[:, 2, 0:2]), "se_s")
                scr.prel("pb2", "se_s", sE)
                b.v.wait_ge(b.sems["se_s"][0], sE)
                b.vself(b.v.tensor_tensor(scr.tmpa[:, 0:1], Ltok[:, hh:hh + 1], lc0[:, 0:1],
                                          op=ALU.subtract))
                b.vself(b.v.tensor_tensor(scr.tmpa[:, 1:2], lc0[:, 1:2], Ltok[:, hh:hh + 1],
                                          op=ALU.subtract))
                vi = b.vself(b.v.tensor_tensor(scr.tmpa[:, 2:3], lc0[:, 1:2], lc0[:, 0:1],
                                               op=ALU.subtract))
                b.s.wait_ge(b.sems["se_v"][0], vi)
                sA = b.inc(b.s.activation(are[:, 0:3], scr.tmpa[:, 0:3], AF.Exp), "se_s")
                if getattr(K, "dbg_gla_are", None) is not None:
                    b.wait(b.y, "se_s")
                    b.store(K.dbg_gla_are[bt, ch, hh], are[:, 0:3])
                    b.store(K.dbg_gla_lc[bt, ch, hh], lc0[:, 0:2])
                    b.wait(b.s, "st")
                    b.wait(b.v, "st")
                ti, kq = pmm(0, scr.pbig[:, 0, 0:128],
                             lambda kt: wq_[:, kt, hh * 128:(hh + 1) * 128],
                             lambda kt: lnc[:, kt, :])
                b.s.wait_ge(b.sems["se_t"][0], ti)
                sq_ = b.inc(b.s.copy(qh[:], scr.pbig[:, 0, 0:128]), "se_s")
                scr.prel(kq, "se_s", sq_)
                ti, kk = pmm(1, scr.pbig[:, 1, 0:128],
                             lambda kt: lnc[:, kt, :],
                             lambda kt: wk_[:, kt, hh * 128:(hh + 1) * 128])
                b.s.wait_ge(b.sems["se_t"][0], ti)
                sk_ = b.inc(b.s.activation(kh[:], scr.pbig[:, 1, 0:128], AF.Identity,
                                           scale=KSCALE), "se_s")
                scr.prel(kk, "se_s", sk_)
                ti, kv = pmm(2, scr.pbig[:, 2, 0:256],
                             lambda kt: lnc[:, kt, :],
                             lambda kt: wv_[:, kt, hh * 256:(hh + 1) * 256])
                b.s.wait_ge(b.sems["se_t"][0], ti)
                sv_ = b.inc(b.s.copy(vh[:], scr.pbig[:, 2, 0:256]), "se_s")
                scr.prel(kv, "se_s", sv_)
                ti, ko = pmm(0, scr.pbig[:, 0, 0:256],
                             lambda kt: lnc[:, kt, :],
                             lambda kt: wog[:, kt, hh * 256:(hh + 1) * 256])
                b.s.wait_ge(b.sems["se_t"][0], ti)
                so_ = b.inc(b.s.activation(ogh[:], scr.pbig[:, 0, 0:256], AF.Sigmoid),
                            "se_s")
                scr.prel(ko, "se_s", so_)
                b.t.wait_ge(b.sems["se_s"][0], sk_)
                scr.pacq(b, "pbf")
                ti = b.inc(b.t.transpose(scr.pbf[:, 0:128], kh[:], K.ident_bf16[:]),
                           "se_t")
                b.s.wait_ge(b.sems["se_t"][0], ti)
                skd = b.inc(b.s.copy(kd[:], scr.pbf[:, 0:128]), "se_s")
                scr.prel("pbf", "se_s", skd)
                b.t.wait_ge(b.sems["se_s"][0], skd)
                scr.pacq(b, "pb0")
                b.t.matmul(scr.pbig[:, 0, 0:128], kd[:], qh[:], start=True, stop=True)
                scr.pacq(b, "pb1")
                tiD = b.inc(b.t.matmul(scr.pbig[:, 1, 0:128], K.ones_r_f32[:],
                                       Lrow[:], start=True, stop=True), "se_t")
                b.v.wait_ge(b.sems["se_t"][0], tiD)
                viD = b.vself(b.v.tensor_tensor(scr.pbig[:, 1, 0:128], scr.pbig[:, 1, 0:128],
                                                mtri[:], op=ALU.add))
                b.s.wait_ge(b.sems["se_v"][0], viD)
                siD = b.inc(b.s.activation(Dbf[:], scr.pbig[:, 1, 0:128], AF.Exp,
                                           bias=nLtok[:, hh:hh + 1]), "se_s")
                scr.prel("pb1", "se_s", siD)
                b.v.wait_ge(b.sems["se_s"][0], siD)
                viM = b.vself(b.v.tensor_tensor(Msb[:], scr.pbig[:, 0, 0:128], Dbf[:],
                                                op=ALU.mult))
                scr.prel("pb0", "se_v", viM)
                b.t.wait_ge(b.sems["se_v"][0], viM)
                scr.pacq(b, "paux0")
                tiI = b.inc(b.t.matmul(scr.paux[:, 0, 0:256], Msb[:], vh[:],
                                       start=True, stop=True), "se_t")
                b.t.wait_ge(b.sems["se_s"][0], sA)
                scr.pacq(b, "paux1")
                tiQ = b.inc(b.t.matmul(scr.paux[:, 1, 0:256], qh[:],
                                       S0[:, par, hh, :], start=True, stop=True), "se_t")
                b.s.wait_ge(b.sems["se_t"][0], tiI)
                siI = b.inc(b.s.copy(intr[:], scr.paux[:, 0, 0:256]), "se_s")
                scr.prel("paux0", "se_s", siI)
                b.v.wait_ge(b.sems["se_s"][0], siI)
                b.v.wait_ge(b.sems["se_t"][0], tiQ)
                vo = b.inc(b.v.scalar_tensor_tensor(osb[:], scr.paux[:, 1, 0:256],
                                                    are[:, 0:1], intr[:],
                                                    op0=ALU.mult, op1=ALU.add), "se_v")
                scr.prel("paux1", "se_v", vo)
                vg = b.inc(b.v.tensor_tensor(gt_[:], osb[:], ogh[:], op=ALU.mult), "se_v")
                vk = b.inc(b.v.tensor_scalar(ktl[:], kh[:], are[:, 1:2], None,
                                             op0=ALU.mult), "se_v")
                b.t.wait_ge(b.sems["se_v"][0], vk)
                scr.pacq(b, "paux0")
                tiU = b.inc(b.t.matmul(scr.paux[:, 0, 0:256], ktl[:], vh[:],
                                       start=True, stop=True), "se_t")
                b.v.wait_ge(b.sems["se_t"][0], tiU)
                vS = b.vself(b.v.scalar_tensor_tensor(S0[:, oth, hh, :], S0[:, par, hh, :],
                                                      are[:, 2:3], scr.paux[:, 0, 0:256],
                                                      op0=ALU.mult, op1=ALU.add))
                scr.prel("paux0", "se_v", vS)
                b.t.wait_ge(b.sems["se_v"][0], vg)
                for sub in range(2):
                    scr.pacq(b, "pbf")
                    ti = b.inc(b.t.transpose(scr.pbf[:, 128:256],
                                             gt_[:, sub * 128:(sub + 1) * 128],
                                             K.ident_bf16[:]), "se_t")
                    b.s.wait_ge(b.sems["se_t"][0], ti)
                    sgT = b.inc(b.s.copy(gT[:, hh * 2 + sub, :],
                                         scr.pbf[:, 128:256]), "se_s")
                    scr.prel("pbf", "se_s", sgT)
            vlc = b.vself(b.v.tensor_copy(lcsave[:], lcend[:]))
            lcs_tok = None
            b.t.wait_ge(b.sems["se_v"][0], vlc)
            if True:
                scg = ExitStack()
                xg = b.sb("g_xg", [128, DT, 128], F32, scg)
                b.drain_st()
                txg = b.load(xg[:], xap3(xin, bt)[:, :, csl])
                b.wait_ld(b.v, txg)
                b.t.wait_ge(b.sems["se_s"][0], sgT)
                for dt_ in range(DT):
                    slot = dt_ % 2
                    key = f"pb{slot}"
                    scr.pacq(b, key)
                    for ki in range(16):
                        mm = b.t.matmul(scr.pbig[:, slot, 0:128],
                                        wo_[:, ki, dt_ * 128:(dt_ + 1) * 128],
                                        gT[:, ki, :], start=(ki == 0), stop=(ki == 15))
                    ti = b.inc(mm, "se_t")
                    b.v.wait_ge(b.sems["se_t"][0], ti)
                    vi = b.inc(b.v.tensor_tensor(xg[:, dt_, :], xg[:, dt_, :],
                                                 scr.pbig[:, slot, 0:128], op=ALU.add),
                               "se_v")
                    scr.prel(key, "se_v", vi)
                b.wait(b.y, "se_v")
                b.store(xap3(xout, bt)[:, :, csl], xg[:])
                b.barrier()
                scg.close()
        b.barrier()
    sc.close()



SUB_SEQ = ["l0a", "moe", "hgrn", "moe", "l1a", "l1f", "gla", "l1f"]
EMITS = {"l0a": emit_l0a, "moe": emit_moe, "hgrn": emit_hgrn,
         "l1a": emit_l1a, "l1f": emit_l1ffn, "gla": emit_gla}
WKEYS = {
    "l0a": ["wqT", "wqTs", "kvdT", "kupT", "kupTs", "vupT", "woT"],
    "moe": ["rw", "rb_row", "w1", "b1l", "w2", "b2", "sel16"],
    "hgrn": ["hWiT", "hWfT", "hWgT", "hWoT", "hnw"],
    "l1a": [],
    "l1f": ["fWrT", "fWkT", "fWvT", "fmur", "fmuk"],
    "gla": ["gWq", "gWk", "gWv", "gWo", "gWg", "gbg", "gWog"],
}
_CONSTS = host_constants()
DBG_GLA = False


class KTiles:
    pass


def build_program(calls, pshapes):
    nc = bass.Bass("TRN2", target_bir_lowering=False)
    b = Bld(nc)
    K = KTiles()
    K.aps = {}
    for kk, vv in _CONSTS.items():
        dt = BF16 if vv.dtype == ml_dtypes.bfloat16 else F32
        K.aps[kk] = nc.dram_tensor("c_" + kk, list(vv.shape), dt, kind="ExternalInput").ap()
    K.ident_f32 = b.sb("k_idf", [128, 128], F32)
    K.ident_bf16 = b.sb("k_idb", [128, 128], BF16)
    K.ones_r_f32 = b.sb("k_or", [1, 128], F32)
    K.ones_c_f32 = b.sb("k_ocf", [128, 1], F32)
    K.ones_c_bf16 = b.sb("k_ocb", [128, 1], BF16)
    K.ones_row2k = b.sb("k_o2k", [8, S], BF16)
    K.epsln = b.sb("k_eps", [1, 1], F32)
    for nm in ("ident_f32", "ident_bf16", "ones_r_f32", "ones_c_f32",
               "ones_c_bf16", "ones_row2k", "epsln"):
        b.load(getattr(K, nm)[:], K.aps[nm])
    scr = Scratch(b)
    scr.eps = K.epsln
    if DBG_GLA:
        K.dbg_gla_Lf = nc.dram_tensor("dbg_gla_Lf", [BPC, 8, S], F32,
                                      kind="ExternalOutput").ap()
        K.dbg_gla_are = nc.dram_tensor("dbg_gla_are", [BPC, 16, 8, 128, 3], F32,
                                       kind="ExternalOutput").ap()
        K.dbg_gla_lc = nc.dram_tensor("dbg_gla_lc", [BPC, 16, 8, 128, 2], F32,
                                      kind="ExternalOutput").ap()
        K.dbg_gla_x1 = nc.dram_tensor("dbg_gla_x1", [BPC, 16, 128, DT, 128], F32,
                                      kind="ExternalOutput").ap()
        K.dbg_gla_lg = nc.dram_tensor("dbg_gla_lg", [BPC, 16, 8, 128], F32,
                                      kind="ExternalOutput").ap()
    Wap = {}
    need = set()
    for cname in calls:
        need.update(WKEYS[cname])
    for wk in sorted(need):
        if wk in _CONSTS:
            Wap[wk] = K.aps[wk]
        else:
            Wap[wk] = nc.dram_tensor("w_" + wk, list(pshapes[wk]), F32,
                                     kind="ExternalInput").ap()
    xin = nc.dram_tensor("xin", [BPC, D, S], F32, kind="ExternalInput").ap()
    xout = nc.dram_tensor("xout", [BPC, D, S], F32, kind="ExternalOutput").ap()
    mids = [nc.dram_tensor(f"xmid{i}", [BPC, D, S], F32, kind="Internal").ap()
            for i in range(2)]
    for e in (b.t, b.v, b.s, b.g, b.y):
        b.wait_ld_all(e)
    for i, cname in enumerate(calls):
        src = xin if i == 0 else mids[(i - 1) % 2]
        dst = xout if i == len(calls) - 1 else mids[i % 2]
        EMITS[cname](b, K, scr, Wap, src, dst)
    b.drain_st()
    b.es.close()
    return nc


def in_map_for(calls, p, xshard):
    m = {("c_" + kk): vv for kk, vv in _CONSTS.items()}
    need = set()
    for cname in calls:
        need.update(WKEYS[cname])
    for wk in sorted(need):
        if wk not in _CONSTS:
            m["w_" + wk] = p[wk]
    m["xin"] = xshard
    return m


FUSED = True


def kernel(**inputs):
    p = prep_inputs(inputs)
    pshapes = {kk: vv.shape for kk, vv in p.items()}
    x = np.asarray(inputs["L0_attn_a_x"])
    ncore = 8
    shards = [np.ascontiguousarray(x[c * BPC:(c + 1) * BPC].transpose(0, 2, 1))
              for c in range(ncore)]
    if FUSED:
        calls = SUB_SEQ
        nc = build_program(calls, pshapes)
        in_maps = [in_map_for(calls, p, shards[c]) for c in range(ncore)]
        res = run_bass_kernel_spmd(nc, in_maps, core_ids=list(range(ncore)))
        cur = [res.results[c]["xout"] for c in range(ncore)]
    else:
        progs = {name: build_program([name], pshapes)
                 for name in dict.fromkeys(SUB_SEQ)}
        cur = shards
        for name in SUB_SEQ:
            in_maps = [in_map_for([name], p, cur[c]) for c in range(ncore)]
            res = run_bass_kernel_spmd(progs[name], in_maps,
                                       core_ids=list(range(ncore)))
            cur = [res.results[c]["xout"] for c in range(ncore)]
    outs = [np.ascontiguousarray(np.asarray(cur[c]).transpose(0, 2, 1))
            for c in range(ncore)]
    return np.concatenate(outs, axis=0).astype(np.float32)


# version 06
# speedup vs baseline: 1.1823x; 1.1823x over previous
import sys
sys.path.insert(0, '/opt/trn_rl_repo')
from contextlib import ExitStack

import numpy as np
import ml_dtypes

import concourse.bass as bass
import concourse.mybir as mybir
from concourse.bass_utils import run_bass_kernel_spmd

F32 = mybir.dt.float32
BF16 = mybir.dt.bfloat16
AF = mybir.ActivationFunctionType
ALU = mybir.AluOpType
AX = mybir.AxisListType

S = 2048
D = 1024
DT = D // 128
BPC = 2
NEG = -1.0e30
KSCALE = float(1.0 / np.sqrt(128.0))



def host_constants():
    bf = ml_dtypes.bfloat16
    c = {}
    c["ident_f32"] = np.eye(128, dtype=np.float32)
    c["ident_bf16"] = np.eye(128, dtype=bf)
    c["ones_r_f32"] = np.ones((1, 128), np.float32)
    c["ones_c_f32"] = np.ones((128, 1), np.float32)
    c["ones_c_bf16"] = np.ones((128, 1), bf)
    c["ones_row2k"] = np.ones((8, S), ml_dtypes.bfloat16)
    kp = np.arange(128)[:, None]
    cq = np.arange(896)[None, :]
    c["mask_causal"] = np.where(kp <= cq - 384, 0.0, NEG).astype(np.float32)
    t_ = np.arange(128)
    c["mask_tri"] = np.where(t_[:, None] <= t_[None, :], 0.0, NEG).astype(np.float32)
    hd = 64
    inv = 1.0 / (10000.0 ** (np.arange(0, hd, 2, dtype=np.float64) / hd))
    pos = np.arange(S, dtype=np.float64)
    j = np.arange(128)
    th = pos[None, :] * inv[(j % hd) % 32][:, None]
    c["rope_cos"] = np.cos(th).astype(np.float32)
    sgn = np.where((j % hd) < 32, -1.0, 1.0)[:, None]
    c["rope_sin"] = (sgn * np.sin(th)).astype(np.float32)
    c["epsln"] = np.full((1, 1), 1e-5, np.float32)
    sel = np.zeros((16, 16, 128), np.float32)
    for e in range(16):
        sel[e, e, :] = 1.0
    c["sel16"] = sel
    return c


def _rope_perm():
    j = np.arange(1024)
    h, d = j // 64, j % 64
    sw = np.where(d < 32, d + 32, d - 32)
    return h * 64 + sw


def prep_inputs(inputs):
    f = np.ascontiguousarray
    perm = _rope_perm()
    p = {}
    p["wqT"] = f(inputs["L0_attn_a_wq"].T)
    p["wqTs"] = f(p["wqT"][:, perm])
    p["kvdT"] = f(inputs["L0_attn_a_w_kv_down"].T)
    p["kupT"] = f(inputs["L0_attn_a_w_k_up"].T)
    p["kupTs"] = f(p["kupT"][:, perm])
    p["vupT"] = f(inputs["L0_attn_a_w_v_up"].T)
    p["woT"] = f(inputs["L0_attn_a_wo"].T)
    p["rw"] = f(inputs["L0_ffn_router_w"])
    p["rb_row"] = f(inputs["L0_ffn_router_b"].reshape(1, 16))
    p["w1"] = f(inputs["L0_ffn_w1"])
    p["b1l"] = f(inputs["L0_ffn_b1"].reshape(16, 16, 128).transpose(0, 2, 1))
    p["w2"] = f(inputs["L0_ffn_w2"])
    p["b2"] = f(inputs["L0_ffn_b2"])
    p["hWiT"] = f(inputs["L0_attn_b_Wi"].T)
    p["hWfT"] = f(inputs["L0_attn_b_Wf"].T)
    p["hWgT"] = f(inputs["L0_attn_b_Wg"].T)
    p["hWoT"] = f(inputs["L0_attn_b_Wo"].T)
    p["hnw"] = f(inputs["L0_attn_b_norm_w"].reshape(DT, 128).T)
    p["gWq"] = f(inputs["L1_attn_b_Wq"])
    p["gWk"] = f(inputs["L1_attn_b_Wk"])
    p["gWv"] = f(inputs["L1_attn_b_Wv"])
    p["gWo"] = f(inputs["L1_attn_b_Wo"])
    p["gWg"] = f(inputs["L1_attn_b_Wg"].reshape(DT, 128, 8).transpose(1, 0, 2))
    p["gbg"] = f(inputs["L1_attn_b_bg"].reshape(8, 1))
    p["gWog"] = f(inputs["L1_attn_b_Wog"])
    p["fWrT"] = f(inputs["L1_ffn_W_r"].T)
    p["fWkT"] = f(inputs["L1_ffn_W_k"].T)
    p["fWvT"] = f(inputs["L1_ffn_W_v"].T)
    p["fmur"] = f(inputs["L1_ffn_mu_r"].reshape(DT, 128).T)
    p["fmuk"] = f(inputs["L1_ffn_mu_k"].reshape(DT, 128).T)
    return p




class Bld:
    def __init__(self, nc):
        self.nc = nc
        self.es = ExitStack()
        self.t, self.v, self.s, self.g, self.y = (
            nc.tensor, nc.vector, nc.scalar, nc.gpsimd, nc.sync)
        self.sems = {}
        for n in ("st", "se_t", "se_v", "se_s", "se_g"):
            self.sems[n] = [self.es.enter_context(nc.semaphore(n)), 0]
        self.NLD = 4
        self.ld = [[self.es.enter_context(nc.semaphore(f"ld{i}")), 0]
                   for i in range(self.NLD)]
        self.ld_rr = 0
        self.NLG = 2
        self.ldg = [[self.es.enter_context(nc.semaphore(f"ldg{i}")), 0]
                    for i in range(self.NLG)]
        self.ldg_rr = 0
        self.uid = 0

    def nm(self, base):
        self.uid += 1
        return f"{base}_{self.uid}"

    def inc(self, instr, name, amt=1):
        h = self.sems[name]
        h[1] += amt
        instr.then_inc(h[0], amt)
        return h[1]

    def val(self, name):
        return self.sems[name][1]

    def wait(self, eng, name, v=None):
        h = self.sems[name]
        eng.wait_ge(h[0], h[1] if v is None else v)

    def vself(self, instr):
        v = self.inc(instr, "se_v")
        self.v.wait_ge(self.sems["se_v"][0], v)
        return v

    def sself(self, instr):
        v = self.inc(instr, "se_s")
        self.s.wait_ge(self.sems["se_s"][0], v)
        return v

    def sb(self, base, shape, dtype, scope=None, side="left"):
        return (scope or self.es).enter_context(
            self.nc.sbuf_tensor(self.nm(base), shape, dtype, side=side))

    def ps(self, base, shape, dtype, scope=None):
        return (scope or self.es).enter_context(
            self.nc.psum_tensor(self.nm(base), shape, dtype))

    def load(self, out, in_):
        si = self.ld_rr
        self.ld_rr = (self.ld_rr + 1) % self.NLD
        h = self.ld[si]
        if h[1] > 0:
            self.y.wait_ge(h[0], h[1])
        h[1] += 16
        self.y.dma_start(out=out, in_=in_).then_inc(h[0], 16)
        return (si, h[1])

    def wait_ld(self, eng, tok):
        eng.wait_ge(self.ld[tok[0]][0], tok[1])

    def load_g(self, out, in_):
        si = self.ldg_rr
        self.ldg_rr = (self.ldg_rr + 1) % self.NLG
        h = self.ldg[si]
        if h[1] > 0:
            self.g.wait_ge(h[0], h[1])
        h[1] += 16
        self.g.dma_start(out=out, in_=in_).then_inc(h[0], 16)
        return (si, h[1])

    def wait_ldg(self, eng, tok):
        eng.wait_ge(self.ldg[tok[0]][0], tok[1])

    def wait_ld_all(self, eng):
        for h in self.ld:
            if h[1] > 0:
                eng.wait_ge(h[0], h[1])

    def store(self, out, in_):
        return self.inc(self.y.dma_start(out=out, in_=in_), "st", 16)

    def drain_st(self):
        h = self.sems["st"]
        if h[1] > 0:
            for e in (self.t, self.v, self.s, self.g, self.y):
                e.wait_ge(h[0], h[1])

    def barrier(self):
        self.nc.all_engine_barrier()


class Scratch:
    def __init__(self, b: Bld):
        self.pbig = b.ps("p_big", [128, 3, 512], F32)
        self.paux = b.ps("p_aux", [128, 2, 512], F32)
        self.pst = b.ps("p_st", [1, 2, 512], F32)
        self.pbf = b.ps("p_bf", [128, 1024], BF16)
        self.sq2 = b.sb("s_sq2", [128, 2, 512], BF16)
        self.ab = b.sb("s_ab", [1, 2, S], F32)
        self.mv = b.sb("s_mv", [1, 2, 512], F32)
        self.Abc = b.sb("s_Abc", [128, 512], F32)
        self.Bbc = b.sb("s_Bbc", [128, 512], F32)
        self.tmpa = b.sb("s_tmpa", [128, 512], F32)
        self.tmpb = b.sb("s_tmpb", [128, 512], F32)
        self.stage = b.sb("s_stage", [128, 2, 512], F32)
        self.stg_par = 0
        self.stg_tok = [0, 0]
        self.ptok = {}

    def pacq(self, b: Bld, key):
        tok = self.ptok.get(key)
        if tok is not None:
            b.wait(b.t, tok[0], tok[1])

    def prel(self, key, sem, val):
        self.ptok[key] = (sem, val)


def load_wbf(b: Bld, scr: Scratch, dst, src_ap, kt_total, fdim, tguard=None):
    src3 = src_ap.rearrange("(kt p) f -> p kt f", p=128)
    nh = (fdim + 511) // 512
    gi = None
    if tguard is not None:
        b.g.wait_ge(b.sems["se_t"][0], tguard)
    for k0 in range(kt_total):
        for hh in range(nh):
            f0 = hh * 512
            f1 = min(fdim, f0 + 512)
            par = scr.stg_par
            scr.stg_par = 1 - scr.stg_par
            if scr.stg_tok[par] > 0:
                b.y.wait_ge(b.sems["se_g"][0], scr.stg_tok[par])
            tok = b.load(scr.stage[:, par, 0:f1 - f0], src3[:, k0, f0:f1])
            b.wait_ld(b.g, tok)
            gi = b.inc(b.g.tensor_copy(dst[:, k0, f0:f1], scr.stage[:, par, 0:f1 - f0]),
                       "se_g")
            scr.stg_tok[par] = gi
    return gi



def emit_ln(b, K, scr, x3, ln3, ncols, apply=True, out_off=0):
    cw = min(512, ncols)
    si3 = vi_st = None
    vln = None
    tis = []
    for c0 in range(0, ncols, cw):
        sl = slice(c0, c0 + cw)
        osl = slice(c0 + out_off, c0 + out_off + cw)
        for kt in range(DT):
            if len(tis) >= 2:
                b.s.wait_ge(b.sems["se_t"][0], tis[-2])
            sqs = scr.sq2[:, kt % 2, 0:cw]
            si = b.inc(b.s.activation(sqs, x3[:, kt, sl], AF.Square), "se_s")
            if kt == 0:
                if vi_st is not None:
                    b.t.wait_ge(b.sems["se_v"][0], vi_st)
                scr.pacq(b, "pst0")
                scr.pacq(b, "pst1")
            b.t.matmul(scr.pst[:, 0, 0:cw], K.ones_c_f32[:], x3[:, kt, sl],
                       start=(kt == 0), stop=(kt == DT - 1))
            b.t.wait_ge(b.sems["se_s"][0], si)
            mm = b.t.matmul(scr.pst[:, 1, 0:cw], K.ones_c_bf16[:], sqs,
                            start=(kt == 0), stop=(kt == DT - 1))
            tis.append(b.inc(mm, "se_t"))
        ti2 = tis[-1]
        m, var = scr.mv[0:1, 0, 0:cw], scr.mv[0:1, 1, 0:cw]
        a, bsh = scr.ab[0:1, 0, osl], scr.ab[0:1, 1, osl]
        b.v.wait_ge(b.sems["se_t"][0], ti2)
        b.vself(b.v.tensor_scalar(m, scr.pst[:, 0, 0:cw], 1.0 / D, None, op0=ALU.mult))
        b.vself(b.v.tensor_scalar(var, scr.pst[:, 1, 0:cw], 1.0 / D, None, op0=ALU.mult))
        b.vself(b.v.tensor_tensor(scr.tmpa[0:1, 0:cw], m, m, op=ALU.mult))
        vi_st = b.vself(b.v.tensor_tensor(var, var, scr.tmpa[0:1, 0:cw], op=ALU.subtract))
        scr.prel("pst0", "se_v", vi_st)
        scr.prel("pst1", "se_v", vi_st)
        b.s.wait_ge(b.sems["se_v"][0], vi_st)
        b.sself(b.s.activation(scr.tmpa[0:1, 0:cw], var, AF.Ln, bias=scr.eps[0:1, 0:1]))
        si2 = b.sself(b.s.activation(a, scr.tmpa[0:1, 0:cw], AF.Exp, scale=-0.5))
        b.v.wait_ge(b.sems["se_s"][0], si2)
        vi2 = b.vself(b.v.scalar_tensor_tensor(bsh, m, -1.0, a, op0=ALU.mult, op1=ALU.mult))
        b.t.wait_ge(b.sems["se_v"][0], vi2)
        if si3 is not None:
            b.t.wait_ge(b.sems["se_s"][0], si3)
        scr.pacq(b, "paux0")
        scr.pacq(b, "paux1")
        b.t.matmul(scr.paux[:, 0, 0:cw], K.ones_r_f32[:], a, start=True, stop=True)
        ti3 = b.inc(b.t.matmul(scr.paux[:, 1, 0:cw], K.ones_r_f32[:], bsh,
                               start=True, stop=True), "se_t")
        b.s.wait_ge(b.sems["se_t"][0], ti3)
        b.s.copy(scr.Abc[:, 0:cw], scr.paux[:, 0, 0:cw])
        si3 = b.inc(b.s.copy(scr.Bbc[:, 0:cw], scr.paux[:, 1, 0:cw]), "se_s")
        scr.prel("paux0", "se_s", si3)
        scr.prel("paux1", "se_s", si3)
        if apply:
            b.v.wait_ge(b.sems["se_s"][0], si3)
            for kt in range(DT):
                b.v.tensor_tensor(scr.tmpb[:, 0:cw], x3[:, kt, sl], scr.Abc[:, 0:cw],
                                  op=ALU.mult)
                ins = b.v.tensor_tensor(ln3[:, kt, osl], scr.tmpb[:, 0:cw], scr.Bbc[:, 0:cw],
                                       op=ALU.add)
            vln = b.inc(ins, "se_v")
    return vln


def xap3(x_ap, batch):
    return x_ap[batch].rearrange("(kt p) s -> p kt s", p=128)


def ln_from_hbm(b, K, scr, xin, bt, ln):
    for half in range(2):
        sch = ExitStack()
        xh = b.sb("lnx", [128, DT, 1024], F32, sch)
        b.drain_st()
        tx = b.load(xh[:], xap3(xin, bt)[:, :, half * 1024:(half + 1) * 1024])
        for e in (b.s, b.t, b.v):
            b.wait_ld(e, tx)
        vln = emit_ln(b, K, scr, xh, ln, 1024, out_off=half * 1024)
        b.barrier()
        sch.close()
    return vln



def emit_l1ffn(b, K, scr, Wap, xin, xout):
    sc = ExitStack()
    b.drain_st()
    mu = b.sb("f_mu", [128, 2, DT], F32, sc)
    om = b.sb("f_om", [128, 2, DT], F32, sc)
    wr = b.sb("f_wr", [128, DT, D], BF16, sc)
    wv = b.sb("f_wv", [128, DT, D], BF16, sc)
    ln = b.sb("f_ln", [128, DT, S], BF16, sc)
    xr = b.sb("f_xr", [128, DT, S], BF16, sc)
    rt = b.sb("f_rt", [128, 512], BF16, sc)
    kact = ln

    tmu1 = b.load(mu[:, 0, :], Wap["fmur"])
    tmu2 = b.load(mu[:, 1, :], Wap["fmuk"])
    b.wait_ld(b.v, tmu1)
    b.wait_ld(b.v, tmu2)
    vom = b.inc(b.v.tensor_scalar(om[:], mu[:], -1.0, 1.0, op0=ALU.mult, op1=ALU.add),
                "se_v")
    gr = load_wbf(b, scr, wr, Wap["fWrT"], DT, D)
    gv = load_wbf(b, scr, wv, Wap["fWvT"], DT, D)

    for bt in range(BPC):
        vln = ln_from_hbm(b, K, scr, xin, bt, ln)
        scb = ExitStack()
        xk = b.sb("f_xk", [128, DT, S], BF16, scb)
        wk = b.sb("f_wk", [128, DT, D], BF16, scb)
        gk = load_wbf(b, scr, wk, Wap["fWkT"], DT, D)
        b.v.wait_ge(b.sems["se_v"][0], vom)
        for kt in range(DT):
            for dst, mi in ((xr, 0), (xk, 1)):
                b.v.tensor_scalar(dst[:, kt, :], ln[:, kt, :], mu[:, mi, kt:kt + 1],
                                  None, op0=ALU.mult)
                for c0 in range(0, S, 512):
                    lo = max(c0, 1)
                    hi = c0 + 512
                    b.v.tensor_scalar(scr.tmpb[:, 0:hi - lo], ln[:, kt, lo - 1:hi - 1],
                                      om[:, mi, kt:kt + 1], None, op0=ALU.mult)
                    ins = b.v.tensor_tensor(dst[:, kt, lo:hi], dst[:, kt, lo:hi],
                                            scr.tmpb[:, 0:hi - lo], op=ALU.add)
        vsh = b.inc(ins, "se_v")
        b.t.wait_ge(b.sems["se_v"][0], vsh)
        b.t.wait_ge(b.sems["se_g"][0], gk)
        for c in range(4):
            sl = slice(c * 512, (c + 1) * 512)
            for ft in range(DT):
                slot = (ft * 4 + c) % 3
                key = f"pb{slot}"
                scr.pacq(b, key)
                for kt in range(DT):
                    mm = b.t.matmul(scr.pbig[:, slot, :], wk[:, kt, ft * 128:(ft + 1) * 128],
                                    xk[:, kt, sl], start=(kt == 0), stop=(kt == DT - 1))
                ti = b.inc(mm, "se_t")
                b.s.wait_ge(b.sems["se_t"][0], ti)
                si = b.inc(b.s.activation(scr.tmpa[:, :], scr.pbig[:, slot, :], AF.Relu),
                           "se_s")
                scr.prel(key, "se_s", si)
                b.v.wait_ge(b.sems["se_s"][0], si)
                vi = b.inc(b.v.tensor_tensor(kact[:, ft, sl], scr.tmpa[:, :], scr.tmpa[:, :],
                                             op=ALU.mult), "se_v")
        scb.close()
        scb2 = ExitStack()
        x2 = b.sb("f_x2", [128, DT, S], F32, scb2)
        b.wait(b.y, "se_v")
        b.drain_st()
        tx2 = b.load(x2[:], xap3(xin, bt))
        b.wait_ld(b.v, tx2)
        b.t.wait_ge(b.sems["se_v"][0], vi)
        b.t.wait_ge(b.sems["se_g"][0], gv)
        for c in range(4):
            sl = slice(c * 512, (c + 1) * 512)
            for ft in range(DT):
                slot = (ft + c) % 2
                key = f"pb{slot}"
                scr.pacq(b, key)
                for kt in range(DT):
                    mm = b.t.matmul(scr.pbig[:, slot, :], wr[:, kt, ft * 128:(ft + 1) * 128],
                                    xr[:, kt, sl], start=(kt == 0), stop=(kt == DT - 1))
                ti = b.inc(mm, "se_t")
                b.s.wait_ge(b.sems["se_t"][0], ti)
                b.s.wait_ge(b.sems["se_v"][0], b.val("se_v"))
                si = b.inc(b.s.activation(rt[:], scr.pbig[:, slot, :], AF.Sigmoid), "se_s")
                scr.prel(key, "se_s", si)
                scr.pacq(b, "pb2")
                for kt in range(DT):
                    mm = b.t.matmul(scr.pbig[:, 2, :], wv[:, kt, ft * 128:(ft + 1) * 128],
                                    kact[:, kt, sl], start=(kt == 0), stop=(kt == DT - 1))
                ti = b.inc(mm, "se_t")
                b.v.wait_ge(b.sems["se_t"][0], ti)
                b.v.wait_ge(b.sems["se_s"][0], si)
                b.v.tensor_tensor(scr.tmpa[:, :], scr.pbig[:, 2, :], rt[:], op=ALU.mult)
                vi = b.inc(b.v.tensor_tensor(x2[:, ft, sl], x2[:, ft, sl], scr.tmpa[:, :],
                                             op=ALU.add), "se_v")
                scr.prel("pb2", "se_v", vi)
        b.wait(b.y, "se_v")
        b.store(xap3(xout, bt), x2[:])
        b.barrier()
        scb2.close()
    sc.close()



def emit_hgrn(b, K, scr, Wap, xin, xout):
    sc = ExitStack()
    b.drain_st()
    nw = b.sb("h_nw", [128, DT], F32, sc)
    ln = b.sb("h_ln", [128, DT, S], BF16, sc)
    h = b.sb("h_h", [128, DT, S], BF16, sc)

    b.load(nw[:], Wap["hnw"])

    for bt in range(BPC):
        ln_from_hbm(b, K, scr, xin, bt, ln)
        scb = ExitStack()
        wi = b.sb("h_wi", [128, DT, D], BF16, scb)
        wf = b.sb("h_wf", [128, DT, D], BF16, scb)
        sit = b.sb("h_si", [128, S], BF16, scb)
        ftg = b.sb("h_f", [128, S], BF16, scb)
        inp = b.sb("h_inp", [128, S], BF16, scb)
        tmpS = b.sb("h_tmpS", [128, S], F32, scb)
        load_wbf(b, scr, wi, Wap["hWiT"], DT, D)
        gf = load_wbf(b, scr, wf, Wap["hWfT"], DT, D)
        b.t.wait_ge(b.sems["se_g"][0], gf)
        vsc = b.val("se_v")
        for dt_ in range(DT):
            b.s.wait_ge(b.sems["se_v"][0], vsc)
            for c in range(4):
                sl = slice(c * 512, (c + 1) * 512)
                for w_, act, dst, slot in ((wi, AF.Silu, sit, c % 2), (wf, AF.Sigmoid, ftg, 2)):
                    key = f"pb{slot}"
                    scr.pacq(b, key)
                    for kt in range(DT):
                        mm = b.t.matmul(scr.pbig[:, slot, :],
                                        w_[:, kt, dt_ * 128:(dt_ + 1) * 128],
                                        ln[:, kt, sl], start=(kt == 0), stop=(kt == DT - 1))
                    ti = b.inc(mm, "se_t")
                    b.s.wait_ge(b.sems["se_t"][0], ti)
                    si = b.inc(b.s.activation(dst[:, sl], scr.pbig[:, slot, :], act), "se_s")
                    scr.prel(key, "se_s", si)
            b.v.wait_ge(b.sems["se_s"][0], si)
            b.v.tensor_scalar(tmpS[:], ftg[:], -1.0, 1.0, op0=ALU.mult, op1=ALU.add)
            b.v.tensor_tensor(inp[:], sit[:], tmpS[:], op=ALU.mult)
            vsc = b.inc(b.v.tensor_tensor_scan(h[:, dt_, :], ftg[:], inp[:], 0.0,
                                               op0=ALU.mult, op1=ALU.add), "se_v")
        b.barrier()
        scb.close()
        scb = ExitStack()
        Arms = b.sb("h_Arms", [128, 4, 512], F32, scb)
        wg = b.sb("h_wg", [128, DT, D], BF16, scb)
        gg = load_wbf(b, scr, wg, Wap["hWgT"], DT, D)
        b.s.wait_ge(b.sems["se_v"][0], vsc)
        tis = []
        for c in range(4):
            sl = slice(c * 512, (c + 1) * 512)
            for kt in range(DT):
                if len(tis) >= 2:
                    b.s.wait_ge(b.sems["se_t"][0], tis[-2])
                sqs = scr.sq2[:, kt % 2, :]
                si = b.inc(b.s.activation(sqs, h[:, kt, sl], AF.Square), "se_s")
                if kt == 0:
                    scr.pacq(b, "pst1")
                b.t.wait_ge(b.sems["se_s"][0], si)
                mm = b.t.matmul(scr.pst[:, 1, :], K.ones_c_bf16[:], sqs,
                                start=(kt == 0), stop=(kt == DT - 1))
                tis.append(b.inc(mm, "se_t"))
            ti2 = tis[-1]
            b.s.wait_ge(b.sems["se_t"][0], ti2)
            b.sself(b.s.activation(scr.ab[0:1, 0, 0:512], scr.pst[:, 1, :], AF.Ln,
                                   bias=scr.eps[0:1, 0:1], scale=1.0 / D))
            si = b.sself(b.s.activation(scr.ab[0:1, 1, 0:512], scr.ab[0:1, 0, 0:512],
                                        AF.Exp, scale=-0.5))
            scr.prel("pst1", "se_s", si)
            b.t.wait_ge(b.sems["se_s"][0], si)
            scr.pacq(b, "paux0")
            ti3 = b.inc(b.t.matmul(scr.paux[:, 0, :], K.ones_r_f32[:],
                                   scr.ab[0:1, 1, 0:512], start=True, stop=True), "se_t")
            b.s.wait_ge(b.sems["se_t"][0], ti3)
            si = b.inc(b.s.copy(Arms[:, c, :], scr.paux[:, 0, :]), "se_s")
            scr.prel("paux0", "se_s", si)
        b.t.wait_ge(b.sems["se_g"][0], gg)
        b.v.wait_ge(b.sems["se_s"][0], si)
        for dt_ in range(DT):
            for c in range(4):
                sl = slice(c * 512, (c + 1) * 512)
                slot = (dt_ * 4 + c) % 3
                key = f"pb{slot}"
                scr.pacq(b, key)
                for kt in range(DT):
                    mm = b.t.matmul(scr.pbig[:, slot, :], wg[:, kt, dt_ * 128:(dt_ + 1) * 128],
                                    ln[:, kt, sl], start=(kt == 0), stop=(kt == DT - 1))
                ti = b.inc(mm, "se_t")
                b.s.wait_ge(b.sems["se_t"][0], ti)
                b.s.wait_ge(b.sems["se_v"][0], b.val("se_v"))
                si = b.inc(b.s.activation(scr.tmpa[:, :], scr.pbig[:, slot, :], AF.Silu),
                           "se_s")
                scr.prel(key, "se_s", si)
                b.v.wait_ge(b.sems["se_s"][0], si)
                b.v.tensor_tensor(scr.tmpb[:, :], h[:, dt_, sl], Arms[:, c, :], op=ALU.mult)
                b.v.tensor_scalar(scr.tmpb[:, :], scr.tmpb[:, :], nw[:, dt_:dt_ + 1],
                                  None, op0=ALU.mult)
                vi = b.inc(b.v.tensor_tensor(h[:, dt_, sl], scr.tmpb[:, :], scr.tmpa[:, :],
                                             op=ALU.mult), "se_v")
        b.barrier()
        scb.close()
        scb2 = ExitStack()
        wo = b.sb("h_wo", [128, DT, D], BF16, scb2)
        x2 = b.sb("h_x2", [128, DT, S], F32, scb2)
        go = load_wbf(b, scr, wo, Wap["hWoT"], DT, D)
        b.wait(b.y, "se_v")
        b.drain_st()
        tx2 = b.load(x2[:], xap3(xin, bt))
        b.wait_ld(b.v, tx2)
        b.t.wait_ge(b.sems["se_g"][0], go)
        b.t.wait_ge(b.sems["se_v"][0], vi)
        for c in range(4):
            sl = slice(c * 512, (c + 1) * 512)
            for ft in range(DT):
                slot = (ft * 4 + c) % 3
                key = f"pb{slot}"
                scr.pacq(b, key)
                for kt in range(DT):
                    mm = b.t.matmul(scr.pbig[:, slot, :], wo[:, kt, ft * 128:(ft + 1) * 128],
                                    h[:, kt, sl], start=(kt == 0), stop=(kt == DT - 1))
                ti = b.inc(mm, "se_t")
                b.v.wait_ge(b.sems["se_t"][0], ti)
                vi = b.inc(b.v.tensor_tensor(x2[:, ft, sl], x2[:, ft, sl],
                                             scr.pbig[:, slot, :], op=ALU.add), "se_v")
                scr.prel(key, "se_v", vi)
        b.wait(b.y, "se_v")
        b.store(xap3(xout, bt), x2[:])
        b.barrier()
        scb2.close()
    sc.close()



def emit_moe(b, K, scr, Wap, xin, xout):
    HS = 1024
    sc = ExitStack()
    b.drain_st()
    rw = b.sb("m_rw", [128, DT, 16], F32, sc)
    rbr = b.sb("m_rbr", [1, 16], F32, sc)
    rbbc = b.sb("m_rbbc", [128, 16], F32, sc)
    csbc = b.sb("m_csbc", [128, 16], F32, sc)
    sel = b.sb("m_sel", [16, 16, 128], F32, sc)
    b2 = b.sb("m_b2", [16, D], F32, sc)
    cT = b.sb("m_cT", [16, HS], F32, sc)
    l0T = cT
    atok = b.sb("m_atok", [128, 2], F32, sc)
    ttok = b.sb("m_ttok", [128, 3, 16], F32, sc)
    red = b.sb("m_red", [128, 4], F32, sc)
    b1sb = b.sb("m_b1", [128, 16], F32, sc)
    cbc = b.sb("m_cbc", [128, 512], F32, sc)
    hmid = b.sb("m_hmid", [128, 16, 512], BF16, sc)
    gel = b.sb("m_gel", [128, 512], BF16, sc)
    ln = b.sb("m_ln", [128, DT, HS], BF16, sc)
    w1 = b.sb("m_w1", [128, DT, 2048], BF16, sc)
    w2 = b.sb("m_w2", [128, 16, D], BF16, sc)
    x_sb = b.sb("m_x", [128, DT, HS], F32, sc)

    b.load(rw[:], Wap["rw"].rearrange("(kt p) e -> p kt e", p=128))
    b.load(sel[:], Wap["sel16"])
    b.load(rbr[:], Wap["rb_row"])
    tb2 = b.load(b2[:], Wap["b2"])
    b.wait_ld(b.t, tb2)
    scr.pacq(b, "paux0")
    ti = b.inc(b.t.matmul(scr.paux[:, 0, 0:16], K.ones_r_f32[:], rbr[:],
                          start=True, stop=True), "se_t")
    b.s.wait_ge(b.sems["se_t"][0], ti)
    si = b.inc(b.s.copy(rbbc[:], scr.paux[:, 0, 0:16]), "se_s")
    scr.prel("paux0", "se_s", si)
    scr.pacq(b, "pst0")
    for kt in range(DT):
        mm = b.t.matmul(scr.pst[:, 0, 0:16], K.ones_c_f32[:], rw[:, kt, :],
                        start=(kt == 0), stop=(kt == DT - 1))
    ti = b.inc(mm, "se_t")
    b.s.wait_ge(b.sems["se_t"][0], ti)
    si = b.inc(b.s.copy(scr.ab[0:1, 0, 0:16], scr.pst[:, 0, 0:16]), "se_s")
    scr.prel("pst0", "se_s", si)
    b.t.wait_ge(b.sems["se_s"][0], si)
    scr.pacq(b, "paux0")
    ti = b.inc(b.t.matmul(scr.paux[:, 0, 0:16], K.ones_r_f32[:], scr.ab[0:1, 0, 0:16],
                          start=True, stop=True), "se_t")
    b.s.wait_ge(b.sems["se_t"][0], ti)
    si = b.inc(b.s.copy(csbc[:], scr.paux[:, 0, 0:16]), "se_s")
    scr.prel("paux0", "se_s", si)

    lastw1 = lastw2 = lastgel = None
    for bt in range(BPC):
        for hf in range(2):
            hsl = slice(hf * HS, (hf + 1) * HS)
            b.drain_st()
            tx = b.load(x_sb[:], xap3(xin, bt)[:, :, hsl])
            for e in (b.s, b.t, b.v):
                b.wait_ld(e, tx)
            emit_ln(b, K, scr, x_sb, ln, HS)
            for c in range(2):
                sl = slice(c * 512, (c + 1) * 512)
                slot = c % 3
                key = f"pb{slot}"
                scr.pacq(b, key)
                for kt in range(DT):
                    mm = b.t.matmul(scr.pbig[0:16, slot, :], rw[:, kt, :], x_sb[:, kt, sl],
                                    start=(kt == 0), stop=(kt == DT - 1))
                ti = b.inc(mm, "se_t")
                b.s.wait_ge(b.sems["se_t"][0], ti)
                si = b.inc(b.s.copy(l0T[:, sl], scr.pbig[0:16, slot, :]), "se_s")
                scr.prel(key, "se_s", si)
            if hasattr(K, "dbg_lg"):
                b.wait(b.y, "se_s")
                b.store(K.dbg_lg[bt * 2 + hf], l0T[:])
            for it in range(8):
                tsl = slice(it * 128, (it + 1) * 128)
                b.t.wait_ge(b.sems["se_s"][0], si)
                scr.pacq(b, "pb0")
                tlg = b.inc(b.t.transpose(scr.pbig[:, 0, 0:16], l0T[:, tsl],
                                          K.ident_f32[0:16, 0:16]), "se_t")
                scr.pacq(b, "pb1")
                b.t.transpose(scr.pbig[:, 1, 0:1], scr.ab[0:1, 0, tsl],
                              K.ident_f32[0:1, 0:1])
                tab = b.inc(b.t.transpose(scr.pbig[:, 1, 1:2], scr.ab[0:1, 1, tsl],
                                          K.ident_f32[0:1, 0:1]), "se_t")
                b.v.wait_ge(b.sems["se_t"][0], tab)
                b.vself(b.v.tensor_copy(atok[:, 0:1], scr.pbig[:, 1, 0:1]))
                vi = b.vself(b.v.tensor_copy(atok[:, 1:2], scr.pbig[:, 1, 1:2]))
                scr.prel("pb1", "se_v", vi)
                b.vself(b.v.tensor_scalar(ttok[:, 0, :], scr.pbig[:, 0, 0:16],
                                          atok[:, 0:1], None, op0=ALU.mult))
                vi = b.vself(b.v.scalar_tensor_tensor(ttok[:, 0, :], csbc[:], atok[:, 1:2],
                                                      ttok[:, 0, :], op0=ALU.mult,
                                                      op1=ALU.add))
                scr.prel("pb0", "se_v", vi)
                b.vself(b.v.tensor_tensor(ttok[:, 0, :], ttok[:, 0, :], rbbc[:], op=ALU.add))
                vi = b.vself(b.v.reduce_max(red[:, 0:1], ttok[:, 0, :], axis=AX.X,
                                            negate=True))
                b.s.wait_ge(b.sems["se_v"][0], vi)
                si = b.inc(b.s.activation(ttok[:, 1, :], ttok[:, 0, :], AF.Exp,
                                          bias=red[:, 0:1]), "se_s")
                b.v.wait_ge(b.sems["se_s"][0], si)
                b.vself(b.v.tensor_copy(ttok[:, 0, :], ttok[:, 1, :]))
                b.vself(b.v.memset(ttok[:, 2, :], 0.0))
                b.vself(b.v.memset(red[:, 2:3], 0.0))
                for j in range(4):
                    b.vself(b.v.reduce_max(red[:, 1:2], ttok[:, 0, :], axis=AX.X))
                    b.vself(b.v.tensor_tensor(red[:, 2:3], red[:, 2:3], red[:, 1:2],
                                              op=ALU.add))
                    b.vself(b.v.tensor_scalar(scr.tmpa[:, 0:16], ttok[:, 0, :],
                                              red[:, 1:2], None, op0=ALU.is_ge))
                    b.vself(b.v.tensor_tensor(ttok[:, 2, :], ttok[:, 2, :],
                                              scr.tmpa[:, 0:16], op=ALU.add))
                    b.vself(b.v.scalar_tensor_tensor(ttok[:, 0, :], scr.tmpa[:, 0:16], NEG,
                                                     ttok[:, 0, :], op0=ALU.mult,
                                                     op1=ALU.add))
                b.vself(b.v.reciprocal(red[:, 3:4], red[:, 2:3]))
                b.vself(b.v.tensor_tensor(ttok[:, 1, :], ttok[:, 1, :], ttok[:, 2, :],
                                          op=ALU.mult))
                vi = b.vself(b.v.tensor_scalar(ttok[:, 1, :], ttok[:, 1, :], red[:, 3:4],
                                               None, op0=ALU.mult))
                b.t.wait_ge(b.sems["se_v"][0], vi)
                scr.pacq(b, "pb2")
                ti = b.inc(b.t.transpose(scr.pbig[0:16, 2, 0:128], ttok[:, 1, :],
                                         K.ident_f32[:]), "se_t")
                b.s.wait_ge(b.sems["se_t"][0], ti)
                si = b.inc(b.s.copy(cT[:, tsl], scr.pbig[0:16, 2, 0:128]), "se_s")
                scr.prel("pb2", "se_s", si)
            b.wait(b.t, "se_s")
            if hasattr(K, "dbg_cT"):
                b.wait(b.y, "se_s")
                b.store(K.dbg_cT[bt * 2 + hf], cT[:])
            precast = (K.moe_call == 0 and bt == 0 and hf == 0)
            nexp = getattr(K, "dbg_nexp", 16)
            w1v = [K.w1bf[e].rearrange("(kt p) f -> p kt f", p=128)
                   for e in range(16)]
            w2v = [K.w2bf[e].rearrange("(kt p) f -> p kt f", p=128)
                   for e in range(16)]
            if not precast:
                if lastw1 is not None:
                    b.y.wait_ge(b.sems["se_t"][0], lastw1)
                    b.g.wait_ge(b.sems["se_t"][0], lastw1)
                tw1a = b.load(w1[:, 0:4, :], w1v[0][:, 0:4, :])
                tw1b = b.load_g(w1[:, 4:8, :], w1v[0][:, 4:8, :])
                if lastw2 is not None:
                    b.y.wait_ge(b.sems["se_t"][0], lastw2)
                    b.g.wait_ge(b.sems["se_t"][0], lastw2)
                tw2a = b.load(w2[:, 0:8, :], w2v[0][:, 0:8, :])
                tw2b = b.load_g(w2[:, 8:16, :], w2v[0][:, 8:16, :])
            for e in range(nexp):
                if lastgel is not None:
                    b.y.wait_ge(b.sems["se_s"][0], lastgel)
                tb1 = b.load(b1sb[:], Wap["b1l"][e])
                if precast:
                    g1 = load_wbf(b, scr, w1, Wap["w1"][e], DT, 2048,
                                  tguard=lastw1)
                    b.y.wait_ge(b.sems["se_g"][0], g1)
                    b.store(w1v[e], w1[:])
                    g2 = load_wbf(b, scr, w2, Wap["w2"][e], 16, D,
                                  tguard=lastw2)
                    b.y.wait_ge(b.sems["se_g"][0], g2)
                    b.store(w2v[e], w2[:])
                    b.wait_ld(b.s, tb1)
                    b.t.wait_ge(b.sems["se_g"][0], g1)
                else:
                    b.wait_ld(b.s, tb1)
                    b.wait_ld(b.t, tw1a)
                    b.wait_ldg(b.t, tw1b)
                for c in range(2):
                    sl = slice(c * 512, (c + 1) * 512)
                    scr.pacq(b, "paux0")
                    ti = b.inc(b.t.matmul(scr.paux[:, 0, :], sel[:, e, :], cT[:, sl],
                                          start=True, stop=True), "se_t")
                    b.s.wait_ge(b.sems["se_t"][0], ti)
                    b.s.wait_ge(b.sems["se_v"][0], b.val("se_v"))
                    sic = b.inc(b.s.copy(cbc[:], scr.paux[:, 0, :]), "se_s")
                    scr.prel("paux0", "se_s", sic)
                    b.v.wait_ge(b.sems["se_s"][0], sic)
                    for fi in range(16):
                        slot = fi % 3
                        key = f"pb{slot}"
                        scr.pacq(b, key)
                        for kt in range(DT):
                            mm = b.t.matmul(scr.pbig[:, slot, :],
                                            w1[:, kt, fi * 128:(fi + 1) * 128],
                                            ln[:, kt, sl], start=(kt == 0),
                                            stop=(kt == DT - 1))
                        ti = b.inc(mm, "se_t")
                        lastw1 = ti
                        b.s.wait_ge(b.sems["se_t"][0], ti)
                        b.s.wait_ge(b.sems["se_v"][0], b.val("se_v"))
                        si = b.inc(b.s.activation(gel[:], scr.pbig[:, slot, :], AF.Gelu,
                                                  bias=b1sb[:, fi:fi + 1]), "se_s")
                        lastgel = si
                        scr.prel(key, "se_s", si)
                        b.v.wait_ge(b.sems["se_s"][0], si)
                        vi = b.inc(b.v.tensor_tensor(hmid[:, fi, :], gel[:], cbc[:],
                                                     op=ALU.mult), "se_v")
                    if not precast and c == 1 and e + 1 < nexp:
                        b.y.wait_ge(b.sems["se_t"][0], lastw1)
                        b.g.wait_ge(b.sems["se_t"][0], lastw1)
                        tw1a = b.load(w1[:, 0:4, :], w1v[e + 1][:, 0:4, :])
                        tw1b = b.load_g(w1[:, 4:8, :], w1v[e + 1][:, 4:8, :])
                    if c == 0:
                        if precast:
                            b.t.wait_ge(b.sems["se_g"][0], g2)
                        else:
                            b.wait_ld(b.t, tw2a)
                            b.wait_ldg(b.t, tw2b)
                    b.t.wait_ge(b.sems["se_v"][0], vi)
                    nob2 = hasattr(K, "dbg_nob2") or e > 0
                    for dt_ in range(DT):
                        key = f"paux{dt_ % 2}"
                        scr.pacq(b, key)
                        if not nob2:
                            b.t.matmul(scr.paux[:, dt_ % 2, :],
                                       b2[:, dt_ * 128:(dt_ + 1) * 128],
                                       cT[:, sl], start=True, stop=False)
                        for ki in range(16):
                            mm = b.t.matmul(scr.paux[:, dt_ % 2, :],
                                            w2[:, ki, dt_ * 128:(dt_ + 1) * 128],
                                            hmid[:, ki, :], start=(nob2 and ki == 0),
                                            stop=(ki == 15))
                        ti = b.inc(mm, "se_t")
                        lastw2 = ti
                        b.v.wait_ge(b.sems["se_t"][0], ti)
                        vi2 = b.inc(b.v.tensor_tensor(x_sb[:, dt_, sl], x_sb[:, dt_, sl],
                                                      scr.paux[:, dt_ % 2, :], op=ALU.add),
                                    "se_v")
                        scr.prel(key, "se_v", vi2)
                if not precast and e + 1 < nexp:
                    b.y.wait_ge(b.sems["se_t"][0], lastw2)
                    b.g.wait_ge(b.sems["se_t"][0], lastw2)
                    tw2a = b.load(w2[:, 0:8, :], w2v[e + 1][:, 0:8, :])
                    tw2b = b.load_g(w2[:, 8:16, :], w2v[e + 1][:, 8:16, :])
            b.wait(b.y, "se_v")
            b.store(xap3(xout, bt)[:, :, hsl], x_sb[:])
            b.barrier()
    K.moe_call += 1
    sc.close()



def emit_l0a(b, K, scr, Wap, xin, xout):
    sc = ExitStack()
    b.drain_st()
    Ebuf = b.sb("a_E", [128, 2, 512], BF16, sc)
    bz = b.sb("a_bz", [128, 512], F32, sc)
    mc = b.sb("a_mc", [128, 896], F32, sc)
    b.load(mc[:], K.aps["mask_causal"])

    def rope_pair(wa, wb, src3, kt_tot, dst, rc, rs):
        for ft in range(DT):
            for c in range(4):
                sl = slice(c * 512, (c + 1) * 512)
                scr.pacq(b, "pb0")
                scr.pacq(b, "pb1")
                for kt in range(kt_tot):
                    b.t.matmul(scr.pbig[:, 0, :], wa[:, kt, ft * 128:(ft + 1) * 128],
                               src3[:, kt, sl], start=(kt == 0), stop=(kt == kt_tot - 1))
                for kt in range(kt_tot):
                    mm = b.t.matmul(scr.pbig[:, 1, :], wb[:, kt, ft * 128:(ft + 1) * 128],
                                    src3[:, kt, sl], start=(kt == 0), stop=(kt == kt_tot - 1))
                ti = b.inc(mm, "se_t")
                b.v.wait_ge(b.sems["se_t"][0], ti)
                b.v.tensor_tensor(scr.tmpa[:, :], scr.pbig[:, 0, :], rc[:, sl], op=ALU.mult)
                b.v.tensor_tensor(scr.tmpb[:, :], scr.pbig[:, 1, :], rs[:, sl], op=ALU.mult)
                vi = b.inc(b.v.tensor_tensor(dst[:, ft, sl], scr.tmpa[:, :], scr.tmpb[:, :],
                                             op=ALU.add), "se_v")
                scr.prel("pb0", "se_v", vi)
                scr.prel("pb1", "se_v", vi)

    for bt in range(BPC):
        sc_ln = ExitStack()
        ln = b.sb("a_ln", [128, DT, S], BF16, sc_ln)
        ln_from_hbm(b, K, scr, xin, bt, ln)
        rb_attn = ExitStack()
        attn = b.sb("a_attn", [128, DT, S], BF16, rb_attn, side="right")
        rb2 = ExitStack()
        q = b.sb("a_q", [128, DT, S], BF16, rb2, side="right")
        scw = ExitStack()
        rc = b.sb("a_rc", [128, S], F32, scw)
        rs = b.sb("a_rs", [128, S], F32, scw)
        wq = b.sb("a_wq", [128, DT, D], BF16, scw)
        wqs = b.sb("a_wqs", [128, DT, D], BF16, scw)
        b.load(rc[:], K.aps["rope_cos"])
        b.load(rs[:], K.aps["rope_sin"])
        load_wbf(b, scr, wq, Wap["wqT"], DT, D)
        gqs = load_wbf(b, scr, wqs, Wap["wqTs"], DT, D)
        b.t.wait_ge(b.sems["se_g"][0], gqs)
        rope_pair(wq, wqs, ln, DT, q, rc, rs)
        b.barrier()
        scw.close()
        lat = b.sb("a_lat", [128, 4, S], BF16, rb2, side="right")
        scw = ExitStack()
        kvd = b.sb("a_kvd", [128, DT, 512], BF16, scw)
        gl = load_wbf(b, scr, kvd, Wap["kvdT"], DT, 512)
        b.t.wait_ge(b.sems["se_g"][0], gl)
        for ft in range(4):
            for c in range(4):
                sl = slice(c * 512, (c + 1) * 512)
                slot = (ft * 4 + c) % 3
                key = f"pb{slot}"
                scr.pacq(b, key)
                for kt in range(DT):
                    mm = b.t.matmul(scr.pbig[:, slot, :], kvd[:, kt, ft * 128:(ft + 1) * 128],
                                    ln[:, kt, sl], start=(kt == 0), stop=(kt == DT - 1))
                ti = b.inc(mm, "se_t")
                b.s.wait_ge(b.sems["se_t"][0], ti)
                si = b.inc(b.s.copy(lat[:, ft, sl], scr.pbig[:, slot, :]), "se_s")
                scr.prel(key, "se_s", si)
        b.barrier()
        scw.close()
        sc_ln.close()
        k = b.sb("a_k", [128, DT, S], BF16, rb2, side="right")
        scw = ExitStack()
        rc = b.sb("a_rc2", [128, S], F32, scw)
        rs = b.sb("a_rs2", [128, S], F32, scw)
        kup = b.sb("a_kup", [128, 4, D], BF16, scw)
        kups = b.sb("a_kups", [128, 4, D], BF16, scw)
        b.load(rc[:], K.aps["rope_cos"])
        b.load(rs[:], K.aps["rope_sin"])
        load_wbf(b, scr, kup, Wap["kupT"], 4, D)
        gks = load_wbf(b, scr, kups, Wap["kupTs"], 4, D)
        b.t.wait_ge(b.sems["se_g"][0], gks)
        rope_pair(kup, kups, lat, 4, k, rc, rs)
        b.barrier()
        scw.close()
        v1 = b.sb("a_v1", [128, 16, 16, 64], BF16, rb2, side="right")
        scw = ExitStack()
        vup = b.sb("a_vup", [128, 4, D], BF16, scw)
        gv = load_wbf(b, scr, vup, Wap["vupT"], 4, D)
        b.t.wait_ge(b.sems["se_g"][0], gv)
        for st_ in range(16):
            for fg in range(2):
                slot = (st_ * 2 + fg) % 3
                key = f"pb{slot}"
                scr.pacq(b, key)
                for lt in range(4):
                    mm = b.t.matmul(scr.pbig[:, slot, :],
                                    lat[:, lt, st_ * 128:(st_ + 1) * 128],
                                    vup[:, lt, fg * 512:(fg + 1) * 512],
                                    start=(lt == 0), stop=(lt == 3))
                ti = b.inc(mm, "se_t")
                b.s.wait_ge(b.sems["se_t"][0], ti)
                si = b.inc(b.s.copy(
                    v1[:, st_, fg * 8:(fg + 1) * 8, :],
                    scr.pbig[:, slot, :].rearrange("p (h f) -> p h f", f=64)), "se_s")
                scr.prel(key, "se_s", si)
        b.t.wait_ge(b.sems["se_s"][0], si)
        b.barrier()
        scw.close()
        for h in range(16):
            ft, pb = h // 2, (h % 2) * 64
            for qc in range(4):
                qsl = slice(qc * 512, (qc + 1) * 512)
                ktmax = 4 * qc + 4
                scr.pacq(b, "paux0")
                for kt in range(ktmax):
                    slot = kt % 2
                    key = f"pb{slot}"
                    scr.pacq(b, key)
                    mm = b.t.matmul(scr.pbig[:, slot, :],
                                    k[pb:pb + 64, ft, kt * 128:(kt + 1) * 128],
                                    q[pb:pb + 64, ft, qsl], start=True, stop=True)
                    ti = b.inc(mm, "se_t")
                    if kt >= 4 * qc:
                        o = 384 - 128 * (kt - 4 * qc)
                        b.v.wait_ge(b.sems["se_t"][0], ti)
                        vi = b.inc(b.v.tensor_tensor(scr.pbig[:, slot, :],
                                                     scr.pbig[:, slot, :],
                                                     mc[:, o:o + 512], op=ALU.add), "se_v")
                        b.s.wait_ge(b.sems["se_v"][0], vi)
                    else:
                        b.s.wait_ge(b.sems["se_t"][0], ti)
                    si = b.inc(b.s.activation(Ebuf[:, slot, :], scr.pbig[:, slot, :],
                                              AF.Exp, scale=0.125), "se_s")
                    scr.prel(key, "se_s", si)
                    b.t.wait_ge(b.sems["se_s"][0], si)
                    if kt == 0:
                        scr.pacq(b, "pst0")
                    b.t.matmul(scr.pst[:, 0, :], K.ones_c_bf16[:], Ebuf[:, slot, :],
                               start=(kt == 0), stop=(kt == ktmax - 1))
                    mm = b.t.matmul(scr.paux[pb:pb + 64, 0, :], v1[:, kt, h, :],
                                    Ebuf[:, slot, :],
                                    start=(kt == 0), stop=(kt == ktmax - 1))
                tpv = b.inc(mm, "se_t")
                b.v.wait_ge(b.sems["se_t"][0], tpv)
                vz = b.inc(b.v.reciprocal(scr.ab[0:1, 0, 0:512], scr.pst[:, 0, :]),
                           "se_v")
                scr.prel("pst0", "se_v", vz)
                b.t.wait_ge(b.sems["se_v"][0], vz)
                scr.pacq(b, "paux1")
                ti = b.inc(b.t.matmul(scr.paux[pb:pb + 64, 1, :], K.ones_r_f32[:, 0:64],
                                      scr.ab[0:1, 0, 0:512], start=True, stop=True), "se_t")
                b.s.wait_ge(b.sems["se_t"][0], ti)
                si = b.inc(b.s.copy(bz[pb:pb + 64, :], scr.paux[pb:pb + 64, 1, :]), "se_s")
                scr.prel("paux1", "se_s", si)
                b.v.wait_ge(b.sems["se_s"][0], si)
                vi = b.inc(b.v.tensor_tensor(attn[pb:pb + 64, ft, qsl],
                                             scr.paux[pb:pb + 64, 0, :], bz[pb:pb + 64, :],
                                             op=ALU.mult), "se_v")
                scr.prel("paux0", "se_v", vi)
        b.barrier()
        rb2.close()
        scw = ExitStack()
        wo = b.sb("a_wo", [128, DT, D], BF16, scw)
        x2 = b.sb("a_x2", [128, DT, S], F32, scw)
        go = load_wbf(b, scr, wo, Wap["woT"], DT, D)
        b.wait(b.y, "se_v")
        b.drain_st()
        tx2 = b.load(x2[:], xap3(xin, bt))
        b.t.wait_ge(b.sems["se_g"][0], go)
        b.wait_ld(b.v, tx2)
        for c in range(4):
            sl = slice(c * 512, (c + 1) * 512)
            for ftd in range(DT):
                slot = (ftd * 4 + c) % 3
                key = f"pb{slot}"
                scr.pacq(b, key)
                for kt in range(DT):
                    mm = b.t.matmul(scr.pbig[:, slot, :], wo[:, kt, ftd * 128:(ftd + 1) * 128],
                                    attn[:, kt, sl], start=(kt == 0), stop=(kt == DT - 1))
                ti = b.inc(mm, "se_t")
                b.v.wait_ge(b.sems["se_t"][0], ti)
                vi = b.inc(b.v.tensor_tensor(x2[:, ftd, sl], x2[:, ftd, sl],
                                             scr.pbig[:, slot, :], op=ALU.add), "se_v")
                scr.prel(key, "se_v", vi)
        b.wait(b.y, "se_v")
        b.store(xap3(xout, bt), x2[:])
        b.barrier()
        scw.close()
        rb_attn.close()
    sc.close()



def emit_l1a(b, K, scr, Wap, xin, xout):
    sc = ExitStack()
    b.drain_st()
    ln = b.sb("c_ln", [128, DT, S], BF16, sc)
    x_sb = b.sb("c_x", [128, DT, S], F32, sc)
    xc = b.sb("c_xc", [128, S], BF16, sc)
    sq = b.sb("c_sq", [128, S], BF16, sc)
    nd = b.sb("c_nd", [1, S], F32, sc)
    x1 = b.sb("c_x1", [128, 16, 64], BF16, sc)
    Ebuf = b.sb("c_E", [128, 2, 512], BF16, sc)
    bz = b.sb("c_bz", [128, 512], F32, sc)
    mrow = b.sb("c_m", [128, 1], F32, sc)

    for bt in range(BPC):
        tx = b.load(x_sb[:], xap3(xin, bt))
        for e in (b.s, b.t, b.v):
            b.wait_ld(e, tx)
        emit_ln(b, K, scr, x_sb, ln, S)
        for h in range(16):
            ft, pb = h // 2, (h % 2) * 64
            b.vself(b.v.reduce_sum(mrow[pb:pb + 64, :], ln[pb:pb + 64, ft, :], axis=AX.X,
                                   negate=True))
            b.vself(b.v.tensor_scalar(mrow[pb:pb + 64, :], mrow[pb:pb + 64, :], 1.0 / S,
                                      None, op0=ALU.mult))
            b.vself(b.v.tensor_scalar(xc[pb:pb + 64, :], ln[pb:pb + 64, ft, :],
                                      mrow[pb:pb + 64, :], None, op0=ALU.add))
            vi = b.vself(b.v.tensor_tensor(sq[pb:pb + 64, :], xc[pb:pb + 64, :],
                                           xc[pb:pb + 64, :], op=ALU.mult))
            b.t.wait_ge(b.sems["se_v"][0], vi)
            for c in range(4):
                sl = slice(c * 512, (c + 1) * 512)
                scr.pacq(b, "pst0")
                mm = b.t.matmul(scr.pst[:, 0, :], K.ones_c_bf16[pb:pb + 64, :],
                                sq[pb:pb + 64, sl], start=True, stop=True)
                ti = b.inc(mm, "se_t")
                b.v.wait_ge(b.sems["se_t"][0], ti)
                vi = b.inc(b.v.tensor_scalar(nd[0:1, sl], scr.pst[:, 0, :], -1.0, None,
                                             op0=ALU.mult), "se_v")
                scr.prel("pst0", "se_v", vi)
            scr.pacq(b, "pbf")
            for Tt in range(16):
                mm = b.t.transpose(scr.pbf[:, Tt * 64:(Tt + 1) * 64],
                                   ln[pb:pb + 64, ft, Tt * 128:(Tt + 1) * 128],
                                   K.ident_bf16[pb:pb + 64, pb:pb + 64])
            ti = b.inc(mm, "se_t")
            b.s.wait_ge(b.sems["se_t"][0], ti)
            sx1 = b.inc(b.s.copy(x1[:, :, :],
                                 scr.pbf[:].rearrange("p (t f) -> p t f", f=64)), "se_s")
            scr.prel("pbf", "se_s", sx1)
            b.t.wait_ge(b.sems["se_s"][0], sx1)
            b.t.wait_ge(b.sems["se_v"][0], vi)
            for tc in range(4):
                tsl = slice(tc * 512, (tc + 1) * 512)
                scr.pacq(b, "paux0")
                for Tt in range(16):
                    slot = Tt % 2
                    key = f"pb{slot}"
                    scr.pacq(b, key)
                    b.t.matmul(scr.pbig[:, slot, :], xc[pb:pb + 64, Tt * 128:(Tt + 1) * 128],
                               xc[pb:pb + 64, tsl], start=True, stop=False)
                    mm = b.t.matmul(scr.pbig[:, slot, :], K.ones_r_f32[:], nd[0:1, tsl],
                                    start=False, stop=True)
                    ti = b.inc(mm, "se_t")
                    b.s.wait_ge(b.sems["se_t"][0], ti)
                    si = b.inc(b.s.activation(Ebuf[:, slot, :], scr.pbig[:, slot, :],
                                              AF.Exp), "se_s")
                    scr.prel(key, "se_s", si)
                    b.t.wait_ge(b.sems["se_s"][0], si)
                    if Tt == 0:
                        scr.pacq(b, "pst0")
                    b.t.matmul(scr.pst[:, 0, :], K.ones_c_bf16[:], Ebuf[:, slot, :],
                               start=(Tt == 0), stop=(Tt == 15))
                    mm = b.t.matmul(scr.paux[pb:pb + 64, 0, :], x1[:, Tt, :],
                                    Ebuf[:, slot, :], start=(Tt == 0), stop=(Tt == 15))
                tpv = b.inc(mm, "se_t")
                b.v.wait_ge(b.sems["se_t"][0], tpv)
                vz = b.inc(b.v.reciprocal(scr.ab[0:1, 0, 0:512], scr.pst[:, 0, :]),
                           "se_v")
                scr.prel("pst0", "se_v", vz)
                b.t.wait_ge(b.sems["se_v"][0], vz)
                scr.pacq(b, "paux1")
                ti = b.inc(b.t.matmul(scr.paux[pb:pb + 64, 1, :], K.ones_r_f32[:, 0:64],
                                      scr.ab[0:1, 0, 0:512], start=True, stop=True), "se_t")
                b.s.wait_ge(b.sems["se_t"][0], ti)
                si = b.inc(b.s.copy(bz[pb:pb + 64, :], scr.paux[pb:pb + 64, 1, :]), "se_s")
                scr.prel("paux1", "se_s", si)
                b.v.wait_ge(b.sems["se_s"][0], si)
                b.v.tensor_tensor(scr.tmpa[pb:pb + 64, :], scr.paux[pb:pb + 64, 0, :],
                                  bz[pb:pb + 64, :], op=ALU.mult)
                vi = b.inc(b.v.tensor_tensor(x_sb[pb:pb + 64, ft, tsl],
                                             x_sb[pb:pb + 64, ft, tsl],
                                             scr.tmpa[pb:pb + 64, :], op=ALU.add), "se_v")
                scr.prel("paux0", "se_v", vi)
        b.wait(b.y, "se_v")
        b.store(xap3(xout, bt), x_sb[:])
        b.barrier()
    sc.close()



def emit_gla(b, K, scr, Wap, xin, xout):
    sc = ExitStack()
    b.drain_st()
    wq_ = b.sb("g_wq", [128, DT, D], BF16, sc)
    wk_ = b.sb("g_wk", [128, DT, D], BF16, sc)
    wv_ = b.sb("g_wv", [128, DT, 2048], BF16, sc)
    wog = b.sb("g_wog", [128, DT, 2048], BF16, sc)
    wo_ = b.sb("g_wo", [128, 16, D], BF16, sc)
    wg_ = b.sb("g_wg", [128, DT, 8], F32, sc)
    bg16 = b.sb("g_bg16", [8, 1], F32, sc)
    csg = b.sb("g_csg", [8, 1], F32, sc)
    xc_ = b.sb("g_xc", [128, DT, 128], F32, sc)
    lnc = b.sb("g_lnc", [128, DT, 128], BF16, sc)
    qh = b.sb("g_qh", [128, 128], BF16, sc)
    kh = b.sb("g_kh", [128, 128], BF16, sc)
    ktl = b.sb("g_ktl", [128, 128], BF16, sc)
    kd = b.sb("g_kd", [128, 128], BF16, sc)
    vh = b.sb("g_vh", [128, 256], BF16, sc)
    ogh = b.sb("g_ogh", [128, 256], BF16, sc)
    gt_ = b.sb("g_gt", [128, 256], BF16, sc)
    S0 = b.sb("g_S0", [128, 2, 8, 256], BF16, sc)
    Lf = b.sb("g_Lf", [8, S], F32, sc)
    Ltok = b.sb("g_Ltok", [128, 8], F32, sc)
    nLtok = b.sb("g_nLtok", [128, 8], F32, sc)
    Lrow = b.sb("g_Lrow", [1, 128], F32, sc)
    lcsave = b.sb("g_lcs", [1, 8, 1], F32, sc)
    lcend = b.sb("g_lce", [1, 8, 1], F32, sc)
    lc0 = b.sb("g_lc0", [128, 2], F32, sc)
    Dbf = b.sb("g_D", [128, 128], BF16, sc)
    Msb = b.sb("g_M", [128, 128], BF16, sc)
    osb = b.sb("g_osb", [128, 256], BF16, sc)
    intr = b.sb("g_intr", [128, 256], F32, sc)
    are = b.sb("g_are", [128, 3], F32, sc)
    gT = b.sb("g_gT", [128, 16, 128], BF16, sc)
    mtri = b.sb("g_mtri", [128, 128], F32, sc)

    b.load(mtri[:], K.aps["mask_tri"])
    load_wbf(b, scr, wq_, Wap["gWq"], DT, D)
    load_wbf(b, scr, wk_, Wap["gWk"], DT, D)
    load_wbf(b, scr, wv_, Wap["gWv"], DT, 2048)
    load_wbf(b, scr, wog, Wap["gWog"], DT, 2048)
    go = load_wbf(b, scr, wo_, Wap["gWo"], 16, D)
    tg = b.load(wg_[:], Wap["gWg"])
    tbg = b.load(bg16[:], Wap["gbg"])
    b.wait_ld(b.s, tbg)
    si = b.inc(b.s.mul(bg16[:], bg16[:], 1.0 / 16.0), "se_s")
    b.wait_ld(b.t, tg)
    scr.pacq(b, "pst0")
    for kt in range(DT):
        mm = b.t.matmul(scr.pst[:, 0, 0:8], K.ones_c_f32[:], wg_[:, kt, :],
                        start=(kt == 0), stop=(kt == DT - 1))
    ti = b.inc(mm, "se_t")
    b.s.wait_ge(b.sems["se_t"][0], ti)
    si = b.inc(b.s.copy(scr.ab[0:1, 0, 0:8], scr.pst[:, 0, 0:8]), "se_s")
    scr.prel("pst0", "se_s", si)
    b.t.wait_ge(b.sems["se_s"][0], si)
    scr.pacq(b, "pb0")
    ti = b.inc(b.t.transpose(scr.pbig[0:8, 0, 0:1], scr.ab[0:1, 0, 0:8],
                             K.ident_f32[0:1, 0:1]), "se_t")
    b.s.wait_ge(b.sems["se_t"][0], ti)
    si = b.inc(b.s.copy(csg[:], scr.pbig[0:8, 0, 0:1]), "se_s")
    scr.prel("pb0", "se_s", si)
    b.t.wait_ge(b.sems["se_g"][0], go)

    def pmm(slot, out_ap, mk_lhsT, mk_rhs, nkt=DT, consumer="se_s"):
        key = f"pb{slot}"
        scr.pacq(b, key)
        for kt in range(nkt):
            mm = b.t.matmul(out_ap, mk_lhsT(kt), mk_rhs(kt),
                            start=(kt == 0), stop=(kt == nkt - 1))
        return b.inc(mm, "se_t"), key

    for bt in range(BPC):
        loga = Lf
        b.drain_st()
        vz = b.inc(b.v.memset(S0[:, 0, :, :], 0.0), "se_v")
        vz = b.inc(b.v.memset(lcsave[:], 0.0), "se_v")
        b.t.wait_ge(b.sems["se_v"][0], vz)
        vS = vz
        lcs_tok = None
        for ch in range(16):
            csl = slice(ch * 128, (ch + 1) * 128)
            b.wait(b.y, "se_v")
            tx = b.load(xc_[:], xap3(xin, bt)[:, :, csl])
            for e in (b.s, b.t, b.v):
                b.wait_ld(e, tx)
            scr.pacq(b, "pb2")
            for kt in range(DT):
                mm = b.t.matmul(scr.pbig[0:8, 2, 0:128], wg_[:, kt, :], xc_[:, kt, :],
                                start=(kt == 0), stop=(kt == DT - 1))
            tig = b.inc(mm, "se_t")
            emit_ln(b, K, scr, xc_, lnc, 128, apply=False)
            b.v.wait_ge(b.sems["se_t"][0], tig)
            b.v.wait_ge(b.sems["se_s"][0], b.val("se_s"))
            b.vself(b.v.tensor_tensor(scr.tmpa[0:8, 0:128], scr.pbig[0:8, 2, 0:128],
                                      scr.Abc[0:8, 0:128], op=ALU.mult))
            vi = b.vself(b.v.scalar_tensor_tensor(scr.tmpa[0:8, 0:128], scr.Bbc[0:8, 0:128],
                                                  csg[:, 0:1], scr.tmpa[0:8, 0:128],
                                                  op0=ALU.mult, op1=ALU.add))
            scr.prel("pb2", "se_v", vi)
            if getattr(K, "dbg_gla_x1", None) is not None:
                b.wait(b.y, "se_v")
                b.store(K.dbg_gla_x1[bt, ch], xc_[:])
                b.store(K.dbg_gla_lg[bt, ch], scr.tmpa[0:8, 0:128])
                b.wait(b.s, "st")
                b.wait(b.v, "st")
            b.s.wait_ge(b.sems["se_v"][0], vi)
            b.sself(b.s.activation(scr.tmpb[0:8, 0:128], scr.tmpa[0:8, 0:128], AF.Sigmoid,
                                   scale=1.0 / 16.0, bias=bg16[:]))
            sig = b.sself(b.s.activation(loga[:, csl], scr.tmpb[0:8, 0:128], AF.Ln))
        b.v.wait_ge(b.sems["se_s"][0], sig)
        vL = b.inc(b.v.tensor_tensor_scan(Lf[:], K.ones_row2k[:], loga[:], 0.0,
                                          op0=ALU.mult, op1=ALU.add), "se_v")
        b.t.wait_ge(b.sems["se_v"][0], vL)
        if getattr(K, "dbg_gla_Lf", None) is not None:
            b.wait(b.y, "se_v")
            b.store(K.dbg_gla_Lf[bt], Lf[:])
            b.wait(b.s, "st")
        for ch in range(16):
            csl = slice(ch * 128, (ch + 1) * 128)
            par, oth = ch % 2, 1 - ch % 2
            b.wait(b.y, "se_v")
            tx = b.load(xc_[:], xap3(xin, bt)[:, :, csl])
            for e in (b.s, b.t, b.v):
                b.wait_ld(e, tx)
            vln = emit_ln(b, K, scr, xc_, lnc, 128)
            b.t.wait_ge(b.sems["se_v"][0], vln)
            b.t.wait_ge(b.sems["se_v"][0], vS)
            if lcs_tok is not None:
                b.t.wait_ge(b.sems["se_s"][0], lcs_tok)
            scr.pacq(b, "pb0")
            t1 = b.inc(b.t.transpose(scr.pbig[:, 0, 0:8], Lf[:, csl],
                                     K.ident_f32[0:8, 0:8]), "se_t")
            b.s.wait_ge(b.sems["se_t"][0], t1)
            s1 = b.inc(b.s.copy(Ltok[:], scr.pbig[:, 0, 0:8]), "se_s")
            scr.prel("pb0", "se_s", s1)
            b.v.wait_ge(b.sems["se_s"][0], s1)
            vnl = b.vself(b.v.tensor_scalar(nLtok[:], Ltok[:], -1.0, None, op0=ALU.mult))
            b.v.wait_ge(b.sems["se_v"][0], vnl)
            for hh in range(8):
                hs4 = slice(hh * 256, (hh + 1) * 256)
                scr.pacq(b, "pb0")
                t2 = b.inc(b.t.transpose(scr.pbig[0:1, 0, 0:128], Ltok[:, hh:hh + 1],
                                         K.ident_f32[:]), "se_t")
                b.s.wait_ge(b.sems["se_t"][0], t2)
                s2 = b.inc(b.s.copy(Lrow[:], scr.pbig[0:1, 0, 0:128]), "se_s")
                s2 = b.inc(b.s.copy(lcend[0:1, hh, :], Lrow[0:1, 127:128]), "se_s")
                scr.prel("pb0", "se_s", s2)
                b.t.wait_ge(b.sems["se_s"][0], s2)
                scr.pacq(b, "pb2")
                b.t.matmul(scr.pbig[:, 2, 0:1], K.ones_r_f32[:], lcsave[0:1, hh, 0:1],
                           start=True, stop=True)
                tiE = b.inc(b.t.matmul(scr.pbig[:, 2, 1:2], K.ones_r_f32[:],
                                       Lrow[0:1, 127:128], start=True, stop=True), "se_t")
                b.s.wait_ge(b.sems["se_t"][0], tiE)
                sE = b.inc(b.s.copy(lc0[:, 0:2], scr.pbig[:, 2, 0:2]), "se_s")
                scr.prel("pb2", "se_s", sE)
                b.v.wait_ge(b.sems["se_s"][0], sE)
                b.vself(b.v.tensor_tensor(scr.tmpa[:, 0:1], Ltok[:, hh:hh + 1], lc0[:, 0:1],
                                          op=ALU.subtract))
                b.vself(b.v.tensor_tensor(scr.tmpa[:, 1:2], lc0[:, 1:2], Ltok[:, hh:hh + 1],
                                          op=ALU.subtract))
                vi = b.vself(b.v.tensor_tensor(scr.tmpa[:, 2:3], lc0[:, 1:2], lc0[:, 0:1],
                                               op=ALU.subtract))
                b.s.wait_ge(b.sems["se_v"][0], vi)
                sA = b.inc(b.s.activation(are[:, 0:3], scr.tmpa[:, 0:3], AF.Exp), "se_s")
                if getattr(K, "dbg_gla_are", None) is not None:
                    b.wait(b.y, "se_s")
                    b.store(K.dbg_gla_are[bt, ch, hh], are[:, 0:3])
                    b.store(K.dbg_gla_lc[bt, ch, hh], lc0[:, 0:2])
                    b.wait(b.s, "st")
                    b.wait(b.v, "st")
                ti, kq = pmm(0, scr.pbig[:, 0, 0:128],
                             lambda kt: wq_[:, kt, hh * 128:(hh + 1) * 128],
                             lambda kt: lnc[:, kt, :])
                b.s.wait_ge(b.sems["se_t"][0], ti)
                sq_ = b.inc(b.s.copy(qh[:], scr.pbig[:, 0, 0:128]), "se_s")
                scr.prel(kq, "se_s", sq_)
                ti, kk = pmm(1, scr.pbig[:, 1, 0:128],
                             lambda kt: lnc[:, kt, :],
                             lambda kt: wk_[:, kt, hh * 128:(hh + 1) * 128])
                b.s.wait_ge(b.sems["se_t"][0], ti)
                sk_ = b.inc(b.s.activation(kh[:], scr.pbig[:, 1, 0:128], AF.Identity,
                                           scale=KSCALE), "se_s")
                scr.prel(kk, "se_s", sk_)
                ti, kv = pmm(2, scr.pbig[:, 2, 0:256],
                             lambda kt: lnc[:, kt, :],
                             lambda kt: wv_[:, kt, hh * 256:(hh + 1) * 256])
                b.s.wait_ge(b.sems["se_t"][0], ti)
                sv_ = b.inc(b.s.copy(vh[:], scr.pbig[:, 2, 0:256]), "se_s")
                scr.prel(kv, "se_s", sv_)
                ti, ko = pmm(0, scr.pbig[:, 0, 0:256],
                             lambda kt: lnc[:, kt, :],
                             lambda kt: wog[:, kt, hh * 256:(hh + 1) * 256])
                b.s.wait_ge(b.sems["se_t"][0], ti)
                so_ = b.inc(b.s.activation(ogh[:], scr.pbig[:, 0, 0:256], AF.Sigmoid),
                            "se_s")
                scr.prel(ko, "se_s", so_)
                b.t.wait_ge(b.sems["se_s"][0], sk_)
                scr.pacq(b, "pbf")
                ti = b.inc(b.t.transpose(scr.pbf[:, 0:128], kh[:], K.ident_bf16[:]),
                           "se_t")
                b.s.wait_ge(b.sems["se_t"][0], ti)
                skd = b.inc(b.s.copy(kd[:], scr.pbf[:, 0:128]), "se_s")
                scr.prel("pbf", "se_s", skd)
                b.t.wait_ge(b.sems["se_s"][0], skd)
                scr.pacq(b, "pb0")
                b.t.matmul(scr.pbig[:, 0, 0:128], kd[:], qh[:], start=True, stop=True)
                scr.pacq(b, "pb1")
                tiD = b.inc(b.t.matmul(scr.pbig[:, 1, 0:128], K.ones_r_f32[:],
                                       Lrow[:], start=True, stop=True), "se_t")
                b.v.wait_ge(b.sems["se_t"][0], tiD)
                viD = b.vself(b.v.tensor_tensor(scr.pbig[:, 1, 0:128], scr.pbig[:, 1, 0:128],
                                                mtri[:], op=ALU.add))
                b.s.wait_ge(b.sems["se_v"][0], viD)
                siD = b.inc(b.s.activation(Dbf[:], scr.pbig[:, 1, 0:128], AF.Exp,
                                           bias=nLtok[:, hh:hh + 1]), "se_s")
                scr.prel("pb1", "se_s", siD)
                b.v.wait_ge(b.sems["se_s"][0], siD)
                viM = b.vself(b.v.tensor_tensor(Msb[:], scr.pbig[:, 0, 0:128], Dbf[:],
                                                op=ALU.mult))
                scr.prel("pb0", "se_v", viM)
                b.t.wait_ge(b.sems["se_v"][0], viM)
                scr.pacq(b, "paux0")
                tiI = b.inc(b.t.matmul(scr.paux[:, 0, 0:256], Msb[:], vh[:],
                                       start=True, stop=True), "se_t")
                b.t.wait_ge(b.sems["se_s"][0], sA)
                scr.pacq(b, "paux1")
                tiQ = b.inc(b.t.matmul(scr.paux[:, 1, 0:256], qh[:],
                                       S0[:, par, hh, :], start=True, stop=True), "se_t")
                b.s.wait_ge(b.sems["se_t"][0], tiI)
                siI = b.inc(b.s.copy(intr[:], scr.paux[:, 0, 0:256]), "se_s")
                scr.prel("paux0", "se_s", siI)
                b.v.wait_ge(b.sems["se_s"][0], siI)
                b.v.wait_ge(b.sems["se_t"][0], tiQ)
                vo = b.inc(b.v.scalar_tensor_tensor(osb[:], scr.paux[:, 1, 0:256],
                                                    are[:, 0:1], intr[:],
                                                    op0=ALU.mult, op1=ALU.add), "se_v")
                scr.prel("paux1", "se_v", vo)
                vg = b.inc(b.v.tensor_tensor(gt_[:], osb[:], ogh[:], op=ALU.mult), "se_v")
                vk = b.inc(b.v.tensor_scalar(ktl[:], kh[:], are[:, 1:2], None,
                                             op0=ALU.mult), "se_v")
                b.t.wait_ge(b.sems["se_v"][0], vk)
                scr.pacq(b, "paux0")
                tiU = b.inc(b.t.matmul(scr.paux[:, 0, 0:256], ktl[:], vh[:],
                                       start=True, stop=True), "se_t")
                b.v.wait_ge(b.sems["se_t"][0], tiU)
                vS = b.vself(b.v.scalar_tensor_tensor(S0[:, oth, hh, :], S0[:, par, hh, :],
                                                      are[:, 2:3], scr.paux[:, 0, 0:256],
                                                      op0=ALU.mult, op1=ALU.add))
                scr.prel("paux0", "se_v", vS)
                b.t.wait_ge(b.sems["se_v"][0], vg)
                for sub in range(2):
                    scr.pacq(b, "pbf")
                    ti = b.inc(b.t.transpose(scr.pbf[:, 128:256],
                                             gt_[:, sub * 128:(sub + 1) * 128],
                                             K.ident_bf16[:]), "se_t")
                    b.s.wait_ge(b.sems["se_t"][0], ti)
                    sgT = b.inc(b.s.copy(gT[:, hh * 2 + sub, :],
                                         scr.pbf[:, 128:256]), "se_s")
                    scr.prel("pbf", "se_s", sgT)
            vlc = b.vself(b.v.tensor_copy(lcsave[:], lcend[:]))
            lcs_tok = None
            b.t.wait_ge(b.sems["se_v"][0], vlc)
            if True:
                scg = ExitStack()
                xg = b.sb("g_xg", [128, DT, 128], F32, scg)
                b.drain_st()
                txg = b.load(xg[:], xap3(xin, bt)[:, :, csl])
                b.wait_ld(b.v, txg)
                b.t.wait_ge(b.sems["se_s"][0], sgT)
                for dt_ in range(DT):
                    slot = dt_ % 2
                    key = f"pb{slot}"
                    scr.pacq(b, key)
                    for ki in range(16):
                        mm = b.t.matmul(scr.pbig[:, slot, 0:128],
                                        wo_[:, ki, dt_ * 128:(dt_ + 1) * 128],
                                        gT[:, ki, :], start=(ki == 0), stop=(ki == 15))
                    ti = b.inc(mm, "se_t")
                    b.v.wait_ge(b.sems["se_t"][0], ti)
                    vi = b.inc(b.v.tensor_tensor(xg[:, dt_, :], xg[:, dt_, :],
                                                 scr.pbig[:, slot, 0:128], op=ALU.add),
                               "se_v")
                    scr.prel(key, "se_v", vi)
                b.wait(b.y, "se_v")
                b.store(xap3(xout, bt)[:, :, csl], xg[:])
                b.barrier()
                scg.close()
        b.barrier()
    sc.close()



SUB_SEQ = ["l0a", "moe", "hgrn", "moe", "l1a", "l1f", "gla", "l1f"]
EMITS = {"l0a": emit_l0a, "moe": emit_moe, "hgrn": emit_hgrn,
         "l1a": emit_l1a, "l1f": emit_l1ffn, "gla": emit_gla}
WKEYS = {
    "l0a": ["wqT", "wqTs", "kvdT", "kupT", "kupTs", "vupT", "woT"],
    "moe": ["rw", "rb_row", "w1", "b1l", "w2", "b2", "sel16"],
    "hgrn": ["hWiT", "hWfT", "hWgT", "hWoT", "hnw"],
    "l1a": [],
    "l1f": ["fWrT", "fWkT", "fWvT", "fmur", "fmuk"],
    "gla": ["gWq", "gWk", "gWv", "gWo", "gWg", "gbg", "gWog"],
}
_CONSTS = host_constants()
DBG_GLA = False


class KTiles:
    pass


def build_program(calls, pshapes):
    nc = bass.Bass("TRN2", target_bir_lowering=False)
    b = Bld(nc)
    K = KTiles()
    K.aps = {}
    for kk, vv in _CONSTS.items():
        dt = BF16 if vv.dtype == ml_dtypes.bfloat16 else F32
        K.aps[kk] = nc.dram_tensor("c_" + kk, list(vv.shape), dt, kind="ExternalInput").ap()
    K.ident_f32 = b.sb("k_idf", [128, 128], F32)
    K.ident_bf16 = b.sb("k_idb", [128, 128], BF16)
    K.ones_r_f32 = b.sb("k_or", [1, 128], F32)
    K.ones_c_f32 = b.sb("k_ocf", [128, 1], F32)
    K.ones_c_bf16 = b.sb("k_ocb", [128, 1], BF16)
    K.ones_row2k = b.sb("k_o2k", [8, S], BF16)
    K.epsln = b.sb("k_eps", [1, 1], F32)
    for nm in ("ident_f32", "ident_bf16", "ones_r_f32", "ones_c_f32",
               "ones_c_bf16", "ones_row2k", "epsln"):
        b.load(getattr(K, nm)[:], K.aps[nm])
    scr = Scratch(b)
    scr.eps = K.epsln
    if "moe" in calls:
        K.w1bf = nc.dram_tensor("moe_w1bf", [16, 1024, 2048], BF16,
                                kind="Internal").ap()
        K.w2bf = nc.dram_tensor("moe_w2bf", [16, 2048, 1024], BF16,
                                kind="Internal").ap()
        K.moe_call = 0
    if DBG_GLA:
        K.dbg_gla_Lf = nc.dram_tensor("dbg_gla_Lf", [BPC, 8, S], F32,
                                      kind="ExternalOutput").ap()
        K.dbg_gla_are = nc.dram_tensor("dbg_gla_are", [BPC, 16, 8, 128, 3], F32,
                                       kind="ExternalOutput").ap()
        K.dbg_gla_lc = nc.dram_tensor("dbg_gla_lc", [BPC, 16, 8, 128, 2], F32,
                                      kind="ExternalOutput").ap()
        K.dbg_gla_x1 = nc.dram_tensor("dbg_gla_x1", [BPC, 16, 128, DT, 128], F32,
                                      kind="ExternalOutput").ap()
        K.dbg_gla_lg = nc.dram_tensor("dbg_gla_lg", [BPC, 16, 8, 128], F32,
                                      kind="ExternalOutput").ap()
    Wap = {}
    need = set()
    for cname in calls:
        need.update(WKEYS[cname])
    for wk in sorted(need):
        if wk in _CONSTS:
            Wap[wk] = K.aps[wk]
        else:
            Wap[wk] = nc.dram_tensor("w_" + wk, list(pshapes[wk]), F32,
                                     kind="ExternalInput").ap()
    xin = nc.dram_tensor("xin", [BPC, D, S], F32, kind="ExternalInput").ap()
    xout = nc.dram_tensor("xout", [BPC, D, S], F32, kind="ExternalOutput").ap()
    mids = [nc.dram_tensor(f"xmid{i}", [BPC, D, S], F32, kind="Internal").ap()
            for i in range(2)]
    for e in (b.t, b.v, b.s, b.g, b.y):
        b.wait_ld_all(e)
    for i, cname in enumerate(calls):
        src = xin if i == 0 else mids[(i - 1) % 2]
        dst = xout if i == len(calls) - 1 else mids[i % 2]
        EMITS[cname](b, K, scr, Wap, src, dst)
    b.drain_st()
    b.es.close()
    return nc


def in_map_for(calls, p, xshard):
    m = {("c_" + kk): vv for kk, vv in _CONSTS.items()}
    need = set()
    for cname in calls:
        need.update(WKEYS[cname])
    for wk in sorted(need):
        if wk not in _CONSTS:
            m["w_" + wk] = p[wk]
    m["xin"] = xshard
    return m


FUSED = True


def kernel(**inputs):
    p = prep_inputs(inputs)
    pshapes = {kk: vv.shape for kk, vv in p.items()}
    x = np.asarray(inputs["L0_attn_a_x"])
    ncore = 8
    shards = [np.ascontiguousarray(x[c * BPC:(c + 1) * BPC].transpose(0, 2, 1))
              for c in range(ncore)]
    if FUSED:
        calls = SUB_SEQ
        nc = build_program(calls, pshapes)
        in_maps = [in_map_for(calls, p, shards[c]) for c in range(ncore)]
        res = run_bass_kernel_spmd(nc, in_maps, core_ids=list(range(ncore)))
        cur = [res.results[c]["xout"] for c in range(ncore)]
    else:
        progs = {name: build_program([name], pshapes)
                 for name in dict.fromkeys(SUB_SEQ)}
        cur = shards
        for name in SUB_SEQ:
            in_maps = [in_map_for([name], p, cur[c]) for c in range(ncore)]
            res = run_bass_kernel_spmd(progs[name], in_maps,
                                       core_ids=list(range(ncore)))
            cur = [res.results[c]["xout"] for c in range(ncore)]
    outs = [np.ascontiguousarray(np.asarray(cur[c]).transpose(0, 2, 1))
            for c in range(ncore)]
    return np.concatenate(outs, axis=0).astype(np.float32)


# version 07
# speedup vs baseline: 1.2251x; 1.0361x over previous
import sys
sys.path.insert(0, '/opt/trn_rl_repo')
from contextlib import ExitStack

import numpy as np
import ml_dtypes

import concourse.bass as bass
import concourse.mybir as mybir
from concourse.bass_utils import run_bass_kernel_spmd

F32 = mybir.dt.float32
BF16 = mybir.dt.bfloat16
AF = mybir.ActivationFunctionType
ALU = mybir.AluOpType
AX = mybir.AxisListType

S = 2048
D = 1024
DT = D // 128
BPC = 2
NEG = -1.0e30
KSCALE = float(1.0 / np.sqrt(128.0))



def host_constants():
    bf = ml_dtypes.bfloat16
    c = {}
    c["ident_f32"] = np.eye(128, dtype=np.float32)
    c["ident_bf16"] = np.eye(128, dtype=bf)
    c["ones_r_f32"] = np.ones((1, 128), np.float32)
    c["ones_r_bf16"] = np.ones((1, 128), bf)
    c["ones_c_f32"] = np.ones((128, 1), np.float32)
    c["ones_c_bf16"] = np.ones((128, 1), bf)
    c["ones_row2k"] = np.ones((8, S), ml_dtypes.bfloat16)
    kp = np.arange(128)[:, None]
    cq = np.arange(896)[None, :]
    c["mask_causal"] = np.where(kp <= cq - 384, 0.0, NEG).astype(np.float32)
    t_ = np.arange(128)
    c["mask_tri"] = np.where(t_[:, None] <= t_[None, :], 0.0, NEG).astype(np.float32)
    hd = 64
    inv = 1.0 / (10000.0 ** (np.arange(0, hd, 2, dtype=np.float64) / hd))
    pos = np.arange(S, dtype=np.float64)
    j = np.arange(128)
    th = pos[None, :] * inv[(j % hd) % 32][:, None]
    c["rope_cos"] = np.cos(th).astype(np.float32)
    sgn = np.where((j % hd) < 32, -1.0, 1.0)[:, None]
    c["rope_sin"] = (sgn * np.sin(th)).astype(np.float32)
    c["epsln"] = np.full((1, 1), 1e-5, np.float32)
    sel = np.zeros((16, 16, 128), np.float32)
    for e in range(16):
        sel[e, e, :] = 1.0
    c["sel16"] = sel
    return c


def _rope_perm():
    j = np.arange(1024)
    h, d = j // 64, j % 64
    sw = np.where(d < 32, d + 32, d - 32)
    return h * 64 + sw


def prep_inputs(inputs):
    f = np.ascontiguousarray
    perm = _rope_perm()
    p = {}
    p["wqT"] = f(inputs["L0_attn_a_wq"].T)
    p["wqTs"] = f(p["wqT"][:, perm])
    p["kvdT"] = f(inputs["L0_attn_a_w_kv_down"].T)
    p["kupT"] = f(inputs["L0_attn_a_w_k_up"].T)
    p["kupTs"] = f(p["kupT"][:, perm])
    p["vupT"] = f(inputs["L0_attn_a_w_v_up"].T)
    p["woT"] = f(inputs["L0_attn_a_wo"].T)
    p["rw"] = f(inputs["L0_ffn_router_w"])
    p["rb_row"] = f(inputs["L0_ffn_router_b"].reshape(1, 16))
    p["w1"] = f(inputs["L0_ffn_w1"])
    p["b1l"] = f(inputs["L0_ffn_b1"].reshape(16, 16, 128).transpose(0, 2, 1))
    p["w2"] = f(inputs["L0_ffn_w2"])
    p["b2"] = f(inputs["L0_ffn_b2"])
    p["hWiT"] = f(inputs["L0_attn_b_Wi"].T)
    p["hWfT"] = f(inputs["L0_attn_b_Wf"].T)
    p["hWgT"] = f(inputs["L0_attn_b_Wg"].T)
    p["hWoT"] = f(inputs["L0_attn_b_Wo"].T)
    p["hnw"] = f(inputs["L0_attn_b_norm_w"].reshape(DT, 128).T)
    p["gWq"] = f(inputs["L1_attn_b_Wq"])
    p["gWk"] = f(inputs["L1_attn_b_Wk"])
    p["gWv"] = f(inputs["L1_attn_b_Wv"])
    p["gWo"] = f(inputs["L1_attn_b_Wo"])
    p["gWg"] = f(inputs["L1_attn_b_Wg"].reshape(DT, 128, 8).transpose(1, 0, 2))
    p["gbg"] = f(inputs["L1_attn_b_bg"].reshape(8, 1))
    p["gWog"] = f(inputs["L1_attn_b_Wog"])
    p["fWrT"] = f(inputs["L1_ffn_W_r"].T)
    p["fWkT"] = f(inputs["L1_ffn_W_k"].T)
    p["fWvT"] = f(inputs["L1_ffn_W_v"].T)
    p["fmur"] = f(inputs["L1_ffn_mu_r"].reshape(DT, 128).T)
    p["fmuk"] = f(inputs["L1_ffn_mu_k"].reshape(DT, 128).T)
    return p




class Bld:
    def __init__(self, nc):
        self.nc = nc
        self.es = ExitStack()
        self.t, self.v, self.s, self.g, self.y = (
            nc.tensor, nc.vector, nc.scalar, nc.gpsimd, nc.sync)
        self.sems = {}
        for n in ("st", "se_t", "se_v", "se_s", "se_g"):
            self.sems[n] = [self.es.enter_context(nc.semaphore(n)), 0]
        self.NLD = 4
        self.ld = [[self.es.enter_context(nc.semaphore(f"ld{i}")), 0]
                   for i in range(self.NLD)]
        self.ld_rr = 0
        self.NLG = 2
        self.ldg = [[self.es.enter_context(nc.semaphore(f"ldg{i}")), 0]
                    for i in range(self.NLG)]
        self.ldg_rr = 0
        self.uid = 0

    def nm(self, base):
        self.uid += 1
        return f"{base}_{self.uid}"

    def inc(self, instr, name, amt=1):
        h = self.sems[name]
        h[1] += amt
        instr.then_inc(h[0], amt)
        return h[1]

    def val(self, name):
        return self.sems[name][1]

    def wait(self, eng, name, v=None):
        h = self.sems[name]
        eng.wait_ge(h[0], h[1] if v is None else v)

    def vself(self, instr):
        v = self.inc(instr, "se_v")
        self.v.wait_ge(self.sems["se_v"][0], v)
        return v

    def sself(self, instr):
        v = self.inc(instr, "se_s")
        self.s.wait_ge(self.sems["se_s"][0], v)
        return v

    def sb(self, base, shape, dtype, scope=None, side="left"):
        return (scope or self.es).enter_context(
            self.nc.sbuf_tensor(self.nm(base), shape, dtype, side=side))

    def ps(self, base, shape, dtype, scope=None):
        return (scope or self.es).enter_context(
            self.nc.psum_tensor(self.nm(base), shape, dtype))

    def load(self, out, in_):
        si = self.ld_rr
        self.ld_rr = (self.ld_rr + 1) % self.NLD
        h = self.ld[si]
        if h[1] > 0:
            self.y.wait_ge(h[0], h[1])
        h[1] += 16
        self.y.dma_start(out=out, in_=in_).then_inc(h[0], 16)
        return (si, h[1])

    def wait_ld(self, eng, tok):
        eng.wait_ge(self.ld[tok[0]][0], tok[1])

    def load_g(self, out, in_):
        si = self.ldg_rr
        self.ldg_rr = (self.ldg_rr + 1) % self.NLG
        h = self.ldg[si]
        if h[1] > 0:
            self.g.wait_ge(h[0], h[1])
        h[1] += 16
        self.g.dma_start(out=out, in_=in_).then_inc(h[0], 16)
        return (si, h[1])

    def wait_ldg(self, eng, tok):
        eng.wait_ge(self.ldg[tok[0]][0], tok[1])

    def wait_ld_all(self, eng):
        for h in self.ld:
            if h[1] > 0:
                eng.wait_ge(h[0], h[1])

    def store(self, out, in_):
        return self.inc(self.y.dma_start(out=out, in_=in_), "st", 16)

    def drain_st(self):
        h = self.sems["st"]
        if h[1] > 0:
            for e in (self.t, self.v, self.s, self.g, self.y):
                e.wait_ge(h[0], h[1])

    def barrier(self):
        self.nc.all_engine_barrier()


class Scratch:
    def __init__(self, b: Bld):
        self.pbig = b.ps("p_big", [128, 3, 512], F32)
        self.paux = b.ps("p_aux", [128, 2, 512], F32)
        self.pst = b.ps("p_st", [1, 2, 512], F32)
        self.pbf = b.ps("p_bf", [128, 1024], BF16)
        self.sq2 = b.sb("s_sq2", [128, 2, 512], BF16)
        self.ab = b.sb("s_ab", [1, 2, S], F32)
        self.ab16 = b.sb("s_ab16", [1, 2, 512], BF16)
        self.mv = b.sb("s_mv", [1, 2, 512], F32)
        self.Abc = b.sb("s_Abc", [128, 512], F32)
        self.Bbc = b.sb("s_Bbc", [128, 512], F32)
        self.tmpa = b.sb("s_tmpa", [128, 512], F32)
        self.tmpb = b.sb("s_tmpb", [128, 512], F32)
        self.stage = b.sb("s_stage", [128, 2, 512], F32)
        self.stg_par = 0
        self.stg_tok = [0, 0]
        self.ptok = {}

    def pacq(self, b: Bld, key):
        tok = self.ptok.get(key)
        if tok is not None:
            b.wait(b.t, tok[0], tok[1])

    def prel(self, key, sem, val):
        self.ptok[key] = (sem, val)


def load_wbf(b: Bld, scr: Scratch, dst, src_ap, kt_total, fdim, tguard=None):
    src3 = src_ap.rearrange("(kt p) f -> p kt f", p=128)
    nh = (fdim + 511) // 512
    gi = None
    if tguard is not None:
        b.g.wait_ge(b.sems["se_t"][0], tguard)
    for k0 in range(kt_total):
        for hh in range(nh):
            f0 = hh * 512
            f1 = min(fdim, f0 + 512)
            par = scr.stg_par
            scr.stg_par = 1 - scr.stg_par
            if scr.stg_tok[par] > 0:
                b.y.wait_ge(b.sems["se_g"][0], scr.stg_tok[par])
            tok = b.load(scr.stage[:, par, 0:f1 - f0], src3[:, k0, f0:f1])
            b.wait_ld(b.g, tok)
            gi = b.inc(b.g.tensor_copy(dst[:, k0, f0:f1], scr.stage[:, par, 0:f1 - f0]),
                       "se_g")
            scr.stg_tok[par] = gi
    return gi



def emit_ln(b, K, scr, x3, ln3, ncols, apply=True, out_off=0):
    cw = min(512, ncols)
    si3 = vi_st = ti3p = None
    vln = None
    tis = []
    for c0 in range(0, ncols, cw):
        sl = slice(c0, c0 + cw)
        osl = slice(c0 + out_off, c0 + out_off + cw)
        for kt in range(DT):
            if len(tis) >= 2:
                b.s.wait_ge(b.sems["se_t"][0], tis[-2])
            sqs = scr.sq2[:, kt % 2, 0:cw]
            si = b.inc(b.s.activation(sqs, x3[:, kt, sl], AF.Square), "se_s")
            if kt == 0:
                if vi_st is not None:
                    b.t.wait_ge(b.sems["se_v"][0], vi_st)
                scr.pacq(b, "pst0")
                scr.pacq(b, "pst1")
            b.t.matmul(scr.pst[:, 0, 0:cw], K.ones_c_f32[:], x3[:, kt, sl],
                       start=(kt == 0), stop=(kt == DT - 1))
            b.t.wait_ge(b.sems["se_s"][0], si)
            mm = b.t.matmul(scr.pst[:, 1, 0:cw], K.ones_c_bf16[:], sqs,
                            start=(kt == 0), stop=(kt == DT - 1))
            tis.append(b.inc(mm, "se_t"))
        ti2 = tis[-1]
        m, var = scr.mv[0:1, 0, 0:cw], scr.mv[0:1, 1, 0:cw]
        a, bsh = scr.ab[0:1, 0, osl], scr.ab[0:1, 1, osl]
        b.v.wait_ge(b.sems["se_t"][0], ti2)
        b.vself(b.v.tensor_scalar(m, scr.pst[:, 0, 0:cw], 1.0 / D, None, op0=ALU.mult))
        b.vself(b.v.tensor_scalar(var, scr.pst[:, 1, 0:cw], 1.0 / D, None, op0=ALU.mult))
        b.vself(b.v.tensor_tensor(scr.tmpa[0:1, 0:cw], m, m, op=ALU.mult))
        vi_st = b.vself(b.v.tensor_tensor(var, var, scr.tmpa[0:1, 0:cw], op=ALU.subtract))
        scr.prel("pst0", "se_v", vi_st)
        scr.prel("pst1", "se_v", vi_st)
        b.s.wait_ge(b.sems["se_v"][0], vi_st)
        b.sself(b.s.activation(scr.tmpa[0:1, 0:cw], var, AF.Ln, bias=scr.eps[0:1, 0:1]))
        si2 = b.sself(b.s.activation(a, scr.tmpa[0:1, 0:cw], AF.Exp, scale=-0.5))
        b.v.wait_ge(b.sems["se_s"][0], si2)
        vi2 = b.vself(b.v.scalar_tensor_tensor(bsh, m, -1.0, a, op0=ALU.mult, op1=ALU.mult))
        b.s.wait_ge(b.sems["se_v"][0], vi2)
        if ti3p is not None:
            b.s.wait_ge(b.sems["se_t"][0], ti3p)
        b.s.copy(scr.ab16[0:1, 0, 0:cw], a)
        si_c = b.inc(b.s.copy(scr.ab16[0:1, 1, 0:cw], bsh), "se_s")
        b.t.wait_ge(b.sems["se_s"][0], si_c)
        scr.pacq(b, "paux0")
        scr.pacq(b, "paux1")
        b.t.matmul(scr.paux[:, 0, 0:cw], K.ones_r_bf16[:], scr.ab16[0:1, 0, 0:cw],
                   start=True, stop=True)
        ti3 = b.inc(b.t.matmul(scr.paux[:, 1, 0:cw], K.ones_r_bf16[:],
                               scr.ab16[0:1, 1, 0:cw], start=True, stop=True), "se_t")
        ti3p = ti3
        b.s.wait_ge(b.sems["se_t"][0], ti3)
        b.s.copy(scr.Abc[:, 0:cw], scr.paux[:, 0, 0:cw])
        si3 = b.inc(b.s.copy(scr.Bbc[:, 0:cw], scr.paux[:, 1, 0:cw]), "se_s")
        scr.prel("paux0", "se_s", si3)
        scr.prel("paux1", "se_s", si3)
        if apply:
            b.v.wait_ge(b.sems["se_s"][0], si3)
            for kt in range(DT):
                b.v.tensor_tensor(scr.tmpb[:, 0:cw], x3[:, kt, sl], scr.Abc[:, 0:cw],
                                  op=ALU.mult)
                ins = b.v.tensor_tensor(ln3[:, kt, osl], scr.tmpb[:, 0:cw], scr.Bbc[:, 0:cw],
                                       op=ALU.add)
            vln = b.inc(ins, "se_v")
    return vln


def xap3(x_ap, batch):
    return x_ap[batch].rearrange("(kt p) s -> p kt s", p=128)


def ln_from_hbm(b, K, scr, xin, bt, ln):
    for half in range(2):
        sch = ExitStack()
        xh = b.sb("lnx", [128, DT, 1024], F32, sch)
        b.drain_st()
        tx = b.load(xh[:], xap3(xin, bt)[:, :, half * 1024:(half + 1) * 1024])
        for e in (b.s, b.t, b.v):
            b.wait_ld(e, tx)
        vln = emit_ln(b, K, scr, xh, ln, 1024, out_off=half * 1024)
        b.barrier()
        sch.close()
    return vln



def emit_l1ffn(b, K, scr, Wap, xin, xout):
    sc = ExitStack()
    b.drain_st()
    mu = b.sb("f_mu", [128, 2, DT], F32, sc)
    om = b.sb("f_om", [128, 2, DT], F32, sc)
    wr = b.sb("f_wr", [128, DT, D], BF16, sc)
    wv = b.sb("f_wv", [128, DT, D], BF16, sc)
    ln = b.sb("f_ln", [128, DT, S], BF16, sc)
    xr = b.sb("f_xr", [128, DT, S], BF16, sc)
    rt = b.sb("f_rt", [128, 512], BF16, sc)
    kact = ln

    tmu1 = b.load(mu[:, 0, :], Wap["fmur"])
    tmu2 = b.load(mu[:, 1, :], Wap["fmuk"])
    b.wait_ld(b.v, tmu1)
    b.wait_ld(b.v, tmu2)
    vom = b.inc(b.v.tensor_scalar(om[:], mu[:], -1.0, 1.0, op0=ALU.mult, op1=ALU.add),
                "se_v")
    gr = load_wbf(b, scr, wr, Wap["fWrT"], DT, D)
    gv = load_wbf(b, scr, wv, Wap["fWvT"], DT, D)

    for bt in range(BPC):
        vln = ln_from_hbm(b, K, scr, xin, bt, ln)
        scb = ExitStack()
        xk = b.sb("f_xk", [128, DT, S], BF16, scb)
        wk = b.sb("f_wk", [128, DT, D], BF16, scb)
        gk = load_wbf(b, scr, wk, Wap["fWkT"], DT, D)
        b.v.wait_ge(b.sems["se_v"][0], vom)
        for kt in range(DT):
            for dst, mi in ((xr, 0), (xk, 1)):
                b.v.tensor_scalar(dst[:, kt, :], ln[:, kt, :], mu[:, mi, kt:kt + 1],
                                  None, op0=ALU.mult)
                for c0 in range(0, S, 512):
                    lo = max(c0, 1)
                    hi = c0 + 512
                    b.v.tensor_scalar(scr.tmpb[:, 0:hi - lo], ln[:, kt, lo - 1:hi - 1],
                                      om[:, mi, kt:kt + 1], None, op0=ALU.mult)
                    ins = b.v.tensor_tensor(dst[:, kt, lo:hi], dst[:, kt, lo:hi],
                                            scr.tmpb[:, 0:hi - lo], op=ALU.add)
        vsh = b.inc(ins, "se_v")
        b.t.wait_ge(b.sems["se_v"][0], vsh)
        b.t.wait_ge(b.sems["se_g"][0], gk)
        for c in range(4):
            sl = slice(c * 512, (c + 1) * 512)
            for ft in range(DT):
                slot = (ft * 4 + c) % 3
                key = f"pb{slot}"
                scr.pacq(b, key)
                for kt in range(DT):
                    mm = b.t.matmul(scr.pbig[:, slot, :], wk[:, kt, ft * 128:(ft + 1) * 128],
                                    xk[:, kt, sl], start=(kt == 0), stop=(kt == DT - 1))
                ti = b.inc(mm, "se_t")
                b.s.wait_ge(b.sems["se_t"][0], ti)
                si = b.inc(b.s.activation(scr.tmpa[:, :], scr.pbig[:, slot, :], AF.Relu),
                           "se_s")
                scr.prel(key, "se_s", si)
                b.v.wait_ge(b.sems["se_s"][0], si)
                vi = b.inc(b.v.tensor_tensor(kact[:, ft, sl], scr.tmpa[:, :], scr.tmpa[:, :],
                                             op=ALU.mult), "se_v")
        scb.close()
        scb2 = ExitStack()
        x2 = b.sb("f_x2", [128, DT, S], F32, scb2)
        b.wait(b.y, "se_v")
        b.drain_st()
        tx2 = b.load(x2[:], xap3(xin, bt))
        b.wait_ld(b.v, tx2)
        b.t.wait_ge(b.sems["se_v"][0], vi)
        b.t.wait_ge(b.sems["se_g"][0], gv)
        for c in range(4):
            sl = slice(c * 512, (c + 1) * 512)
            for ft in range(DT):
                slot = (ft + c) % 2
                key = f"pb{slot}"
                scr.pacq(b, key)
                for kt in range(DT):
                    mm = b.t.matmul(scr.pbig[:, slot, :], wr[:, kt, ft * 128:(ft + 1) * 128],
                                    xr[:, kt, sl], start=(kt == 0), stop=(kt == DT - 1))
                ti = b.inc(mm, "se_t")
                b.s.wait_ge(b.sems["se_t"][0], ti)
                b.s.wait_ge(b.sems["se_v"][0], b.val("se_v"))
                si = b.inc(b.s.activation(rt[:], scr.pbig[:, slot, :], AF.Sigmoid), "se_s")
                scr.prel(key, "se_s", si)
                scr.pacq(b, "pb2")
                for kt in range(DT):
                    mm = b.t.matmul(scr.pbig[:, 2, :], wv[:, kt, ft * 128:(ft + 1) * 128],
                                    kact[:, kt, sl], start=(kt == 0), stop=(kt == DT - 1))
                ti = b.inc(mm, "se_t")
                b.v.wait_ge(b.sems["se_t"][0], ti)
                b.v.wait_ge(b.sems["se_s"][0], si)
                b.v.tensor_tensor(scr.tmpa[:, :], scr.pbig[:, 2, :], rt[:], op=ALU.mult)
                vi = b.inc(b.v.tensor_tensor(x2[:, ft, sl], x2[:, ft, sl], scr.tmpa[:, :],
                                             op=ALU.add), "se_v")
                scr.prel("pb2", "se_v", vi)
        b.wait(b.y, "se_v")
        b.store(xap3(xout, bt), x2[:])
        b.barrier()
        scb2.close()
    sc.close()



def emit_hgrn(b, K, scr, Wap, xin, xout):
    sc = ExitStack()
    b.drain_st()
    nw = b.sb("h_nw", [128, DT], F32, sc)
    ln = b.sb("h_ln", [128, DT, S], BF16, sc)
    h = b.sb("h_h", [128, DT, S], BF16, sc)

    b.load(nw[:], Wap["hnw"])

    for bt in range(BPC):
        ln_from_hbm(b, K, scr, xin, bt, ln)
        scb = ExitStack()
        wi = b.sb("h_wi", [128, DT, D], BF16, scb)
        wf = b.sb("h_wf", [128, DT, D], BF16, scb)
        sit = b.sb("h_si", [128, S], BF16, scb)
        ftg = b.sb("h_f", [128, S], BF16, scb)
        inp = b.sb("h_inp", [128, S], BF16, scb)
        tmpS = b.sb("h_tmpS", [128, S], F32, scb)
        load_wbf(b, scr, wi, Wap["hWiT"], DT, D)
        gf = load_wbf(b, scr, wf, Wap["hWfT"], DT, D)
        b.t.wait_ge(b.sems["se_g"][0], gf)
        vsc = b.val("se_v")
        for dt_ in range(DT):
            b.s.wait_ge(b.sems["se_v"][0], vsc)
            for c in range(4):
                sl = slice(c * 512, (c + 1) * 512)
                for w_, act, dst, slot in ((wi, AF.Silu, sit, c % 2), (wf, AF.Sigmoid, ftg, 2)):
                    key = f"pb{slot}"
                    scr.pacq(b, key)
                    for kt in range(DT):
                        mm = b.t.matmul(scr.pbig[:, slot, :],
                                        w_[:, kt, dt_ * 128:(dt_ + 1) * 128],
                                        ln[:, kt, sl], start=(kt == 0), stop=(kt == DT - 1))
                    ti = b.inc(mm, "se_t")
                    b.s.wait_ge(b.sems["se_t"][0], ti)
                    si = b.inc(b.s.activation(dst[:, sl], scr.pbig[:, slot, :], act), "se_s")
                    scr.prel(key, "se_s", si)
            b.v.wait_ge(b.sems["se_s"][0], si)
            b.v.tensor_scalar(tmpS[:], ftg[:], -1.0, 1.0, op0=ALU.mult, op1=ALU.add)
            b.v.tensor_tensor(inp[:], sit[:], tmpS[:], op=ALU.mult)
            vsc = b.inc(b.v.tensor_tensor_scan(h[:, dt_, :], ftg[:], inp[:], 0.0,
                                               op0=ALU.mult, op1=ALU.add), "se_v")
        b.barrier()
        scb.close()
        scb = ExitStack()
        Arms = b.sb("h_Arms", [128, 4, 512], F32, scb)
        wg = b.sb("h_wg", [128, DT, D], BF16, scb)
        gg = load_wbf(b, scr, wg, Wap["hWgT"], DT, D)
        b.s.wait_ge(b.sems["se_v"][0], vsc)
        tis = []
        for c in range(4):
            sl = slice(c * 512, (c + 1) * 512)
            for kt in range(DT):
                if len(tis) >= 2:
                    b.s.wait_ge(b.sems["se_t"][0], tis[-2])
                sqs = scr.sq2[:, kt % 2, :]
                si = b.inc(b.s.activation(sqs, h[:, kt, sl], AF.Square), "se_s")
                if kt == 0:
                    scr.pacq(b, "pst1")
                b.t.wait_ge(b.sems["se_s"][0], si)
                mm = b.t.matmul(scr.pst[:, 1, :], K.ones_c_bf16[:], sqs,
                                start=(kt == 0), stop=(kt == DT - 1))
                tis.append(b.inc(mm, "se_t"))
            ti2 = tis[-1]
            b.s.wait_ge(b.sems["se_t"][0], ti2)
            b.sself(b.s.activation(scr.ab[0:1, 0, 0:512], scr.pst[:, 1, :], AF.Ln,
                                   bias=scr.eps[0:1, 0:1], scale=1.0 / D))
            si = b.sself(b.s.activation(scr.ab16[0:1, 1, 0:512], scr.ab[0:1, 0, 0:512],
                                        AF.Exp, scale=-0.5))
            scr.prel("pst1", "se_s", si)
            b.t.wait_ge(b.sems["se_s"][0], si)
            scr.pacq(b, "paux0")
            ti3 = b.inc(b.t.matmul(scr.paux[:, 0, :], K.ones_r_bf16[:],
                                   scr.ab16[0:1, 1, 0:512], start=True, stop=True), "se_t")
            b.s.wait_ge(b.sems["se_t"][0], ti3)
            si = b.inc(b.s.copy(Arms[:, c, :], scr.paux[:, 0, :]), "se_s")
            scr.prel("paux0", "se_s", si)
        b.t.wait_ge(b.sems["se_g"][0], gg)
        b.v.wait_ge(b.sems["se_s"][0], si)
        for dt_ in range(DT):
            for c in range(4):
                sl = slice(c * 512, (c + 1) * 512)
                slot = (dt_ * 4 + c) % 3
                key = f"pb{slot}"
                scr.pacq(b, key)
                for kt in range(DT):
                    mm = b.t.matmul(scr.pbig[:, slot, :], wg[:, kt, dt_ * 128:(dt_ + 1) * 128],
                                    ln[:, kt, sl], start=(kt == 0), stop=(kt == DT - 1))
                ti = b.inc(mm, "se_t")
                b.s.wait_ge(b.sems["se_t"][0], ti)
                b.s.wait_ge(b.sems["se_v"][0], b.val("se_v"))
                si = b.inc(b.s.activation(scr.tmpa[:, :], scr.pbig[:, slot, :], AF.Silu),
                           "se_s")
                scr.prel(key, "se_s", si)
                b.v.wait_ge(b.sems["se_s"][0], si)
                b.v.tensor_tensor(scr.tmpb[:, :], h[:, dt_, sl], Arms[:, c, :], op=ALU.mult)
                b.v.tensor_scalar(scr.tmpb[:, :], scr.tmpb[:, :], nw[:, dt_:dt_ + 1],
                                  None, op0=ALU.mult)
                vi = b.inc(b.v.tensor_tensor(h[:, dt_, sl], scr.tmpb[:, :], scr.tmpa[:, :],
                                             op=ALU.mult), "se_v")
        b.barrier()
        scb.close()
        scb2 = ExitStack()
        wo = b.sb("h_wo", [128, DT, D], BF16, scb2)
        x2 = b.sb("h_x2", [128, DT, S], F32, scb2)
        go = load_wbf(b, scr, wo, Wap["hWoT"], DT, D)
        b.wait(b.y, "se_v")
        b.drain_st()
        tx2 = b.load(x2[:], xap3(xin, bt))
        b.wait_ld(b.v, tx2)
        b.t.wait_ge(b.sems["se_g"][0], go)
        b.t.wait_ge(b.sems["se_v"][0], vi)
        for c in range(4):
            sl = slice(c * 512, (c + 1) * 512)
            for ft in range(DT):
                slot = (ft * 4 + c) % 3
                key = f"pb{slot}"
                scr.pacq(b, key)
                for kt in range(DT):
                    mm = b.t.matmul(scr.pbig[:, slot, :], wo[:, kt, ft * 128:(ft + 1) * 128],
                                    h[:, kt, sl], start=(kt == 0), stop=(kt == DT - 1))
                ti = b.inc(mm, "se_t")
                b.v.wait_ge(b.sems["se_t"][0], ti)
                vi = b.inc(b.v.tensor_tensor(x2[:, ft, sl], x2[:, ft, sl],
                                             scr.pbig[:, slot, :], op=ALU.add), "se_v")
                scr.prel(key, "se_v", vi)
        b.wait(b.y, "se_v")
        b.store(xap3(xout, bt), x2[:])
        b.barrier()
        scb2.close()
    sc.close()



def emit_moe(b, K, scr, Wap, xin, xout):
    HS = 1024
    sc = ExitStack()
    b.drain_st()
    rw = b.sb("m_rw", [128, DT, 16], F32, sc)
    rbr = b.sb("m_rbr", [1, 16], F32, sc)
    rbbc = b.sb("m_rbbc", [128, 16], F32, sc)
    csbc = b.sb("m_csbc", [128, 16], F32, sc)
    sel = b.sb("m_sel", [16, 16, 128], F32, sc)
    b2 = b.sb("m_b2", [16, D], F32, sc)
    cT = b.sb("m_cT", [16, HS], F32, sc)
    l0T = cT
    atok = b.sb("m_atok", [128, 2], F32, sc)
    ttok = b.sb("m_ttok", [128, 3, 16], F32, sc)
    red = b.sb("m_red", [128, 4], F32, sc)
    b1sb = b.sb("m_b1", [128, 16], F32, sc)
    cbc = b.sb("m_cbc", [128, 512], F32, sc)
    hmid = b.sb("m_hmid", [128, 16, 512], BF16, sc)
    gel = b.sb("m_gel", [128, 512], BF16, sc)
    ln = b.sb("m_ln", [128, DT, HS], BF16, sc)
    w1 = b.sb("m_w1", [128, DT, 2048], BF16, sc)
    w2 = b.sb("m_w2", [128, 16, D], BF16, sc)
    x_sb = b.sb("m_x", [128, DT, HS], F32, sc)

    b.load(rw[:], Wap["rw"].rearrange("(kt p) e -> p kt e", p=128))
    b.load(sel[:], Wap["sel16"])
    b.load(rbr[:], Wap["rb_row"])
    tb2 = b.load(b2[:], Wap["b2"])
    b.wait_ld(b.t, tb2)
    scr.pacq(b, "paux0")
    ti = b.inc(b.t.matmul(scr.paux[:, 0, 0:16], K.ones_r_f32[:], rbr[:],
                          start=True, stop=True), "se_t")
    b.s.wait_ge(b.sems["se_t"][0], ti)
    si = b.inc(b.s.copy(rbbc[:], scr.paux[:, 0, 0:16]), "se_s")
    scr.prel("paux0", "se_s", si)
    scr.pacq(b, "pst0")
    for kt in range(DT):
        mm = b.t.matmul(scr.pst[:, 0, 0:16], K.ones_c_f32[:], rw[:, kt, :],
                        start=(kt == 0), stop=(kt == DT - 1))
    ti = b.inc(mm, "se_t")
    b.s.wait_ge(b.sems["se_t"][0], ti)
    si = b.inc(b.s.copy(scr.ab[0:1, 0, 0:16], scr.pst[:, 0, 0:16]), "se_s")
    scr.prel("pst0", "se_s", si)
    b.t.wait_ge(b.sems["se_s"][0], si)
    scr.pacq(b, "paux0")
    ti = b.inc(b.t.matmul(scr.paux[:, 0, 0:16], K.ones_r_f32[:], scr.ab[0:1, 0, 0:16],
                          start=True, stop=True), "se_t")
    b.s.wait_ge(b.sems["se_t"][0], ti)
    si = b.inc(b.s.copy(csbc[:], scr.paux[:, 0, 0:16]), "se_s")
    scr.prel("paux0", "se_s", si)

    lastw1 = lastw2 = lastgel = None
    for bt in range(BPC):
        for hf in range(2):
            hsl = slice(hf * HS, (hf + 1) * HS)
            b.drain_st()
            tx = b.load(x_sb[:], xap3(xin, bt)[:, :, hsl])
            for e in (b.s, b.t, b.v):
                b.wait_ld(e, tx)
            emit_ln(b, K, scr, x_sb, ln, HS)
            for c in range(2):
                sl = slice(c * 512, (c + 1) * 512)
                slot = c % 3
                key = f"pb{slot}"
                scr.pacq(b, key)
                for kt in range(DT):
                    mm = b.t.matmul(scr.pbig[0:16, slot, :], rw[:, kt, :], x_sb[:, kt, sl],
                                    start=(kt == 0), stop=(kt == DT - 1))
                ti = b.inc(mm, "se_t")
                b.s.wait_ge(b.sems["se_t"][0], ti)
                si = b.inc(b.s.copy(l0T[:, sl], scr.pbig[0:16, slot, :]), "se_s")
                scr.prel(key, "se_s", si)
            if hasattr(K, "dbg_lg"):
                b.wait(b.y, "se_s")
                b.store(K.dbg_lg[bt * 2 + hf], l0T[:])
            for it in range(8):
                tsl = slice(it * 128, (it + 1) * 128)
                b.t.wait_ge(b.sems["se_s"][0], si)
                scr.pacq(b, "pb0")
                tlg = b.inc(b.t.transpose(scr.pbig[:, 0, 0:16], l0T[:, tsl],
                                          K.ident_f32[0:16, 0:16]), "se_t")
                scr.pacq(b, "pb1")
                b.t.transpose(scr.pbig[:, 1, 0:1], scr.ab[0:1, 0, tsl],
                              K.ident_f32[0:1, 0:1])
                tab = b.inc(b.t.transpose(scr.pbig[:, 1, 1:2], scr.ab[0:1, 1, tsl],
                                          K.ident_f32[0:1, 0:1]), "se_t")
                b.v.wait_ge(b.sems["se_t"][0], tab)
                b.vself(b.v.tensor_copy(atok[:, 0:1], scr.pbig[:, 1, 0:1]))
                vi = b.vself(b.v.tensor_copy(atok[:, 1:2], scr.pbig[:, 1, 1:2]))
                scr.prel("pb1", "se_v", vi)
                b.vself(b.v.tensor_scalar(ttok[:, 0, :], scr.pbig[:, 0, 0:16],
                                          atok[:, 0:1], None, op0=ALU.mult))
                vi = b.vself(b.v.scalar_tensor_tensor(ttok[:, 0, :], csbc[:], atok[:, 1:2],
                                                      ttok[:, 0, :], op0=ALU.mult,
                                                      op1=ALU.add))
                scr.prel("pb0", "se_v", vi)
                b.vself(b.v.tensor_tensor(ttok[:, 0, :], ttok[:, 0, :], rbbc[:], op=ALU.add))
                vi = b.vself(b.v.reduce_max(red[:, 0:1], ttok[:, 0, :], axis=AX.X,
                                            negate=True))
                b.s.wait_ge(b.sems["se_v"][0], vi)
                si = b.inc(b.s.activation(ttok[:, 1, :], ttok[:, 0, :], AF.Exp,
                                          bias=red[:, 0:1]), "se_s")
                b.v.wait_ge(b.sems["se_s"][0], si)
                b.vself(b.v.tensor_copy(ttok[:, 0, :], ttok[:, 1, :]))
                b.vself(b.v.memset(ttok[:, 2, :], 0.0))
                b.vself(b.v.memset(red[:, 2:3], 0.0))
                for j in range(4):
                    b.vself(b.v.reduce_max(red[:, 1:2], ttok[:, 0, :], axis=AX.X))
                    b.vself(b.v.tensor_tensor(red[:, 2:3], red[:, 2:3], red[:, 1:2],
                                              op=ALU.add))
                    b.vself(b.v.tensor_scalar(scr.tmpa[:, 0:16], ttok[:, 0, :],
                                              red[:, 1:2], None, op0=ALU.is_ge))
                    b.vself(b.v.tensor_tensor(ttok[:, 2, :], ttok[:, 2, :],
                                              scr.tmpa[:, 0:16], op=ALU.add))
                    b.vself(b.v.scalar_tensor_tensor(ttok[:, 0, :], scr.tmpa[:, 0:16], NEG,
                                                     ttok[:, 0, :], op0=ALU.mult,
                                                     op1=ALU.add))
                b.vself(b.v.reciprocal(red[:, 3:4], red[:, 2:3]))
                b.vself(b.v.tensor_tensor(ttok[:, 1, :], ttok[:, 1, :], ttok[:, 2, :],
                                          op=ALU.mult))
                vi = b.vself(b.v.tensor_scalar(ttok[:, 1, :], ttok[:, 1, :], red[:, 3:4],
                                               None, op0=ALU.mult))
                b.t.wait_ge(b.sems["se_v"][0], vi)
                scr.pacq(b, "pb2")
                ti = b.inc(b.t.transpose(scr.pbig[0:16, 2, 0:128], ttok[:, 1, :],
                                         K.ident_f32[:]), "se_t")
                b.s.wait_ge(b.sems["se_t"][0], ti)
                si = b.inc(b.s.copy(cT[:, tsl], scr.pbig[0:16, 2, 0:128]), "se_s")
                scr.prel("pb2", "se_s", si)
            b.wait(b.t, "se_s")
            if hasattr(K, "dbg_cT"):
                b.wait(b.y, "se_s")
                b.store(K.dbg_cT[bt * 2 + hf], cT[:])
            precast = (K.moe_call == 0 and bt == 0 and hf == 0)
            nexp = getattr(K, "dbg_nexp", 16)
            w1v = [K.w1bf[e].rearrange("(kt p) f -> p kt f", p=128)
                   for e in range(16)]
            w2v = [K.w2bf[e].rearrange("(kt p) f -> p kt f", p=128)
                   for e in range(16)]
            if not precast:
                if lastw1 is not None:
                    b.y.wait_ge(b.sems["se_t"][0], lastw1)
                    b.g.wait_ge(b.sems["se_t"][0], lastw1)
                tw1a = b.load(w1[:, 0:4, :], w1v[0][:, 0:4, :])
                tw1b = b.load_g(w1[:, 4:8, :], w1v[0][:, 4:8, :])
                if lastw2 is not None:
                    b.y.wait_ge(b.sems["se_t"][0], lastw2)
                    b.g.wait_ge(b.sems["se_t"][0], lastw2)
                tw2a = b.load(w2[:, 0:8, :], w2v[0][:, 0:8, :])
                tw2b = b.load_g(w2[:, 8:16, :], w2v[0][:, 8:16, :])
            for e in range(nexp):
                if lastgel is not None:
                    b.y.wait_ge(b.sems["se_s"][0], lastgel)
                tb1 = b.load(b1sb[:], Wap["b1l"][e])
                if precast:
                    g1 = load_wbf(b, scr, w1, Wap["w1"][e], DT, 2048,
                                  tguard=lastw1)
                    b.y.wait_ge(b.sems["se_g"][0], g1)
                    b.store(w1v[e], w1[:])
                    g2 = load_wbf(b, scr, w2, Wap["w2"][e], 16, D,
                                  tguard=lastw2)
                    b.y.wait_ge(b.sems["se_g"][0], g2)
                    b.store(w2v[e], w2[:])
                    b.wait_ld(b.s, tb1)
                    b.t.wait_ge(b.sems["se_g"][0], g1)
                else:
                    b.wait_ld(b.s, tb1)
                    b.wait_ld(b.t, tw1a)
                    b.wait_ldg(b.t, tw1b)
                for c in range(2):
                    sl = slice(c * 512, (c + 1) * 512)
                    scr.pacq(b, "paux0")
                    ti = b.inc(b.t.matmul(scr.paux[:, 0, :], sel[:, e, :], cT[:, sl],
                                          start=True, stop=True), "se_t")
                    b.s.wait_ge(b.sems["se_t"][0], ti)
                    b.s.wait_ge(b.sems["se_v"][0], b.val("se_v"))
                    sic = b.inc(b.s.copy(cbc[:], scr.paux[:, 0, :]), "se_s")
                    scr.prel("paux0", "se_s", sic)
                    b.v.wait_ge(b.sems["se_s"][0], sic)
                    for fi in range(16):
                        slot = fi % 3
                        key = f"pb{slot}"
                        scr.pacq(b, key)
                        for kt in range(DT):
                            mm = b.t.matmul(scr.pbig[:, slot, :],
                                            w1[:, kt, fi * 128:(fi + 1) * 128],
                                            ln[:, kt, sl], start=(kt == 0),
                                            stop=(kt == DT - 1))
                        ti = b.inc(mm, "se_t")
                        lastw1 = ti
                        b.s.wait_ge(b.sems["se_t"][0], ti)
                        b.s.wait_ge(b.sems["se_v"][0], b.val("se_v"))
                        si = b.inc(b.s.activation(gel[:], scr.pbig[:, slot, :], AF.Gelu,
                                                  bias=b1sb[:, fi:fi + 1]), "se_s")
                        lastgel = si
                        scr.prel(key, "se_s", si)
                        b.v.wait_ge(b.sems["se_s"][0], si)
                        vi = b.inc(b.v.tensor_tensor(hmid[:, fi, :], gel[:], cbc[:],
                                                     op=ALU.mult), "se_v")
                    if not precast and c == 1 and e + 1 < nexp:
                        b.y.wait_ge(b.sems["se_t"][0], lastw1)
                        b.g.wait_ge(b.sems["se_t"][0], lastw1)
                        tw1a = b.load(w1[:, 0:4, :], w1v[e + 1][:, 0:4, :])
                        tw1b = b.load_g(w1[:, 4:8, :], w1v[e + 1][:, 4:8, :])
                    if c == 0:
                        if precast:
                            b.t.wait_ge(b.sems["se_g"][0], g2)
                        else:
                            b.wait_ld(b.t, tw2a)
                            b.wait_ldg(b.t, tw2b)
                    b.t.wait_ge(b.sems["se_v"][0], vi)
                    nob2 = hasattr(K, "dbg_nob2") or e > 0
                    for dt_ in range(DT):
                        key = f"paux{dt_ % 2}"
                        scr.pacq(b, key)
                        if not nob2:
                            b.t.matmul(scr.paux[:, dt_ % 2, :],
                                       b2[:, dt_ * 128:(dt_ + 1) * 128],
                                       cT[:, sl], start=True, stop=False)
                        for ki in range(16):
                            mm = b.t.matmul(scr.paux[:, dt_ % 2, :],
                                            w2[:, ki, dt_ * 128:(dt_ + 1) * 128],
                                            hmid[:, ki, :], start=(nob2 and ki == 0),
                                            stop=(ki == 15))
                        ti = b.inc(mm, "se_t")
                        lastw2 = ti
                        b.v.wait_ge(b.sems["se_t"][0], ti)
                        vi2 = b.inc(b.v.tensor_tensor(x_sb[:, dt_, sl], x_sb[:, dt_, sl],
                                                      scr.paux[:, dt_ % 2, :], op=ALU.add),
                                    "se_v")
                        scr.prel(key, "se_v", vi2)
                if not precast and e + 1 < nexp:
                    b.y.wait_ge(b.sems["se_t"][0], lastw2)
                    b.g.wait_ge(b.sems["se_t"][0], lastw2)
                    tw2a = b.load(w2[:, 0:8, :], w2v[e + 1][:, 0:8, :])
                    tw2b = b.load_g(w2[:, 8:16, :], w2v[e + 1][:, 8:16, :])
            b.wait(b.y, "se_v")
            b.store(xap3(xout, bt)[:, :, hsl], x_sb[:])
            b.barrier()
    K.moe_call += 1
    sc.close()



def emit_l0a(b, K, scr, Wap, xin, xout):
    sc = ExitStack()
    b.drain_st()
    Ebuf = b.sb("a_E", [128, 2, 512], BF16, sc)
    bz = b.sb("a_bz", [128, 512], F32, sc)
    mc = b.sb("a_mc", [128, 896], F32, sc)
    b.load(mc[:], K.aps["mask_causal"])

    def rope_pair(wa, wb, src3, kt_tot, dst, rc, rs):
        for ft in range(DT):
            for c in range(4):
                sl = slice(c * 512, (c + 1) * 512)
                scr.pacq(b, "pb0")
                scr.pacq(b, "pb1")
                for kt in range(kt_tot):
                    b.t.matmul(scr.pbig[:, 0, :], wa[:, kt, ft * 128:(ft + 1) * 128],
                               src3[:, kt, sl], start=(kt == 0), stop=(kt == kt_tot - 1))
                for kt in range(kt_tot):
                    mm = b.t.matmul(scr.pbig[:, 1, :], wb[:, kt, ft * 128:(ft + 1) * 128],
                                    src3[:, kt, sl], start=(kt == 0), stop=(kt == kt_tot - 1))
                ti = b.inc(mm, "se_t")
                b.v.wait_ge(b.sems["se_t"][0], ti)
                b.v.tensor_tensor(scr.tmpa[:, :], scr.pbig[:, 0, :], rc[:, sl], op=ALU.mult)
                b.v.tensor_tensor(scr.tmpb[:, :], scr.pbig[:, 1, :], rs[:, sl], op=ALU.mult)
                vi = b.inc(b.v.tensor_tensor(dst[:, ft, sl], scr.tmpa[:, :], scr.tmpb[:, :],
                                             op=ALU.add), "se_v")
                scr.prel("pb0", "se_v", vi)
                scr.prel("pb1", "se_v", vi)

    for bt in range(BPC):
        sc_ln = ExitStack()
        ln = b.sb("a_ln", [128, DT, S], BF16, sc_ln)
        ln_from_hbm(b, K, scr, xin, bt, ln)
        rb_attn = ExitStack()
        attn = b.sb("a_attn", [128, DT, S], BF16, rb_attn, side="right")
        rb2 = ExitStack()
        q = b.sb("a_q", [128, DT, S], BF16, rb2, side="right")
        scw = ExitStack()
        rc = b.sb("a_rc", [128, S], F32, scw)
        rs = b.sb("a_rs", [128, S], F32, scw)
        wq = b.sb("a_wq", [128, DT, D], BF16, scw)
        wqs = b.sb("a_wqs", [128, DT, D], BF16, scw)
        b.load(rc[:], K.aps["rope_cos"])
        b.load(rs[:], K.aps["rope_sin"])
        load_wbf(b, scr, wq, Wap["wqT"], DT, D)
        gqs = load_wbf(b, scr, wqs, Wap["wqTs"], DT, D)
        b.t.wait_ge(b.sems["se_g"][0], gqs)
        rope_pair(wq, wqs, ln, DT, q, rc, rs)
        b.barrier()
        scw.close()
        lat = b.sb("a_lat", [128, 4, S], BF16, rb2, side="right")
        scw = ExitStack()
        kvd = b.sb("a_kvd", [128, DT, 512], BF16, scw)
        gl = load_wbf(b, scr, kvd, Wap["kvdT"], DT, 512)
        b.t.wait_ge(b.sems["se_g"][0], gl)
        for ft in range(4):
            for c in range(4):
                sl = slice(c * 512, (c + 1) * 512)
                slot = (ft * 4 + c) % 3
                key = f"pb{slot}"
                scr.pacq(b, key)
                for kt in range(DT):
                    mm = b.t.matmul(scr.pbig[:, slot, :], kvd[:, kt, ft * 128:(ft + 1) * 128],
                                    ln[:, kt, sl], start=(kt == 0), stop=(kt == DT - 1))
                ti = b.inc(mm, "se_t")
                b.s.wait_ge(b.sems["se_t"][0], ti)
                si = b.inc(b.s.copy(lat[:, ft, sl], scr.pbig[:, slot, :]), "se_s")
                scr.prel(key, "se_s", si)
        b.barrier()
        scw.close()
        sc_ln.close()
        k = b.sb("a_k", [128, DT, S], BF16, rb2, side="right")
        scw = ExitStack()
        rc = b.sb("a_rc2", [128, S], F32, scw)
        rs = b.sb("a_rs2", [128, S], F32, scw)
        kup = b.sb("a_kup", [128, 4, D], BF16, scw)
        kups = b.sb("a_kups", [128, 4, D], BF16, scw)
        b.load(rc[:], K.aps["rope_cos"])
        b.load(rs[:], K.aps["rope_sin"])
        load_wbf(b, scr, kup, Wap["kupT"], 4, D)
        gks = load_wbf(b, scr, kups, Wap["kupTs"], 4, D)
        b.t.wait_ge(b.sems["se_g"][0], gks)
        rope_pair(kup, kups, lat, 4, k, rc, rs)
        b.barrier()
        scw.close()
        v1 = b.sb("a_v1", [128, 16, 16, 64], BF16, rb2, side="right")
        scw = ExitStack()
        vup = b.sb("a_vup", [128, 4, D], BF16, scw)
        gv = load_wbf(b, scr, vup, Wap["vupT"], 4, D)
        b.t.wait_ge(b.sems["se_g"][0], gv)
        for st_ in range(16):
            for fg in range(2):
                slot = (st_ * 2 + fg) % 3
                key = f"pb{slot}"
                scr.pacq(b, key)
                for lt in range(4):
                    mm = b.t.matmul(scr.pbig[:, slot, :],
                                    lat[:, lt, st_ * 128:(st_ + 1) * 128],
                                    vup[:, lt, fg * 512:(fg + 1) * 512],
                                    start=(lt == 0), stop=(lt == 3))
                ti = b.inc(mm, "se_t")
                b.s.wait_ge(b.sems["se_t"][0], ti)
                si = b.inc(b.s.copy(
                    v1[:, st_, fg * 8:(fg + 1) * 8, :],
                    scr.pbig[:, slot, :].rearrange("p (h f) -> p h f", f=64)), "se_s")
                scr.prel(key, "se_s", si)
        b.t.wait_ge(b.sems["se_s"][0], si)
        b.barrier()
        scw.close()
        for h in range(16):
            ft, pb = h // 2, (h % 2) * 64
            for qc in range(4):
                qsl = slice(qc * 512, (qc + 1) * 512)
                ktmax = 4 * qc + 4
                scr.pacq(b, "paux0")
                for kt in range(ktmax):
                    slot = kt % 2
                    key = f"pb{slot}"
                    scr.pacq(b, key)
                    mm = b.t.matmul(scr.pbig[:, slot, :],
                                    k[pb:pb + 64, ft, kt * 128:(kt + 1) * 128],
                                    q[pb:pb + 64, ft, qsl], start=True, stop=True)
                    ti = b.inc(mm, "se_t")
                    if kt >= 4 * qc:
                        o = 384 - 128 * (kt - 4 * qc)
                        b.v.wait_ge(b.sems["se_t"][0], ti)
                        vi = b.inc(b.v.tensor_tensor(scr.pbig[:, slot, :],
                                                     scr.pbig[:, slot, :],
                                                     mc[:, o:o + 512], op=ALU.add), "se_v")
                        b.s.wait_ge(b.sems["se_v"][0], vi)
                    else:
                        b.s.wait_ge(b.sems["se_t"][0], ti)
                    si = b.inc(b.s.activation(Ebuf[:, slot, :], scr.pbig[:, slot, :],
                                              AF.Exp, scale=0.125), "se_s")
                    scr.prel(key, "se_s", si)
                    b.t.wait_ge(b.sems["se_s"][0], si)
                    if kt == 0:
                        scr.pacq(b, "pst0")
                    b.t.matmul(scr.pst[:, 0, :], K.ones_c_bf16[:], Ebuf[:, slot, :],
                               start=(kt == 0), stop=(kt == ktmax - 1))
                    mm = b.t.matmul(scr.paux[pb:pb + 64, 0, :], v1[:, kt, h, :],
                                    Ebuf[:, slot, :],
                                    start=(kt == 0), stop=(kt == ktmax - 1))
                tpv = b.inc(mm, "se_t")
                b.v.wait_ge(b.sems["se_t"][0], tpv)
                vz = b.inc(b.v.reciprocal(scr.ab[0:1, 0, 0:512], scr.pst[:, 0, :]),
                           "se_v")
                scr.prel("pst0", "se_v", vz)
                b.s.wait_ge(b.sems["se_v"][0], vz)
                vz = b.inc(b.s.copy(scr.ab16[0:1, 0, 0:512], scr.ab[0:1, 0, 0:512]),
                           "se_s")
                b.t.wait_ge(b.sems["se_s"][0], vz)
                scr.pacq(b, "paux1")
                ti = b.inc(b.t.matmul(scr.paux[pb:pb + 64, 1, :], K.ones_r_bf16[:, 0:64],
                                      scr.ab16[0:1, 0, 0:512], start=True, stop=True), "se_t")
                b.s.wait_ge(b.sems["se_t"][0], ti)
                si = b.inc(b.s.copy(bz[pb:pb + 64, :], scr.paux[pb:pb + 64, 1, :]), "se_s")
                scr.prel("paux1", "se_s", si)
                b.v.wait_ge(b.sems["se_s"][0], si)
                vi = b.inc(b.v.tensor_tensor(attn[pb:pb + 64, ft, qsl],
                                             scr.paux[pb:pb + 64, 0, :], bz[pb:pb + 64, :],
                                             op=ALU.mult), "se_v")
                scr.prel("paux0", "se_v", vi)
        b.barrier()
        rb2.close()
        scw = ExitStack()
        wo = b.sb("a_wo", [128, DT, D], BF16, scw)
        x2 = b.sb("a_x2", [128, DT, S], F32, scw)
        go = load_wbf(b, scr, wo, Wap["woT"], DT, D)
        b.wait(b.y, "se_v")
        b.drain_st()
        tx2 = b.load(x2[:], xap3(xin, bt))
        b.t.wait_ge(b.sems["se_g"][0], go)
        b.wait_ld(b.v, tx2)
        for c in range(4):
            sl = slice(c * 512, (c + 1) * 512)
            for ftd in range(DT):
                slot = (ftd * 4 + c) % 3
                key = f"pb{slot}"
                scr.pacq(b, key)
                for kt in range(DT):
                    mm = b.t.matmul(scr.pbig[:, slot, :], wo[:, kt, ftd * 128:(ftd + 1) * 128],
                                    attn[:, kt, sl], start=(kt == 0), stop=(kt == DT - 1))
                ti = b.inc(mm, "se_t")
                b.v.wait_ge(b.sems["se_t"][0], ti)
                vi = b.inc(b.v.tensor_tensor(x2[:, ftd, sl], x2[:, ftd, sl],
                                             scr.pbig[:, slot, :], op=ALU.add), "se_v")
                scr.prel(key, "se_v", vi)
        b.wait(b.y, "se_v")
        b.store(xap3(xout, bt), x2[:])
        b.barrier()
        scw.close()
        rb_attn.close()
    sc.close()



def emit_l1a(b, K, scr, Wap, xin, xout):
    sc = ExitStack()
    b.drain_st()
    ln = b.sb("c_ln", [128, DT, S], BF16, sc)
    x_sb = b.sb("c_x", [128, DT, S], F32, sc)
    xc = b.sb("c_xc", [128, S], BF16, sc)
    sq = b.sb("c_sq", [128, S], BF16, sc)
    nd = b.sb("c_nd", [1, S], F32, sc)
    x1 = b.sb("c_x1", [128, 16, 64], BF16, sc)
    Ebuf = b.sb("c_E", [128, 2, 512], BF16, sc)
    bz = b.sb("c_bz", [128, 512], F32, sc)
    mrow = b.sb("c_m", [128, 1], F32, sc)

    for bt in range(BPC):
        tx = b.load(x_sb[:], xap3(xin, bt))
        for e in (b.s, b.t, b.v):
            b.wait_ld(e, tx)
        emit_ln(b, K, scr, x_sb, ln, S)
        for h in range(16):
            ft, pb = h // 2, (h % 2) * 64
            b.vself(b.v.reduce_sum(mrow[pb:pb + 64, :], ln[pb:pb + 64, ft, :], axis=AX.X,
                                   negate=True))
            b.vself(b.v.tensor_scalar(mrow[pb:pb + 64, :], mrow[pb:pb + 64, :], 1.0 / S,
                                      None, op0=ALU.mult))
            b.vself(b.v.tensor_scalar(xc[pb:pb + 64, :], ln[pb:pb + 64, ft, :],
                                      mrow[pb:pb + 64, :], None, op0=ALU.add))
            vi = b.vself(b.v.tensor_tensor(sq[pb:pb + 64, :], xc[pb:pb + 64, :],
                                           xc[pb:pb + 64, :], op=ALU.mult))
            b.t.wait_ge(b.sems["se_v"][0], vi)
            for c in range(4):
                sl = slice(c * 512, (c + 1) * 512)
                scr.pacq(b, "pst0")
                mm = b.t.matmul(scr.pst[:, 0, :], K.ones_c_bf16[pb:pb + 64, :],
                                sq[pb:pb + 64, sl], start=True, stop=True)
                ti = b.inc(mm, "se_t")
                b.v.wait_ge(b.sems["se_t"][0], ti)
                vi = b.inc(b.v.tensor_scalar(nd[0:1, sl], scr.pst[:, 0, :], -1.0, None,
                                             op0=ALU.mult), "se_v")
                scr.prel("pst0", "se_v", vi)
            scr.pacq(b, "pbf")
            for Tt in range(16):
                mm = b.t.transpose(scr.pbf[:, Tt * 64:(Tt + 1) * 64],
                                   ln[pb:pb + 64, ft, Tt * 128:(Tt + 1) * 128],
                                   K.ident_bf16[pb:pb + 64, pb:pb + 64])
            ti = b.inc(mm, "se_t")
            b.s.wait_ge(b.sems["se_t"][0], ti)
            sx1 = b.inc(b.s.copy(x1[:, :, :],
                                 scr.pbf[:].rearrange("p (t f) -> p t f", f=64)), "se_s")
            scr.prel("pbf", "se_s", sx1)
            b.t.wait_ge(b.sems["se_s"][0], sx1)
            b.t.wait_ge(b.sems["se_v"][0], vi)
            for tc in range(4):
                tsl = slice(tc * 512, (tc + 1) * 512)
                scr.pacq(b, "paux0")
                for Tt in range(16):
                    slot = Tt % 2
                    key = f"pb{slot}"
                    scr.pacq(b, key)
                    b.t.matmul(scr.pbig[:, slot, :], xc[pb:pb + 64, Tt * 128:(Tt + 1) * 128],
                               xc[pb:pb + 64, tsl], start=True, stop=False)
                    mm = b.t.matmul(scr.pbig[:, slot, :], K.ones_r_f32[:], nd[0:1, tsl],
                                    start=False, stop=True)
                    ti = b.inc(mm, "se_t")
                    b.s.wait_ge(b.sems["se_t"][0], ti)
                    si = b.inc(b.s.activation(Ebuf[:, slot, :], scr.pbig[:, slot, :],
                                              AF.Exp), "se_s")
                    scr.prel(key, "se_s", si)
                    b.t.wait_ge(b.sems["se_s"][0], si)
                    if Tt == 0:
                        scr.pacq(b, "pst0")
                    b.t.matmul(scr.pst[:, 0, :], K.ones_c_bf16[:], Ebuf[:, slot, :],
                               start=(Tt == 0), stop=(Tt == 15))
                    mm = b.t.matmul(scr.paux[pb:pb + 64, 0, :], x1[:, Tt, :],
                                    Ebuf[:, slot, :], start=(Tt == 0), stop=(Tt == 15))
                tpv = b.inc(mm, "se_t")
                b.v.wait_ge(b.sems["se_t"][0], tpv)
                vz = b.inc(b.v.reciprocal(scr.ab[0:1, 0, 0:512], scr.pst[:, 0, :]),
                           "se_v")
                scr.prel("pst0", "se_v", vz)
                b.s.wait_ge(b.sems["se_v"][0], vz)
                vz = b.inc(b.s.copy(scr.ab16[0:1, 0, 0:512], scr.ab[0:1, 0, 0:512]),
                           "se_s")
                b.t.wait_ge(b.sems["se_s"][0], vz)
                scr.pacq(b, "paux1")
                ti = b.inc(b.t.matmul(scr.paux[pb:pb + 64, 1, :], K.ones_r_bf16[:, 0:64],
                                      scr.ab16[0:1, 0, 0:512], start=True, stop=True), "se_t")
                b.s.wait_ge(b.sems["se_t"][0], ti)
                si = b.inc(b.s.copy(bz[pb:pb + 64, :], scr.paux[pb:pb + 64, 1, :]), "se_s")
                scr.prel("paux1", "se_s", si)
                b.v.wait_ge(b.sems["se_s"][0], si)
                b.v.tensor_tensor(scr.tmpa[pb:pb + 64, :], scr.paux[pb:pb + 64, 0, :],
                                  bz[pb:pb + 64, :], op=ALU.mult)
                vi = b.inc(b.v.tensor_tensor(x_sb[pb:pb + 64, ft, tsl],
                                             x_sb[pb:pb + 64, ft, tsl],
                                             scr.tmpa[pb:pb + 64, :], op=ALU.add), "se_v")
                scr.prel("paux0", "se_v", vi)
        b.wait(b.y, "se_v")
        b.store(xap3(xout, bt), x_sb[:])
        b.barrier()
    sc.close()



def emit_gla(b, K, scr, Wap, xin, xout):
    sc = ExitStack()
    b.drain_st()
    wq_ = b.sb("g_wq", [128, DT, D], BF16, sc)
    wk_ = b.sb("g_wk", [128, DT, D], BF16, sc)
    wv_ = b.sb("g_wv", [128, DT, 2048], BF16, sc)
    wog = b.sb("g_wog", [128, DT, 2048], BF16, sc)
    wo_ = b.sb("g_wo", [128, 16, D], BF16, sc)
    wg_ = b.sb("g_wg", [128, DT, 8], F32, sc)
    bg16 = b.sb("g_bg16", [8, 1], F32, sc)
    csg = b.sb("g_csg", [8, 1], F32, sc)
    xc_ = b.sb("g_xc", [128, DT, 128], F32, sc)
    lnc = b.sb("g_lnc", [128, DT, 128], BF16, sc)
    qh = b.sb("g_qh", [128, 128], BF16, sc)
    kh = b.sb("g_kh", [128, 128], BF16, sc)
    ktl = b.sb("g_ktl", [128, 128], BF16, sc)
    kd = b.sb("g_kd", [128, 128], BF16, sc)
    vh = b.sb("g_vh", [128, 256], BF16, sc)
    ogh = b.sb("g_ogh", [128, 256], BF16, sc)
    gt_ = b.sb("g_gt", [128, 256], BF16, sc)
    S0 = b.sb("g_S0", [128, 2, 8, 256], BF16, sc)
    Lf = b.sb("g_Lf", [8, S], F32, sc)
    Ltok = b.sb("g_Ltok", [128, 8], F32, sc)
    nLtok = b.sb("g_nLtok", [128, 8], F32, sc)
    Lrow = b.sb("g_Lrow", [1, 128], F32, sc)
    lcsave = b.sb("g_lcs", [1, 8, 1], F32, sc)
    lcend = b.sb("g_lce", [1, 8, 1], F32, sc)
    lc0 = b.sb("g_lc0", [128, 2], F32, sc)
    Dbf = b.sb("g_D", [128, 128], BF16, sc)
    Msb = b.sb("g_M", [128, 128], BF16, sc)
    osb = b.sb("g_osb", [128, 256], BF16, sc)
    intr = b.sb("g_intr", [128, 256], F32, sc)
    are = b.sb("g_are", [128, 3], F32, sc)
    gT = b.sb("g_gT", [128, 16, 128], BF16, sc)
    mtri = b.sb("g_mtri", [128, 128], F32, sc)

    b.load(mtri[:], K.aps["mask_tri"])
    load_wbf(b, scr, wq_, Wap["gWq"], DT, D)
    load_wbf(b, scr, wk_, Wap["gWk"], DT, D)
    load_wbf(b, scr, wv_, Wap["gWv"], DT, 2048)
    load_wbf(b, scr, wog, Wap["gWog"], DT, 2048)
    go = load_wbf(b, scr, wo_, Wap["gWo"], 16, D)
    tg = b.load(wg_[:], Wap["gWg"])
    tbg = b.load(bg16[:], Wap["gbg"])
    b.wait_ld(b.s, tbg)
    si = b.inc(b.s.mul(bg16[:], bg16[:], 1.0 / 16.0), "se_s")
    b.wait_ld(b.t, tg)
    scr.pacq(b, "pst0")
    for kt in range(DT):
        mm = b.t.matmul(scr.pst[:, 0, 0:8], K.ones_c_f32[:], wg_[:, kt, :],
                        start=(kt == 0), stop=(kt == DT - 1))
    ti = b.inc(mm, "se_t")
    b.s.wait_ge(b.sems["se_t"][0], ti)
    si = b.inc(b.s.copy(scr.ab[0:1, 0, 0:8], scr.pst[:, 0, 0:8]), "se_s")
    scr.prel("pst0", "se_s", si)
    b.t.wait_ge(b.sems["se_s"][0], si)
    scr.pacq(b, "pb0")
    ti = b.inc(b.t.transpose(scr.pbig[0:8, 0, 0:1], scr.ab[0:1, 0, 0:8],
                             K.ident_f32[0:1, 0:1]), "se_t")
    b.s.wait_ge(b.sems["se_t"][0], ti)
    si = b.inc(b.s.copy(csg[:], scr.pbig[0:8, 0, 0:1]), "se_s")
    scr.prel("pb0", "se_s", si)
    b.t.wait_ge(b.sems["se_g"][0], go)

    def pmm(slot, out_ap, mk_lhsT, mk_rhs, nkt=DT, consumer="se_s"):
        key = f"pb{slot}"
        scr.pacq(b, key)
        for kt in range(nkt):
            mm = b.t.matmul(out_ap, mk_lhsT(kt), mk_rhs(kt),
                            start=(kt == 0), stop=(kt == nkt - 1))
        return b.inc(mm, "se_t"), key

    for bt in range(BPC):
        loga = Lf
        b.drain_st()
        vz = b.inc(b.v.memset(S0[:, 0, :, :], 0.0), "se_v")
        vz = b.inc(b.v.memset(lcsave[:], 0.0), "se_v")
        b.t.wait_ge(b.sems["se_v"][0], vz)
        vS = vz
        lcs_tok = None
        for ch in range(16):
            csl = slice(ch * 128, (ch + 1) * 128)
            b.wait(b.y, "se_v")
            tx = b.load(xc_[:], xap3(xin, bt)[:, :, csl])
            for e in (b.s, b.t, b.v):
                b.wait_ld(e, tx)
            scr.pacq(b, "pb2")
            for kt in range(DT):
                mm = b.t.matmul(scr.pbig[0:8, 2, 0:128], wg_[:, kt, :], xc_[:, kt, :],
                                start=(kt == 0), stop=(kt == DT - 1))
            tig = b.inc(mm, "se_t")
            emit_ln(b, K, scr, xc_, lnc, 128, apply=False)
            b.v.wait_ge(b.sems["se_t"][0], tig)
            b.v.wait_ge(b.sems["se_s"][0], b.val("se_s"))
            b.vself(b.v.tensor_tensor(scr.tmpa[0:8, 0:128], scr.pbig[0:8, 2, 0:128],
                                      scr.Abc[0:8, 0:128], op=ALU.mult))
            vi = b.vself(b.v.scalar_tensor_tensor(scr.tmpa[0:8, 0:128], scr.Bbc[0:8, 0:128],
                                                  csg[:, 0:1], scr.tmpa[0:8, 0:128],
                                                  op0=ALU.mult, op1=ALU.add))
            scr.prel("pb2", "se_v", vi)
            if getattr(K, "dbg_gla_x1", None) is not None:
                b.wait(b.y, "se_v")
                b.store(K.dbg_gla_x1[bt, ch], xc_[:])
                b.store(K.dbg_gla_lg[bt, ch], scr.tmpa[0:8, 0:128])
                b.wait(b.s, "st")
                b.wait(b.v, "st")
            b.s.wait_ge(b.sems["se_v"][0], vi)
            b.sself(b.s.activation(scr.tmpb[0:8, 0:128], scr.tmpa[0:8, 0:128], AF.Sigmoid,
                                   scale=1.0 / 16.0, bias=bg16[:]))
            sig = b.sself(b.s.activation(loga[:, csl], scr.tmpb[0:8, 0:128], AF.Ln))
        b.v.wait_ge(b.sems["se_s"][0], sig)
        vL = b.inc(b.v.tensor_tensor_scan(Lf[:], K.ones_row2k[:], loga[:], 0.0,
                                          op0=ALU.mult, op1=ALU.add), "se_v")
        b.t.wait_ge(b.sems["se_v"][0], vL)
        if getattr(K, "dbg_gla_Lf", None) is not None:
            b.wait(b.y, "se_v")
            b.store(K.dbg_gla_Lf[bt], Lf[:])
            b.wait(b.s, "st")
        for ch in range(16):
            csl = slice(ch * 128, (ch + 1) * 128)
            par, oth = ch % 2, 1 - ch % 2
            b.wait(b.y, "se_v")
            tx = b.load(xc_[:], xap3(xin, bt)[:, :, csl])
            for e in (b.s, b.t, b.v):
                b.wait_ld(e, tx)
            vln = emit_ln(b, K, scr, xc_, lnc, 128)
            b.t.wait_ge(b.sems["se_v"][0], vln)
            b.t.wait_ge(b.sems["se_v"][0], vS)
            if lcs_tok is not None:
                b.t.wait_ge(b.sems["se_s"][0], lcs_tok)
            scr.pacq(b, "pb0")
            t1 = b.inc(b.t.transpose(scr.pbig[:, 0, 0:8], Lf[:, csl],
                                     K.ident_f32[0:8, 0:8]), "se_t")
            b.s.wait_ge(b.sems["se_t"][0], t1)
            s1 = b.inc(b.s.copy(Ltok[:], scr.pbig[:, 0, 0:8]), "se_s")
            scr.prel("pb0", "se_s", s1)
            b.v.wait_ge(b.sems["se_s"][0], s1)
            vnl = b.vself(b.v.tensor_scalar(nLtok[:], Ltok[:], -1.0, None, op0=ALU.mult))
            b.v.wait_ge(b.sems["se_v"][0], vnl)
            for hh in range(8):
                hs4 = slice(hh * 256, (hh + 1) * 256)
                scr.pacq(b, "pb0")
                t2 = b.inc(b.t.transpose(scr.pbig[0:1, 0, 0:128], Ltok[:, hh:hh + 1],
                                         K.ident_f32[:]), "se_t")
                b.s.wait_ge(b.sems["se_t"][0], t2)
                s2 = b.inc(b.s.copy(Lrow[:], scr.pbig[0:1, 0, 0:128]), "se_s")
                s2 = b.inc(b.s.copy(lcend[0:1, hh, :], Lrow[0:1, 127:128]), "se_s")
                scr.prel("pb0", "se_s", s2)
                b.t.wait_ge(b.sems["se_s"][0], s2)
                scr.pacq(b, "pb2")
                b.t.matmul(scr.pbig[:, 2, 0:1], K.ones_r_f32[:], lcsave[0:1, hh, 0:1],
                           start=True, stop=True)
                tiE = b.inc(b.t.matmul(scr.pbig[:, 2, 1:2], K.ones_r_f32[:],
                                       Lrow[0:1, 127:128], start=True, stop=True), "se_t")
                b.s.wait_ge(b.sems["se_t"][0], tiE)
                sE = b.inc(b.s.copy(lc0[:, 0:2], scr.pbig[:, 2, 0:2]), "se_s")
                scr.prel("pb2", "se_s", sE)
                b.v.wait_ge(b.sems["se_s"][0], sE)
                b.vself(b.v.tensor_tensor(scr.tmpa[:, 0:1], Ltok[:, hh:hh + 1], lc0[:, 0:1],
                                          op=ALU.subtract))
                b.vself(b.v.tensor_tensor(scr.tmpa[:, 1:2], lc0[:, 1:2], Ltok[:, hh:hh + 1],
                                          op=ALU.subtract))
                vi = b.vself(b.v.tensor_tensor(scr.tmpa[:, 2:3], lc0[:, 1:2], lc0[:, 0:1],
                                               op=ALU.subtract))
                b.s.wait_ge(b.sems["se_v"][0], vi)
                sA = b.inc(b.s.activation(are[:, 0:3], scr.tmpa[:, 0:3], AF.Exp), "se_s")
                if getattr(K, "dbg_gla_are", None) is not None:
                    b.wait(b.y, "se_s")
                    b.store(K.dbg_gla_are[bt, ch, hh], are[:, 0:3])
                    b.store(K.dbg_gla_lc[bt, ch, hh], lc0[:, 0:2])
                    b.wait(b.s, "st")
                    b.wait(b.v, "st")
                ti, kq = pmm(0, scr.pbig[:, 0, 0:128],
                             lambda kt: wq_[:, kt, hh * 128:(hh + 1) * 128],
                             lambda kt: lnc[:, kt, :])
                b.s.wait_ge(b.sems["se_t"][0], ti)
                sq_ = b.inc(b.s.copy(qh[:], scr.pbig[:, 0, 0:128]), "se_s")
                scr.prel(kq, "se_s", sq_)
                ti, kk = pmm(1, scr.pbig[:, 1, 0:128],
                             lambda kt: lnc[:, kt, :],
                             lambda kt: wk_[:, kt, hh * 128:(hh + 1) * 128])
                b.s.wait_ge(b.sems["se_t"][0], ti)
                sk_ = b.inc(b.s.activation(kh[:], scr.pbig[:, 1, 0:128], AF.Identity,
                                           scale=KSCALE), "se_s")
                scr.prel(kk, "se_s", sk_)
                ti, kv = pmm(2, scr.pbig[:, 2, 0:256],
                             lambda kt: lnc[:, kt, :],
                             lambda kt: wv_[:, kt, hh * 256:(hh + 1) * 256])
                b.s.wait_ge(b.sems["se_t"][0], ti)
                sv_ = b.inc(b.s.copy(vh[:], scr.pbig[:, 2, 0:256]), "se_s")
                scr.prel(kv, "se_s", sv_)
                ti, ko = pmm(0, scr.pbig[:, 0, 0:256],
                             lambda kt: lnc[:, kt, :],
                             lambda kt: wog[:, kt, hh * 256:(hh + 1) * 256])
                b.s.wait_ge(b.sems["se_t"][0], ti)
                so_ = b.inc(b.s.activation(ogh[:], scr.pbig[:, 0, 0:256], AF.Sigmoid),
                            "se_s")
                scr.prel(ko, "se_s", so_)
                b.t.wait_ge(b.sems["se_s"][0], sk_)
                scr.pacq(b, "pbf")
                ti = b.inc(b.t.transpose(scr.pbf[:, 0:128], kh[:], K.ident_bf16[:]),
                           "se_t")
                b.s.wait_ge(b.sems["se_t"][0], ti)
                skd = b.inc(b.s.copy(kd[:], scr.pbf[:, 0:128]), "se_s")
                scr.prel("pbf", "se_s", skd)
                b.t.wait_ge(b.sems["se_s"][0], skd)
                scr.pacq(b, "pb0")
                b.t.matmul(scr.pbig[:, 0, 0:128], kd[:], qh[:], start=True, stop=True)
                scr.pacq(b, "pb1")
                tiD = b.inc(b.t.matmul(scr.pbig[:, 1, 0:128], K.ones_r_f32[:],
                                       Lrow[:], start=True, stop=True), "se_t")
                b.v.wait_ge(b.sems["se_t"][0], tiD)
                viD = b.vself(b.v.tensor_tensor(scr.pbig[:, 1, 0:128], scr.pbig[:, 1, 0:128],
                                                mtri[:], op=ALU.add))
                b.s.wait_ge(b.sems["se_v"][0], viD)
                siD = b.inc(b.s.activation(Dbf[:], scr.pbig[:, 1, 0:128], AF.Exp,
                                           bias=nLtok[:, hh:hh + 1]), "se_s")
                scr.prel("pb1", "se_s", siD)
                b.v.wait_ge(b.sems["se_s"][0], siD)
                viM = b.vself(b.v.tensor_tensor(Msb[:], scr.pbig[:, 0, 0:128], Dbf[:],
                                                op=ALU.mult))
                scr.prel("pb0", "se_v", viM)
                b.t.wait_ge(b.sems["se_v"][0], viM)
                scr.pacq(b, "paux0")
                tiI = b.inc(b.t.matmul(scr.paux[:, 0, 0:256], Msb[:], vh[:],
                                       start=True, stop=True), "se_t")
                b.t.wait_ge(b.sems["se_s"][0], sA)
                scr.pacq(b, "paux1")
                tiQ = b.inc(b.t.matmul(scr.paux[:, 1, 0:256], qh[:],
                                       S0[:, par, hh, :], start=True, stop=True), "se_t")
                b.s.wait_ge(b.sems["se_t"][0], tiI)
                siI = b.inc(b.s.copy(intr[:], scr.paux[:, 0, 0:256]), "se_s")
                scr.prel("paux0", "se_s", siI)
                b.v.wait_ge(b.sems["se_s"][0], siI)
                b.v.wait_ge(b.sems["se_t"][0], tiQ)
                vo = b.inc(b.v.scalar_tensor_tensor(osb[:], scr.paux[:, 1, 0:256],
                                                    are[:, 0:1], intr[:],
                                                    op0=ALU.mult, op1=ALU.add), "se_v")
                scr.prel("paux1", "se_v", vo)
                vg = b.inc(b.v.tensor_tensor(gt_[:], osb[:], ogh[:], op=ALU.mult), "se_v")
                vk = b.inc(b.v.tensor_scalar(ktl[:], kh[:], are[:, 1:2], None,
                                             op0=ALU.mult), "se_v")
                b.t.wait_ge(b.sems["se_v"][0], vk)
                scr.pacq(b, "paux0")
                tiU = b.inc(b.t.matmul(scr.paux[:, 0, 0:256], ktl[:], vh[:],
                                       start=True, stop=True), "se_t")
                b.v.wait_ge(b.sems["se_t"][0], tiU)
                vS = b.vself(b.v.scalar_tensor_tensor(S0[:, oth, hh, :], S0[:, par, hh, :],
                                                      are[:, 2:3], scr.paux[:, 0, 0:256],
                                                      op0=ALU.mult, op1=ALU.add))
                scr.prel("paux0", "se_v", vS)
                b.t.wait_ge(b.sems["se_v"][0], vg)
                for sub in range(2):
                    scr.pacq(b, "pbf")
                    ti = b.inc(b.t.transpose(scr.pbf[:, 128:256],
                                             gt_[:, sub * 128:(sub + 1) * 128],
                                             K.ident_bf16[:]), "se_t")
                    b.s.wait_ge(b.sems["se_t"][0], ti)
                    sgT = b.inc(b.s.copy(gT[:, hh * 2 + sub, :],
                                         scr.pbf[:, 128:256]), "se_s")
                    scr.prel("pbf", "se_s", sgT)
            vlc = b.vself(b.v.tensor_copy(lcsave[:], lcend[:]))
            lcs_tok = None
            b.t.wait_ge(b.sems["se_v"][0], vlc)
            if True:
                scg = ExitStack()
                xg = b.sb("g_xg", [128, DT, 128], F32, scg)
                b.drain_st()
                txg = b.load(xg[:], xap3(xin, bt)[:, :, csl])
                b.wait_ld(b.v, txg)
                b.t.wait_ge(b.sems["se_s"][0], sgT)
                for dt_ in range(DT):
                    slot = dt_ % 2
                    key = f"pb{slot}"
                    scr.pacq(b, key)
                    for ki in range(16):
                        mm = b.t.matmul(scr.pbig[:, slot, 0:128],
                                        wo_[:, ki, dt_ * 128:(dt_ + 1) * 128],
                                        gT[:, ki, :], start=(ki == 0), stop=(ki == 15))
                    ti = b.inc(mm, "se_t")
                    b.v.wait_ge(b.sems["se_t"][0], ti)
                    vi = b.inc(b.v.tensor_tensor(xg[:, dt_, :], xg[:, dt_, :],
                                                 scr.pbig[:, slot, 0:128], op=ALU.add),
                               "se_v")
                    scr.prel(key, "se_v", vi)
                b.wait(b.y, "se_v")
                b.store(xap3(xout, bt)[:, :, csl], xg[:])
                b.barrier()
                scg.close()
        b.barrier()
    sc.close()



SUB_SEQ = ["l0a", "moe", "hgrn", "moe", "l1a", "l1f", "gla", "l1f"]
EMITS = {"l0a": emit_l0a, "moe": emit_moe, "hgrn": emit_hgrn,
         "l1a": emit_l1a, "l1f": emit_l1ffn, "gla": emit_gla}
WKEYS = {
    "l0a": ["wqT", "wqTs", "kvdT", "kupT", "kupTs", "vupT", "woT"],
    "moe": ["rw", "rb_row", "w1", "b1l", "w2", "b2", "sel16"],
    "hgrn": ["hWiT", "hWfT", "hWgT", "hWoT", "hnw"],
    "l1a": [],
    "l1f": ["fWrT", "fWkT", "fWvT", "fmur", "fmuk"],
    "gla": ["gWq", "gWk", "gWv", "gWo", "gWg", "gbg", "gWog"],
}
_CONSTS = host_constants()
DBG_GLA = False


class KTiles:
    pass


def build_program(calls, pshapes):
    nc = bass.Bass("TRN2", target_bir_lowering=False)
    b = Bld(nc)
    K = KTiles()
    K.aps = {}
    for kk, vv in _CONSTS.items():
        dt = BF16 if vv.dtype == ml_dtypes.bfloat16 else F32
        K.aps[kk] = nc.dram_tensor("c_" + kk, list(vv.shape), dt, kind="ExternalInput").ap()
    K.ident_f32 = b.sb("k_idf", [128, 128], F32)
    K.ident_bf16 = b.sb("k_idb", [128, 128], BF16)
    K.ones_r_f32 = b.sb("k_or", [1, 128], F32)
    K.ones_r_bf16 = b.sb("k_orb", [1, 128], BF16)
    K.ones_c_f32 = b.sb("k_ocf", [128, 1], F32)
    K.ones_c_bf16 = b.sb("k_ocb", [128, 1], BF16)
    K.ones_row2k = b.sb("k_o2k", [8, S], BF16)
    K.epsln = b.sb("k_eps", [1, 1], F32)
    for nm in ("ident_f32", "ident_bf16", "ones_r_f32", "ones_r_bf16", "ones_c_f32",
               "ones_c_bf16", "ones_row2k", "epsln"):
        b.load(getattr(K, nm)[:], K.aps[nm])
    scr = Scratch(b)
    scr.eps = K.epsln
    if "moe" in calls:
        K.w1bf = nc.dram_tensor("moe_w1bf", [16, 1024, 2048], BF16,
                                kind="Internal").ap()
        K.w2bf = nc.dram_tensor("moe_w2bf", [16, 2048, 1024], BF16,
                                kind="Internal").ap()
        K.moe_call = 0
    if DBG_GLA:
        K.dbg_gla_Lf = nc.dram_tensor("dbg_gla_Lf", [BPC, 8, S], F32,
                                      kind="ExternalOutput").ap()
        K.dbg_gla_are = nc.dram_tensor("dbg_gla_are", [BPC, 16, 8, 128, 3], F32,
                                       kind="ExternalOutput").ap()
        K.dbg_gla_lc = nc.dram_tensor("dbg_gla_lc", [BPC, 16, 8, 128, 2], F32,
                                      kind="ExternalOutput").ap()
        K.dbg_gla_x1 = nc.dram_tensor("dbg_gla_x1", [BPC, 16, 128, DT, 128], F32,
                                      kind="ExternalOutput").ap()
        K.dbg_gla_lg = nc.dram_tensor("dbg_gla_lg", [BPC, 16, 8, 128], F32,
                                      kind="ExternalOutput").ap()
    Wap = {}
    need = set()
    for cname in calls:
        need.update(WKEYS[cname])
    for wk in sorted(need):
        if wk in _CONSTS:
            Wap[wk] = K.aps[wk]
        else:
            Wap[wk] = nc.dram_tensor("w_" + wk, list(pshapes[wk]), F32,
                                     kind="ExternalInput").ap()
    xin = nc.dram_tensor("xin", [BPC, D, S], F32, kind="ExternalInput").ap()
    xout = nc.dram_tensor("xout", [BPC, D, S], F32, kind="ExternalOutput").ap()
    mids = [nc.dram_tensor(f"xmid{i}", [BPC, D, S], F32, kind="Internal").ap()
            for i in range(2)]
    for e in (b.t, b.v, b.s, b.g, b.y):
        b.wait_ld_all(e)
    for i, cname in enumerate(calls):
        src = xin if i == 0 else mids[(i - 1) % 2]
        dst = xout if i == len(calls) - 1 else mids[i % 2]
        EMITS[cname](b, K, scr, Wap, src, dst)
    b.drain_st()
    b.es.close()
    return nc


def in_map_for(calls, p, xshard):
    m = {("c_" + kk): vv for kk, vv in _CONSTS.items()}
    need = set()
    for cname in calls:
        need.update(WKEYS[cname])
    for wk in sorted(need):
        if wk not in _CONSTS:
            m["w_" + wk] = p[wk]
    m["xin"] = xshard
    return m


FUSED = True


def kernel(**inputs):
    p = prep_inputs(inputs)
    pshapes = {kk: vv.shape for kk, vv in p.items()}
    x = np.asarray(inputs["L0_attn_a_x"])
    ncore = 8
    shards = [np.ascontiguousarray(x[c * BPC:(c + 1) * BPC].transpose(0, 2, 1))
              for c in range(ncore)]
    if FUSED:
        calls = SUB_SEQ
        nc = build_program(calls, pshapes)
        in_maps = [in_map_for(calls, p, shards[c]) for c in range(ncore)]
        res = run_bass_kernel_spmd(nc, in_maps, core_ids=list(range(ncore)))
        cur = [res.results[c]["xout"] for c in range(ncore)]
    else:
        progs = {name: build_program([name], pshapes)
                 for name in dict.fromkeys(SUB_SEQ)}
        cur = shards
        for name in SUB_SEQ:
            in_maps = [in_map_for([name], p, cur[c]) for c in range(ncore)]
            res = run_bass_kernel_spmd(progs[name], in_maps,
                                       core_ids=list(range(ncore)))
            cur = [res.results[c]["xout"] for c in range(ncore)]
    outs = [np.ascontiguousarray(np.asarray(cur[c]).transpose(0, 2, 1))
            for c in range(ncore)]
    return np.concatenate(outs, axis=0).astype(np.float32)
